# version 01
# baseline (speedup 1.0000x reference)
import contextlib
import numpy as np
import concourse.bass as bass
import concourse.mybir as mybir
from concourse.bass_utils import run_bass_kernel_spmd

F32 = mybir.dt.float32
BF16 = mybir.dt.bfloat16
U8 = mybir.dt.uint8
I32 = mybir.dt.int32
AF = mybir.ActivationFunctionType
ALU = mybir.AluOpType
AX = mybir.AxisListType.X

B, S, D = 2, 4096, 2048
H, DH, DR = 16, 128, 64
QR, KVR = 1536, 512
HID, NHC = 5632, 44
NT, NTQ = 32, 8
OWN = 1024
SCALE = float((DH + DR) ** -0.5)
EPS = 1e-5
TWO_PI = 2 * np.pi
CW0 = float(np.float32(6.28125))
CW1 = float(np.float32(TWO_PI - 6.28125))
CW2 = float(TWO_PI - np.float64(np.float32(6.28125)) - np.float64(np.float32(TWO_PI - 6.28125)))
MAGIC = float(np.float32(1.5 * 2 ** 23))

DEBUG = False


class Sem:
    __slots__ = ("h", "n")

    def __init__(self, h):
        self.h = h
        self.n = 0

    def inc(self, instr, k=1):
        instr.then_inc(self.h, k)
        self.n += k
        return self.n

    def dma(self, instr):
        return self.inc(instr, 16)

    def wait(self, eng, n=None):
        eng.wait_ge(self.h, self.n if n is None else n)


def build_program(debug=False):
    nc = bass.Bass("TRN2", target_bir_lowering=False)

    xb = nc.dram_tensor("xb", [S, D], F32, kind="ExternalInput").ap()
    wqd = nc.dram_tensor("wqd", [D, QR], F32, kind="ExternalInput").ap()
    wqu = nc.dram_tensor("wqu", [QR, D], F32, kind="ExternalInput").ap()
    wqr = nc.dram_tensor("wqr", [QR, H * DR], F32, kind="ExternalInput").ap()
    wkd = nc.dram_tensor("wkd", [D, KVR], F32, kind="ExternalInput").ap()
    wku = nc.dram_tensor("wku", [KVR, D], F32, kind="ExternalInput").ap()
    wvu = nc.dram_tensor("wvu", [KVR, D], F32, kind="ExternalInput").ap()
    wkr = nc.dram_tensor("wkr", [KVR, H * DR], F32, kind="ExternalInput").ap()
    wo = nc.dram_tensor("wo", [D, D], F32, kind="ExternalInput").ap()
    qg2 = nc.dram_tensor("qg2", [128, 12], F32, kind="ExternalInput").ap()
    qb2 = nc.dram_tensor("qb2", [128, 12], F32, kind="ExternalInput").ap()
    kg2 = nc.dram_tensor("kg2", [128, 4], F32, kind="ExternalInput").ap()
    kb2 = nc.dram_tensor("kb2", [128, 4], F32, kind="ExternalInput").ap()
    posP = nc.dram_tensor("posP", [128, NT], F32, kind="ExternalInput").ap()
    fg = nc.dram_tensor("fg", [NHC, 128, 16, 128], U8, kind="ExternalInput").ap()
    fu = nc.dram_tensor("fu", [NHC, 128, 16, 128], U8, kind="ExternalInput").ap()
    fd = nc.dram_tensor("fd", [NHC, 128, D], U8, kind="ExternalInput").ap()
    sg2 = nc.dram_tensor("sg2", [128, NHC], F32, kind="ExternalInput").ap()
    su2 = nc.dram_tensor("su2", [128, NHC], F32, kind="ExternalInput").ap()
    sdr = nc.dram_tensor("sdr", [1, D], F32, kind="ExternalInput").ap()

    y = nc.dram_tensor("y", [OWN, D], F32, kind="ExternalOutput").ap()

    kr_tm = nc.dram_tensor("kr_tm", [S, H * DR], BF16, kind="Internal").ap()
    qc_hbm = nc.dram_tensor("qc_hbm", [H, 128, OWN], BF16, kind="Internal").ap()
    qr_hbm = nc.dram_tensor("qr_hbm", [8, 128, OWN], BF16, kind="Internal").ap()
    ffn_y = nc.dram_tensor("ffn_y", [OWN, D], F32, kind="Internal").ap()
    bb = nc.dram_tensor("bb", [4, D], F32, kind="Internal").ap()

    taps = {}
    if debug:
        taps["t_xn"] = nc.dram_tensor("t_xn", [128, 16 * OWN], F32, kind="ExternalOutput").ap()
        taps["t_ckv"] = nc.dram_tensor("t_ckv", [128, 4 * S], F32, kind="ExternalOutput").ap()
        taps["t_cq"] = nc.dram_tensor("t_cq", [128, 12 * OWN], F32, kind="ExternalOutput").ap()
        taps["t_qr"] = nc.dram_tensor("t_qr", [128, 8 * OWN], F32, kind="ExternalOutput").ap()
        taps["t_ffn"] = nc.dram_tensor("t_ffn", [OWN, D], F32, kind="ExternalOutput").ap()
        taps["t_ofm"] = nc.dram_tensor("t_ofm", [128, H * OWN], F32, kind="ExternalOutput").ap()
        taps["t_cos"] = nc.dram_tensor("t_cos", [128, NT * 32], F32, kind="ExternalOutput").ap()
        taps["t_ifr"] = nc.dram_tensor("t_ifr", [1, 32], F32, kind="ExternalOutput").ap()

    stx = contextlib.ExitStack()
    E = stx.enter_context

    sy = Sem(E(nc.semaphore("sy")))
    gd = Sem(E(nc.semaphore("gd")))
    pe = Sem(E(nc.semaphore("pe")))
    dv = Sem(E(nc.semaphore("dv")))
    ac = Sem(E(nc.semaphore("ac")))
    gp = Sem(E(nc.semaphore("gp")))
    sems = (sy, gd, pe, dv, ac, gp)

    block = E(nc.Block())

    def barrier():
        snap = [(s.h, s.n) for s in sems]
        for name in ("sync", "vector", "scalar", "tensor", "gpsimd"):
            def f(e, snap=snap):
                for h, n in snap:
                    e.wait_ge(h, n)
            getattr(block, name)(f)

    def dump_tap(src_ap_fn, dst_ap, nchunks, chunk_cols, scratch):
        for a in range(nchunks):
            @block.scalar
            def _(s, a=a):
                ac.inc(s.copy(scratch[:, 0:chunk_cols], src_ap_fn(a)))

            @block.sync
            def _(sync, a=a):
                ac.wait(sync)
                sy.dma(sync.dma_start(out=dst_ap[:, a * chunk_cols:(a + 1) * chunk_cols],
                                      in_=scratch[:, 0:chunk_cols]))
            barrier()

    P_ident = E(nc.sbuf_tensor("P_ident", [128, 128], BF16))
    P_identf = E(nc.sbuf_tensor("P_identf", [128, 128], F32))
    P_ones = E(nc.sbuf_tensor("P_ones", [1, 128], F32))
    P_cos = E(nc.sbuf_tensor("P_cos", [128, NT, 32], F32))
    P_sin = E(nc.sbuf_tensor("P_sin", [128, NT, 32], F32))
    P_qg = E(nc.sbuf_tensor("P_qg", [128, 12], F32))
    P_qb = E(nc.sbuf_tensor("P_qb", [128, 12], F32))
    P_kg = E(nc.sbuf_tensor("P_kg", [128, 4], F32))
    P_kb = E(nc.sbuf_tensor("P_kb", [128, 4], F32))
    P_qbb = E(nc.sbuf_tensor("P_qbb", [128, 12], BF16))
    P_kbb = E(nc.sbuf_tensor("P_kbb", [128, 4], BF16))
    P_sg = E(nc.sbuf_tensor("P_sg", [128, NHC], F32))
    P_su = E(nc.sbuf_tensor("P_su", [128, NHC], F32))
    P_qcb = E(nc.sbuf_tensor("P_qcb", [128, 16], F32))
    P_vb = E(nc.sbuf_tensor("P_vb", [128, 16], F32))

    ckv_fm = E(nc.sbuf_tensor("ckv_fm", [128, 4, S], BF16))
    ST1 = E(nc.sbuf_tensor("ST1", [128, 16], F32))
    ST2 = E(nc.sbuf_tensor("ST2", [128, 8], F32))
    stack_a = contextlib.ExitStack()
    xn_fm = stack_a.enter_context(nc.sbuf_tensor("xn_fm", [128, 16, OWN], BF16))
    SQ = stack_a.enter_context(nc.sbuf_tensor("SQ", [128, 2048], F32))

    @block.sync
    def _(sync):
        sy.dma(sync.dma_start(out=P_qg[:], in_=qg2[:, :]))
        sy.dma(sync.dma_start(out=P_qb[:], in_=qb2[:, :]))
        sy.dma(sync.dma_start(out=P_kg[:], in_=kg2[:, :]))
        sy.dma(sync.dma_start(out=P_kb[:], in_=kb2[:, :]))
        sy.dma(sync.dma_start(out=P_sg[:], in_=sg2[:, :]))
        sy.dma(sync.dma_start(out=P_su[:], in_=su2[:, :]))
    ld0 = sy.n

    with nc.sbuf_tensor("T_pos", [128, NT], F32) as T_pos, \
         nc.sbuf_tensor("T_ioti", [1, 32], I32) as T_ioti, \
         nc.sbuf_tensor("T_iot", [1, 32], F32) as T_iot, \
         nc.sbuf_tensor("T_g", [1, 32], F32) as T_g, \
         nc.sbuf_tensor("T_m", [1, 32], F32) as T_m, \
         nc.sbuf_tensor("T_f", [1, 32], F32) as T_f, \
         nc.sbuf_tensor("T_ifb", [128, 32], F32) as T_ifb, \
         nc.sbuf_tensor("T_th", [128, NT, 32], F32) as T_th, \
         nc.sbuf_tensor("T_k", [128, NT, 32], F32) as T_k, \
         nc.sbuf_tensor("T_w", [128, NT, 32], F32) as T_w, \
         nc.psum_tensor("T_ps", [128, 512], F32) as T_ps:

        @block.sync
        def _(sync):
            sy.dma(sync.dma_start(out=T_pos[:], in_=posP[:, :]))
        ld1 = sy.n

        @block.gpsimd
        def _(g):
            g.memset(P_ident[:], 0.0)
            g.affine_select(out=P_ident[:], in_=P_ident[:], compare_op=ALU.not_equal,
                            fill=1.0, base=0, pattern=[[-1, 128]], channel_multiplier=1)
            g.memset(P_identf[:], 0.0)
            g.affine_select(out=P_identf[:], in_=P_identf[:], compare_op=ALU.not_equal,
                            fill=1.0, base=0, pattern=[[-1, 128]], channel_multiplier=1)
            g.memset(P_ones[:], 1.0)
            g.iota(T_ioti[:], pattern=[[1, 32]], base=0, channel_multiplier=0)
            g.tensor_copy(T_iot[:], T_ioti[:])
            gp.inc(g.memset(T_m[:], float(-np.log(10000.0) / 32.0)))

        @block.scalar
        def _(s):
            gp.wait(s)
            ac.inc(s.activation(T_g[:], T_m[:], AF.Exp))

        @block.vector
        def _(v):
            ac.wait(v)
            v.tensor_tensor(T_m[:], T_g[:], T_g[:], ALU.mult)
            for _sq in range(4):
                v.drain()
                v.tensor_tensor(T_m[:], T_m[:], T_m[:], ALU.mult)
            v.drain()
            v.tensor_scalar(T_m[:], T_m[:], -10000.0, 1.0, ALU.mult, ALU.add)
            v.drain()
            v.tensor_scalar(T_m[:], T_m[:], float(1.0 / 32.0), 1.0, ALU.mult, ALU.add)
            v.drain()
            v.tensor_tensor(T_g[:], T_g[:], T_m[:], ALU.mult)
            v.drain()
            v.memset(T_f[:], 1.0)
            for k in range(5):
                b = 2 ** k
                sl = T_f[0:1, :].rearrange("p (a b) -> p a b", b=2 * b)[:, :, b:2 * b]
                v.tensor_scalar_mul(sl, sl, T_g[0:1, 0:1])
                v.drain()
                if k < 4:
                    v.tensor_tensor(T_g[:], T_g[:], T_g[:], ALU.mult)
                    v.drain()
            dv.inc(v.drain())

        @block.tensor
        def _(t):
            dv.wait(t)
            pe.inc(t.matmul(T_ps[:, 0:32], P_ones[:], T_f[:], start=True, stop=True))

        @block.vector
        def _(v):
            pe.wait(v)
            v.tensor_copy(T_ifb[:], T_ps[:, 0:32])
            sy.wait(v, ld1)
            v.drain()
            for t in range(NT):
                v.tensor_scalar_mul(T_th[:, t], T_ifb[:], T_pos[:, t:t + 1])
            v.drain()
            TH = T_th[:].rearrange("p a c -> p (a c)")
            K_ = T_k[:].rearrange("p a c -> p (a c)")
            W_ = T_w[:].rearrange("p a c -> p (a c)")
            v.tensor_scalar(K_, TH, float(1.0 / TWO_PI), MAGIC, ALU.mult, ALU.add)
            v.drain()
            v.tensor_scalar_add(K_, K_, -MAGIC)
            v.drain()
            v.scalar_tensor_tensor(W_, K_, -CW0, TH, ALU.mult, ALU.add)
            v.drain()
            v.scalar_tensor_tensor(W_, K_, -CW1, W_, ALU.mult, ALU.add)
            v.drain()
            v.scalar_tensor_tensor(W_, K_, -CW2, W_, ALU.mult, ALU.add)
            v.drain()
            v.tensor_scalar_min(W_, W_, float(np.pi))
            v.drain()
            v.tensor_scalar_max(W_, W_, float(-np.pi))
            v.drain()
            v.tensor_scalar_add(K_, W_, float(np.pi / 2))
            v.drain()
            v.tensor_scalar(TH, K_, float(np.pi), None, ALU.is_gt)
            v.drain()
            v.scalar_tensor_tensor(K_, TH, float(-TWO_PI), K_, ALU.mult, ALU.add)
            dv.inc(v.drain())

        @block.scalar
        def _(s):
            dv.wait(s)
            s.activation(P_sin[:].rearrange("p a c -> p (a c)"),
                         T_w[:].rearrange("p a c -> p (a c)"), AF.Sin)
            ac.inc(s.activation(P_cos[:].rearrange("p a c -> p (a c)"),
                                T_k[:].rearrange("p a c -> p (a c)"), AF.Sin))

        if debug:
            @block.sync
            def _(sync):
                ac.wait(sync)
                dv.wait(sync)
                sy.dma(sync.dma_start(out=taps["t_cos"][:, :],
                                      in_=P_cos[:].rearrange("p a c -> p (a c)")))
                sy.dma(sync.dma_start(out=taps["t_ifr"][:, :], in_=T_f[:]))
        barrier()

    def emit_ln(src_ap, n_feat, sq_ap, stats, out_ap):
        @block.vector
        def _(v):
            v.reduce_sum(stats[:, 0:1], src_ap, axis=AX)
            v.drain()
            v.tensor_scalar_mul(stats[:, 1:2], stats[:, 0:1], float(-1.0 / n_feat))
            dv.inc(v.drain())

        @block.scalar
        def _(s):
            dv.wait(s)
            ac.inc(s.activation(sq_ap, src_ap, AF.Square, bias=stats[:, 1:2],
                                accum_out=stats[:, 2:3]))

        @block.vector
        def _(v):
            ac.wait(v)
            v.tensor_scalar(stats[:, 3:4], stats[:, 2:3], float(1.0 / n_feat), EPS,
                            ALU.mult, ALU.add)
            dv.inc(v.drain())

        @block.scalar
        def _(s):
            dv.wait(s)
            ac.inc(s.activation(stats[:, 4:5], stats[:, 3:4], AF.Sqrt))

        @block.vector
        def _(v):
            ac.wait(v)
            v.reciprocal(stats[:, 5:6], stats[:, 4:5])
            v.drain()
            v.tensor_tensor(stats[:, 6:7], stats[:, 1:2], stats[:, 5:6], ALU.mult)
            dv.inc(v.drain())

        @block.scalar
        def _(s):
            dv.wait(s)
            ac.inc(s.activation(out_ap, src_ap, AF.Identity,
                                bias=stats[:, 6:7], scale=stats[:, 5:6]))

    with nc.sbuf_tensor("XT0", [128, D], F32) as XT0, \
         nc.sbuf_tensor("XT1", [128, D], F32) as XT1, \
         nc.sbuf_tensor("XNB0", [128, D], BF16) as XNB0, \
         nc.sbuf_tensor("XNB1", [128, D], BF16) as XNB1, \
         nc.psum_tensor("TRP", [128, 2048], BF16) as TRP:

        XT, XNB = [XT0, XT1], [XNB0, XNB1]
        x_done, evac_done = {}, {}
        for t in range(NTQ):
            bi = t % 2

            @block.sync
            def _(sync, t=t, bi=bi):
                if t >= 2:
                    ac.wait(sync, x_done[t - 2])
                sy.dma(sync.dma_start(out=XT[bi][:], in_=xb[t * 128:(t + 1) * 128, :]))
            ld = sy.n

            @block.vector
            def _(v, ld=ld):
                sy.wait(v, ld)
            emit_ln(XT[bi][:], D, SQ[:], ST1[:, 8 * bi:8 * bi + 8], XNB[bi][:])
            x_done[t] = ac.n

            @block.tensor
            def _(t_, t=t, bi=bi):
                ac.wait(t_)
                if t >= 1:
                    dv.wait(t_, evac_done[t - 1])
                for fc in range(16):
                    ins = t_.transpose(TRP[:, fc * 128:(fc + 1) * 128],
                                       XNB[bi][:, fc * 128:(fc + 1) * 128], P_ident[:])
                pe.inc(ins)

            @block.vector
            def _(v, t=t):
                pe.wait(v)
                dv.inc(v.tensor_copy(xn_fm[:, :, t * 128:(t + 1) * 128],
                                     TRP[:].rearrange("p (a c) -> p a c", c=128)))
                evac_done[t] = dv.n
        barrier()

    if debug:
        with nc.sbuf_tensor("DBG0", [128, 2048], F32) as DBG0:
            dump_tap(lambda a: xn_fm[:].rearrange("p a c -> p (a c)")[:, a * 2048:(a + 1) * 2048],
                     taps["t_xn"], 8, 2048, DBG0)

    NQ4 = 4
    QHC = NHC // NQ4
    with nc.sbuf_tensor("SDB", [128, D], F32) as SDB, \
         nc.sbuf_tensor("SDROW", [1, D], F32) as SDROW, \
         nc.sbuf_tensor("G8", [128, D], U8) as G8, \
         nc.sbuf_tensor("U8T", [128, D], U8) as U8T, \
         nc.sbuf_tensor("D8", [128, D], U8) as D8, \
         nc.sbuf_tensor("GW", [128, 16, 128], BF16) as GW, \
         nc.sbuf_tensor("UW", [128, 16, 128], BF16) as UW, \
         nc.sbuf_tensor("SILU", [128, OWN], BF16) as SILU, \
         nc.sbuf_tensor("MQ", [128, QHC, OWN], BF16) as MQ, \
         nc.sbuf_tensor("DWQ", [128, QHC, D], BF16) as DWQ, \
         nc.sbuf_tensor("YSB", [128, D], F32) as YSB, \
         nc.sbuf_tensor("YOLD", [128, D], F32) as YOLD, \
         nc.psum_tensor("G_ps", [128, 1024], F32) as G_ps, \
         nc.psum_tensor("U_ps", [128, 1024], F32) as U_ps, \
         nc.psum_tensor("Y_ps", [128, 2048], F32) as Y_ps:

        @block.sync
        def _(sync):
            sy.dma(sync.dma_start(out=SDROW[:], in_=sdr[:, :]))
        ldsd = sy.n

        @block.tensor
        def _(t):
            sy.wait(t, ldsd)
            for j in range(4):
                ins = t.matmul(Y_ps[:, j * 512:(j + 1) * 512], P_ones[:],
                               SDROW[:, j * 512:(j + 1) * 512], start=True, stop=True)
            pe.inc(ins)

        @block.vector
        def _(v):
            pe.wait(v)
            dv.inc(v.tensor_copy(SDB[:], Y_ps[:]))
        sdb_done = dv.n

        fold_done, gu_done, silu_done, m_done = {}, {}, {}, {}
        down_done, yevac_done, dfold_done = {}, {}, {}
        for q4 in range(NQ4):
            for i in range(QHC):
                hc = q4 * QHC + i

                @block.gpsimd
                def _(g, hc=hc, i=i, q4=q4):
                    if hc == 0:
                        dv.wait(g, sdb_done)
                    else:
                        gp.wait(g)
                        dv.wait(g, dfold_done[hc - 1])
                        pe.wait(g, gu_done[hc - 1])
                        if i == 0:
                            pe.wait(g, down_done[q4 - 1])
                    gd.dma(g.dma_start(out=G8[:], in_=fg[hc].rearrange("p a c -> p (a c)")))
                    gd.dma(g.dma_start(out=U8T[:], in_=fu[hc].rearrange("p a c -> p (a c)")))
                    gd.dma(g.dma_start(out=D8[:], in_=fd[hc]))
                    g.wait_ge(gd.h, gd.n)
                    g.tensor_scalar_add(GW[:].rearrange("p a c -> p (a c)"), G8[:], -128.0)
                    gp.inc(g.tensor_scalar_add(UW[:].rearrange("p a c -> p (a c)"), U8T[:], -128.0))
                    fold_done[hc] = gp.n

                @block.vector
                def _(v, hc=hc, i=i):
                    gp.wait(v, fold_done[hc])
                    dv.inc(v.scalar_tensor_tensor(DWQ[:, i], D8[:], -128.0, SDB[:],
                                                  ALU.add, ALU.mult))
                    dfold_done[hc] = dv.n

                @block.tensor
                def _(t, hc=hc):
                    gp.wait(t, fold_done[hc])
                    if hc > 0:
                        ac.wait(t, silu_done[hc - 1])
                        dv.wait(t, m_done[hc - 1])
                    for dc in range(16):
                        for j in range(2):
                            t.matmul(G_ps[:, j * 512:(j + 1) * 512], GW[:, dc],
                                     xn_fm[:, dc, j * 512:(j + 1) * 512],
                                     start=(dc == 0), stop=(dc == 15))
                    for dc in range(16):
                        for j in range(2):
                            ins = t.matmul(U_ps[:, j * 512:(j + 1) * 512], UW[:, dc],
                                           xn_fm[:, dc, j * 512:(j + 1) * 512],
                                           start=(dc == 0), stop=(dc == 15))
                    pe.inc(ins)
                    gu_done[hc] = pe.n

                @block.scalar
                def _(s, hc=hc):
                    pe.wait(s, gu_done[hc])
                    if hc > 0:
                        dv.wait(s, m_done[hc - 1])
                    ac.inc(s.activation(SILU[:], G_ps[:], AF.Silu, scale=P_sg[:, hc:hc + 1]))
                    silu_done[hc] = ac.n

                @block.vector
                def _(v, hc=hc, i=i):
                    ac.wait(v, silu_done[hc])
                    dv.inc(v.scalar_tensor_tensor(MQ[:, i], U_ps[:], P_su[:, hc:hc + 1],
                                                  SILU[:], ALU.mult, ALU.mult))
                    m_done[hc] = dv.n

            for tt in range(NTQ):
                @block.tensor
                def _(t, q4=q4, tt=tt):
                    dv.wait(t, m_done[q4 * QHC + QHC - 1])
                    dv.wait(t, dfold_done[q4 * QHC + QHC - 1])
                    if q4 > 0 or tt > 0:
                        key = (q4, tt - 1) if tt > 0 else (q4 - 1, NTQ - 1)
                        s_, n_ = yevac_done[key]
                        s_.wait(t, n_)
                    for i in range(QHC):
                        for j in range(4):
                            ins = t.matmul(Y_ps[:, j * 512:(j + 1) * 512],
                                           MQ[:, i, tt * 128:(tt + 1) * 128],
                                           DWQ[:, i, j * 512:(j + 1) * 512],
                                           start=(i == 0), stop=(i == QHC - 1))
                    pe.inc(ins)
                    if tt == NTQ - 1:
                        down_done[q4] = pe.n

                if q4 == 0:
                    @block.scalar
                    def _(s, tt=tt):
                        pe.wait(s)
                        ac.inc(s.copy(YSB[:], Y_ps[:]))
                        yevac_done[(0, tt)] = (ac, ac.n)

                    @block.sync
                    def _(sync, tt=tt):
                        ac.wait(sync)
                        sy.dma(sync.dma_start(out=ffn_y[tt * 128:(tt + 1) * 128, :], in_=YSB[:]))
                else:
                    @block.sync
                    def _(sync, q4=q4, tt=tt):
                        if not (q4 == 1 and tt == 0):
                            dv.wait(sync)
                        sy.dma(sync.dma_start(out=YOLD[:], in_=ffn_y[tt * 128:(tt + 1) * 128, :]))
                    ldy = sy.n

                    @block.vector
                    def _(v, ldy=ldy, q4=q4, tt=tt):
                        sy.wait(v, ldy)
                        pe.wait(v)
                        dv.inc(v.tensor_tensor(YSB[:], Y_ps[:], YOLD[:], ALU.add))
                        yevac_done[(q4, tt)] = (dv, dv.n)

                    @block.sync
                    def _(sync, tt=tt):
                        dv.wait(sync)
                        sy.dma(sync.dma_start(out=ffn_y[tt * 128:(tt + 1) * 128, :], in_=YSB[:]))
        barrier()

    if debug:
        with nc.sbuf_tensor("DBGF", [128, D], F32) as DBGF:
            for tt in range(NTQ):
                @block.sync
                def _(sync, tt=tt):
                    sy.dma(sync.dma_start(out=DBGF[:], in_=ffn_y[tt * 128:(tt + 1) * 128, :]))
                    sy.wait(sync)
                    sy.dma(sync.dma_start(out=taps["t_ffn"][tt * 128:(tt + 1) * 128, :], in_=DBGF[:]))
                barrier()

    stack_b = contextlib.ExitStack()
    cq_fm = stack_b.enter_context(nc.sbuf_tensor("cq_fm", [128, 12, OWN], BF16))
    with nc.sbuf_tensor("WQD", [128, 16, QR], BF16) as WQD, \
         nc.sbuf_tensor("CQT", [128, QR], BF16) as CQT, \
         nc.psum_tensor("CQ_ps", [128, QR], F32) as CQ_ps, \
         nc.psum_tensor("TR3", [128, QR], BF16) as TR3:

        @block.gpsimd
        def _(g):
            for a in range(4):
                gd.dma(g.dma_start(out=WQD[:, 4 * a:4 * (a + 1)],
                                   in_=wqd[a * 512:(a + 1) * 512, :].rearrange("(a p) c -> p a c", p=128)))
        ldw = gd.n

        cq_evac, cqt_done = {}, {}
        for t in range(NTQ):
            @block.tensor
            def _(t_, t=t):
                if t == 0:
                    gd.wait(t_, ldw)
                else:
                    ac.wait(t_, cqt_done[t - 1])
                for dc in range(16):
                    for j in range(3):
                        ins = t_.matmul(CQ_ps[:, j * 512:(j + 1) * 512],
                                        xn_fm[:, dc, t * 128:(t + 1) * 128],
                                        WQD[:, dc, j * 512:(j + 1) * 512],
                                        start=(dc == 0), stop=(dc == 15))
                pe.inc(ins)

            @block.vector
            def _(v):
                pe.wait(v)
            emit_ln(CQ_ps[:], QR, SQ[:, 0:QR], ST2, CQT[:])
            cqt_done[t] = ac.n

            @block.tensor
            def _(t_, t=t):
                ac.wait(t_, cqt_done[t])
                if t >= 1:
                    dv.wait(t_, cq_evac[t - 1])
                for rc in range(12):
                    ins = t_.transpose(TR3[:, rc * 128:(rc + 1) * 128],
                                       CQT[:, rc * 128:(rc + 1) * 128], P_ident[:])
                pe.inc(ins)

            @block.vector
            def _(v, t=t):
                pe.wait(v)
                dv.inc(v.tensor_copy(cq_fm[:, :, t * 128:(t + 1) * 128],
                                     TR3[:].rearrange("p (a c) -> p a c", c=128)))
                cq_evac[t] = dv.n
        barrier()

    if debug:
        with nc.sbuf_tensor("DBG1", [128, 2048], F32) as DBG1:
            dump_tap(lambda a: cq_fm[:].rearrange("p a c -> p (a c)")[:, a * 2048:(a + 1) * 2048],
                     taps["t_cq"], 6, 2048, DBG1)

    with nc.sbuf_tensor("WQU", [128, 12, D], BF16) as WQU, \
         nc.sbuf_tensor("BROW", [1, D], F32) as BROW, \
         nc.sbuf_tensor("B16", [16, 128], F32) as B16, \
         nc.sbuf_tensor("QCS", [128, OWN], BF16) as QCS, \
         nc.psum_tensor("QP", [128, 1024], F32) as QP, \
         nc.psum_tensor("BP", [128, 2048], F32) as BP:

        @block.gpsimd
        def _(g):
            for a in range(3):
                gd.dma(g.dma_start(out=WQU[:, 4 * a:4 * (a + 1)],
                                   in_=wqu[a * 512:(a + 1) * 512, :].rearrange("(a p) c -> p a c", p=128)))
        ldw3 = gd.n

        @block.vector
        def _(v):
            sy.wait(v, ld0)
            gd.wait(v, ldw3)
            v.tensor_copy(P_qbb[:], P_qb[:])
            dv.inc(v.tensor_copy(P_kbb[:], P_kb[:]))
        beta_done = dv.n

        @block.tensor
        def _(t):
            dv.wait(t, beta_done)
            for rc in range(12):
                for j in range(4):
                    ins = t.matmul(BP[0:1, j * 512:(j + 1) * 512],
                                   P_qbb[:, rc:rc + 1], WQU[:, rc, j * 512:(j + 1) * 512],
                                   start=(rc == 0), stop=(rc == 11))
            pe.inc(ins)

        @block.scalar
        def _(s):
            pe.wait(s)
            ac.inc(s.activation(BROW[:], BP[0:1, :], AF.Identity, scale=SCALE))

        @block.sync
        def _(sync):
            ac.wait(sync)
            sy.dma(sync.dma_start(out=bb[0:1, :], in_=BROW[:]))
            sy.wait(sync)
            sy.dma(sync.dma_start(out=B16[:], in_=bb[0, :].rearrange("(a c) -> a c", c=128)))
        ldb = sy.n

        @block.tensor
        def _(t):
            sy.wait(t, ldb)
            pe.inc(t.transpose(BP[:, 0:16], B16[:], P_identf[0:16, 0:16]))

        @block.vector
        def _(v):
            pe.wait(v)
            dv.inc(v.tensor_copy(P_qcb[:], BP[:, 0:16]))
        qcb_done = dv.n

        @block.vector
        def _(v):
            dv.wait(v, qcb_done)
            dv.inc(v.tensor_scalar_mul(ST1[:, 0:12], P_qg[:], SCALE))
        qgs_done = dv.n

        @block.gpsimd
        def _(g):
            dv.wait(g, qgs_done)
            for rc in range(12):
                ins = g.tensor_scalar_mul(WQU[:, rc], WQU[:, rc], ST1[:, rc:rc + 1])
            gp.inc(ins)
        fold_q = gp.n

        qc_done, qcs_st = {}, {}
        for h in range(H):
            @block.tensor
            def _(t, h=h):
                gp.wait(t, fold_q)
                if h > 0:
                    ac.wait(t, qc_done[h - 1])
                for rc in range(12):
                    for j in range(2):
                        ins = t.matmul(QP[:, j * 512:(j + 1) * 512],
                                       WQU[:, rc, h * 128:(h + 1) * 128],
                                       cq_fm[:, rc, j * 512:(j + 1) * 512],
                                       start=(rc == 0), stop=(rc == 11))
                pe.inc(ins)

            @block.scalar
            def _(s, h=h):
                pe.wait(s)
                if h > 0:
                    sy.wait(s, qcs_st[h - 1])
                ac.inc(s.activation(QCS[:], QP[:], AF.Identity, bias=P_qcb[:, h:h + 1]))
                qc_done[h] = ac.n

            @block.sync
            def _(sync, h=h):
                ac.wait(sync, qc_done[h])
                sy.dma(sync.dma_start(out=qc_hbm[h], in_=QCS[:]))
                qcs_st[h] = sy.n

        barrier()

    with nc.sbuf_tensor("WQR", [128, 12, H * DR], BF16) as WQR, \
         nc.sbuf_tensor("P_qrb", [128, H * DR], F32) as P_qrb, \
         nc.sbuf_tensor("BROWr", [1, H * DR], F32) as BROWr, \
         nc.sbuf_tensor("QRT", [128, OWN], F32) as QRT, \
         nc.sbuf_tensor("RTMP", [128, OWN], F32) as RTMP, \
         nc.sbuf_tensor("RT2", [128, 512], F32) as RT2, \
         nc.sbuf_tensor("QRB", [128, OWN], BF16) as QRB, \
         nc.sbuf_tensor("QRE", [128, OWN], BF16) as QRE, \
         nc.psum_tensor("QPr", [128, 1024], F32) as QP, \
         nc.psum_tensor("BPr", [128, 1024], F32) as BPr, \
         nc.psum_tensor("TR4r", [128, 1024], BF16) as TR4:

        @block.gpsimd
        def _(g):
            for a in range(3):
                gd.dma(g.dma_start(out=WQR[:, 4 * a:4 * (a + 1)],
                                   in_=wqr[a * 512:(a + 1) * 512, :].rearrange("(a p) c -> p a c", p=128)))
        ldw3r = gd.n

        @block.tensor
        def _(t):
            gd.wait(t, ldw3r)
            for rc in range(12):
                for j in range(2):
                    ins = t.matmul(BPr[0:1, j * 512:(j + 1) * 512],
                                   P_qbb[:, rc:rc + 1], WQR[:, rc, j * 512:(j + 1) * 512],
                                   start=(rc == 0), stop=(rc == 11))
            pe.inc(ins)

        @block.scalar
        def _(s):
            pe.wait(s)
            ac.inc(s.activation(BROWr[:], BPr[0:1, :], AF.Identity, scale=SCALE))

        @block.tensor
        def _(t):
            ac.wait(t)
            for j in range(2):
                ins = t.matmul(BPr[:, j * 512:(j + 1) * 512], P_ones[:],
                               BROWr[:, j * 512:(j + 1) * 512], start=True, stop=True)
            pe.inc(ins)

        @block.vector
        def _(v):
            pe.wait(v)
            dv.inc(v.tensor_copy(P_qrb[:], BPr[:]))
        qrb_done = dv.n

        @block.gpsimd
        def _(g):
            dv.wait(g, qrb_done)
            for rc in range(12):
                ins = g.tensor_scalar_mul(WQR[:, rc], WQR[:, rc], ST1[:, rc:rc + 1])
            gp.inc(ins)
        fold_qr = gp.n

        qr_evac, qre_st = {}, {}
        for tt in range(NTQ):
            @block.tensor
            def _(t, tt=tt):
                gp.wait(t, fold_qr)
                if tt > 0:
                    dv.wait(t, qr_evac[tt - 1])
                for rc in range(12):
                    for j in range(2):
                        ins = t.matmul(QP[:, j * 512:(j + 1) * 512],
                                       cq_fm[:, rc, tt * 128:(tt + 1) * 128],
                                       WQR[:, rc, j * 512:(j + 1) * 512],
                                       start=(rc == 0), stop=(rc == 11))
                pe.inc(ins)

            @block.vector
            def _(v, tt=tt):
                pe.wait(v)
                v.tensor_tensor(QRT[:], QP[:], P_qrb[:], ALU.add)
                v.drain()
                q4 = QRT[:].rearrange("p (h d f) -> p h d f", h=16, d=2)
                t4 = RTMP[:].rearrange("p (h d f) -> p h d f", h=16, d=2)
                o4 = QRB[:].rearrange("p (h d f) -> p h d f", h=16, d=2)
                cosb = P_cos[:, tt, None, None, :].broadcast_to((128, 16, 2, 32))
                sinh = P_sin[:, tt, None, :].broadcast_to((128, 16, 32))
                r2 = RT2[:].rearrange("p (h f) -> p h f", h=16)
                v.tensor_tensor(t4, q4, cosb, ALU.mult)
                v.scalar_tensor_tensor(r2, q4[:, :, 1, :], -1.0, sinh, ALU.mult, ALU.mult)
                v.drain()
                v.tensor_tensor(o4[:, :, 0, :], t4[:, :, 0, :], r2, ALU.add)
                v.drain()
                v.tensor_tensor(r2, q4[:, :, 0, :], sinh, ALU.mult)
                v.drain()
                v.tensor_tensor(o4[:, :, 1, :], t4[:, :, 1, :], r2, ALU.add)
                dv.inc(v.drain())

            @block.tensor
            def _(t, tt=tt):
                dv.wait(t)
                if tt >= 1:
                    dv.wait(t, qr_evac[tt - 1])
                for j in range(8):
                    ins = t.transpose(TR4[:, j * 128:(j + 1) * 128],
                                      QRB[:, j * 128:(j + 1) * 128], P_ident[:])
                pe.inc(ins)

            @block.vector
            def _(v, tt=tt):
                pe.wait(v)
                if tt > 0:
                    sy.wait(v, qre_st[tt - 1])
                dv.inc(v.tensor_copy(QRE[:], TR4[:]))
                qr_evac[tt] = dv.n

            @block.sync
            def _(sync, tt=tt):
                dv.wait(sync, qr_evac[tt])
                sy.dma(sync.dma_start(out=qr_hbm[:, :, tt * 128:(tt + 1) * 128].rearrange("a p c -> p a c"),
                                      in_=QRE[:].rearrange("p (a c) -> p a c", c=128)))
                qre_st[tt] = sy.n
        barrier()

    stack_b.close()

    if debug:
        with nc.sbuf_tensor("DBG3", [128, 2048], F32) as DBG3:
            for a in range(4):
                @block.gpsimd
                def _(g, a=a):
                    gd.dma(g.dma_start(out=DBG3[:].rearrange("p (a c) -> p a c", c=OWN), in_=qr_hbm[2 * a:2 * a + 2].rearrange("a p c -> p a c")))

                @block.sync
                def _(sync, a=a):
                    gd.wait(sync)
                    sy.dma(sync.dma_start(out=taps["t_qr"][:, a * 2048:(a + 1) * 2048], in_=DBG3[:]))
                barrier()

    with nc.sbuf_tensor("WKD", [128, 16, KVR], BF16) as WKD, \
         nc.sbuf_tensor("XT0b", [128, D], F32) as XT0b, \
         nc.sbuf_tensor("XT1b", [128, D], F32) as XT1b, \
         nc.sbuf_tensor("XNB0b", [128, D], BF16) as XNB0b, \
         nc.sbuf_tensor("XNB1b", [128, D], BF16) as XNB1b, \
         nc.sbuf_tensor("XFM0", [128, 16, 128], BF16) as XFM0, \
         nc.sbuf_tensor("XFM1", [128, 16, 128], BF16) as XFM1, \
         nc.sbuf_tensor("CKVT", [128, KVR], BF16) as CKVT, \
         nc.psum_tensor("TRPb", [128, 2048], BF16) as TRPb, \
         nc.psum_tensor("CKV_ps", [128, KVR], F32) as CKV_ps, \
         nc.psum_tensor("TR2", [128, KVR], BF16) as TR2:

        @block.gpsimd
        def _(g):
            for a in range(4):
                gd.dma(g.dma_start(out=WKD[:, 4 * a:4 * (a + 1)],
                                   in_=wkd[a * 512:(a + 1) * 512, :].rearrange("(a p) c -> p a c", p=128)))
        ldw2 = gd.n

        XTb, XNBb, XFM = [XT0b, XT1b], [XNB0b, XNB1b], [XFM0, XFM1]
        xb_done, xfm_done, ckvt_done, kv_evac = {}, {}, {}, {}
        for t in range(NT):
            bi = t % 2
            if t >= NTQ:
                @block.sync
                def _(sync, t=t, bi=bi):
                    if t >= NTQ + 2:
                        ac.wait(sync, xb_done[t - 2])
                    sy.dma(sync.dma_start(out=XTb[bi][:], in_=xb[t * 128:(t + 1) * 128, :]))
                ld = sy.n

                @block.vector
                def _(v, ld=ld):
                    sy.wait(v, ld)
                emit_ln(XTb[bi][:], D, SQ[:], ST1[:, 8 * bi:8 * bi + 8], XNBb[bi][:])
                xb_done[t] = ac.n

                @block.tensor
                def _(t_, t=t, bi=bi):
                    ac.wait(t_, xb_done[t])
                    if t >= NTQ + 2:
                        dv.wait(t_, xfm_done[t - 2])
                    for fc in range(16):
                        ins = t_.transpose(TRPb[:, fc * 128:(fc + 1) * 128],
                                           XNBb[bi][:, fc * 128:(fc + 1) * 128], P_ident[:])
                    pe.inc(ins)

                @block.vector
                def _(v, t=t, bi=bi):
                    pe.wait(v)
                    dv.inc(v.tensor_copy(XFM[bi][:], TRPb[:].rearrange("p (a c) -> p a c", c=128)))
                    xfm_done[t] = dv.n
                xsrc = XFM[bi]
            else:
                xsrc = None

            @block.tensor
            def _(t_, t=t, xsrc=xsrc):
                if t == 0:
                    gd.wait(t_, ldw2)
                else:
                    ac.wait(t_, ckvt_done[t - 1])
                if xsrc is not None:
                    dv.wait(t_, xfm_done[t])
                for dc in range(16):
                    src = xsrc[:, dc] if xsrc is not None else xn_fm[:, dc, t * 128:(t + 1) * 128]
                    ins = t_.matmul(CKV_ps[:], src, WKD[:, dc], start=(dc == 0), stop=(dc == 15))
                pe.inc(ins)

            @block.vector
            def _(v):
                pe.wait(v)
            emit_ln(CKV_ps[:], KVR, SQ[:, 0:KVR], ST2, CKVT[:])
            ckvt_done[t] = ac.n

            @block.tensor
            def _(t_, t=t):
                ac.wait(t_, ckvt_done[t])
                if t >= 1:
                    dv.wait(t_, kv_evac[t - 1])
                for rc in range(4):
                    ins = t_.transpose(TR2[:, rc * 128:(rc + 1) * 128],
                                       CKVT[:, rc * 128:(rc + 1) * 128], P_ident[:])
                pe.inc(ins)

            @block.vector
            def _(v, t=t):
                pe.wait(v)
                dv.inc(v.tensor_copy(ckv_fm[:, :, t * 128:(t + 1) * 128],
                                     TR2[:].rearrange("p (a c) -> p a c", c=128)))
                kv_evac[t] = dv.n
        barrier()

    stack_a.close()

    if debug:
        with nc.sbuf_tensor("DBG2", [128, 2048], F32) as DBG2:
            dump_tap(lambda a: ckv_fm[:].rearrange("p a c -> p (a c)")[:, a * 2048:(a + 1) * 2048],
                     taps["t_ckv"], 8, 2048, DBG2)

    o_fm = E(nc.sbuf_tensor("o_fm", [128, H, OWN], BF16))
    stack_c = contextlib.ExitStack()
    Wku = stack_c.enter_context(nc.sbuf_tensor("Wku", [128, 4, D], BF16))
    Wvu = stack_c.enter_context(nc.sbuf_tensor("Wvu", [128, 4, D], BF16))
    with nc.sbuf_tensor("WKR", [128, 4, H * DR], BF16) as WKR, \
         nc.sbuf_tensor("BROW2", [1, D], F32) as BROW2, \
         nc.sbuf_tensor("B16b", [16, 128], F32) as B16b, \
         nc.sbuf_tensor("P_krb", [128, H * DR], F32) as P_krb, \
         nc.sbuf_tensor("KRT", [128, 1024], F32) as KRT, \
         nc.sbuf_tensor("RTMPb", [128, 1024], F32) as RTMPb, \
         nc.sbuf_tensor("RT2b", [128, 512], F32) as RT2b, \
         nc.sbuf_tensor("KRB", [128, 1024], BF16) as KRB, \
         nc.psum_tensor("KP", [128, 1024], F32) as KP, \
         nc.psum_tensor("BPb", [128, 1024], F32) as BPb:

        @block.gpsimd
        def _(g):
            gd.dma(g.dma_start(out=Wku[:], in_=wku[:, :].rearrange("(a p) c -> p a c", p=128)))
            gd.dma(g.dma_start(out=Wvu[:], in_=wvu[:, :].rearrange("(a p) c -> p a c", p=128)))
            gd.dma(g.dma_start(out=WKR[:], in_=wkr[:, :].rearrange("(a p) c -> p a c", p=128)))
        ldw4 = gd.n

        vb_done = {}
        for half in range(2):
            @block.tensor
            def _(t, half=half):
                gd.wait(t, ldw4)
                if half == 1:
                    ac.wait(t, vb_done[0])
                for rc in range(4):
                    for j in range(2):
                        ins = t.matmul(BPb[0:1, j * 512:(j + 1) * 512],
                                       P_kbb[:, rc:rc + 1],
                                       Wvu[:, rc, (half * 2 + j) * 512:(half * 2 + j + 1) * 512],
                                       start=(rc == 0), stop=(rc == 3))
                pe.inc(ins)

            @block.scalar
            def _(s, half=half):
                pe.wait(s)
                ac.inc(s.copy(BROW2[:, half * 1024:(half + 1) * 1024], BPb[0:1, :]))
                vb_done[half] = ac.n

        @block.sync
        def _(sync):
            ac.wait(sync, vb_done[1])
            sy.dma(sync.dma_start(out=bb[1:2, :], in_=BROW2[:]))
            sy.wait(sync)
            sy.dma(sync.dma_start(out=B16b[:], in_=bb[1, :].rearrange("(a c) -> a c", c=128)))
        ldb2 = sy.n

        @block.tensor
        def _(t):
            sy.wait(t, ldb2)
            pe.inc(t.transpose(BPb[:, 0:16], B16b[:], P_identf[0:16, 0:16]))

        @block.vector
        def _(v):
            pe.wait(v)
            dv.inc(v.tensor_copy(P_vb[:], BPb[:, 0:16]))
        pvb_done = dv.n

        @block.tensor
        def _(t):
            dv.wait(t, pvb_done)
            for rc in range(4):
                for j in range(2):
                    ins = t.matmul(BPb[0:1, j * 512:(j + 1) * 512],
                                   P_kbb[:, rc:rc + 1], WKR[:, rc, j * 512:(j + 1) * 512],
                                   start=(rc == 0), stop=(rc == 3))
            pe.inc(ins)

        @block.scalar
        def _(s):
            pe.wait(s)
            ac.inc(s.copy(BROW2[:, 0:1024], BPb[0:1, :]))

        @block.tensor
        def _(t):
            ac.wait(t)
            for j in range(2):
                ins = t.matmul(BPb[:, j * 512:(j + 1) * 512], P_ones[:],
                               BROW2[:, j * 512:(j + 1) * 512], start=True, stop=True)
            pe.inc(ins)

        @block.vector
        def _(v):
            pe.wait(v)
            dv.inc(v.tensor_copy(P_krb[:], BPb[:]))
        krb_done = dv.n

        @block.gpsimd
        def _(g):
            dv.wait(g, krb_done)
            for rc in range(4):
                g.tensor_scalar_mul(Wku[:, rc], Wku[:, rc], P_kg[:, rc:rc + 1])
            for rc in range(4):
                g.tensor_scalar_mul(Wvu[:, rc], Wvu[:, rc], P_kg[:, rc:rc + 1])
            for rc in range(4):
                ins = g.tensor_scalar_mul(WKR[:, rc], WKR[:, rc], P_kg[:, rc:rc + 1])
            gp.inc(ins)
        fold_k = gp.n

        kr_done, kr_st = {}, {}
        for tt in range(NT):
            @block.tensor
            def _(t, tt=tt):
                gp.wait(t, fold_k)
                if tt > 0:
                    dv.wait(t, kr_done[tt - 1])
                for rc in range(4):
                    for j in range(2):
                        ins = t.matmul(KP[:, j * 512:(j + 1) * 512],
                                       ckv_fm[:, rc, tt * 128:(tt + 1) * 128],
                                       WKR[:, rc, j * 512:(j + 1) * 512],
                                       start=(rc == 0), stop=(rc == 3))
                pe.inc(ins)

            @block.vector
            def _(v, tt=tt):
                pe.wait(v)
                if tt > 0:
                    sy.wait(v, kr_st[tt - 1])
                v.tensor_tensor(KRT[:], KP[:], P_krb[:], ALU.add)
                v.drain()
                q4 = KRT[:].rearrange("p (h d f) -> p h d f", h=16, d=2)
                t4 = RTMPb[:].rearrange("p (h d f) -> p h d f", h=16, d=2)
                o4 = KRB[:].rearrange("p (h d f) -> p h d f", h=16, d=2)
                cosb = P_cos[:, tt, None, None, :].broadcast_to((128, 16, 2, 32))
                sinh = P_sin[:, tt, None, :].broadcast_to((128, 16, 32))
                r2 = RT2b[:].rearrange("p (h f) -> p h f", h=16)
                v.tensor_tensor(t4, q4, cosb, ALU.mult)
                v.scalar_tensor_tensor(r2, q4[:, :, 1, :], -1.0, sinh, ALU.mult, ALU.mult)
                v.drain()
                v.tensor_tensor(o4[:, :, 0, :], t4[:, :, 0, :], r2, ALU.add)
                v.drain()
                v.tensor_tensor(r2, q4[:, :, 0, :], sinh, ALU.mult)
                v.drain()
                v.tensor_tensor(o4[:, :, 1, :], t4[:, :, 1, :], r2, ALU.add)
                dv.inc(v.drain())
                kr_done[tt] = dv.n

            @block.sync
            def _(sync, tt=tt):
                dv.wait(sync, kr_done[tt])
                sy.dma(sync.dma_start(out=kr_tm[tt * 128:(tt + 1) * 128, :], in_=KRB[:]))
                kr_st[tt] = sy.n
        barrier()

    with nc.sbuf_tensor("KRF0", [128, S], BF16) as KRF0, \
         nc.sbuf_tensor("KRF1", [128, S], BF16) as KRF1, \
         nc.sbuf_tensor("QRP0", [128, OWN], BF16) as QRP0, \
         nc.sbuf_tensor("QRP1", [128, OWN], BF16) as QRP1, \
         nc.sbuf_tensor("QCL0", [128, OWN], BF16) as QCL0, \
         nc.sbuf_tensor("QCL1", [128, OWN], BF16) as QCL1, \
         nc.sbuf_tensor("KC", [128, S], BF16) as KC, \
         nc.sbuf_tensor("VSB", [128, 32, 128], BF16) as VSB, \
         nc.sbuf_tensor("PA", [128, 2048], BF16) as PA, \
         nc.sbuf_tensor("PBH", [128, 2048], BF16) as PBH, \
         nc.sbuf_tensor("PTS", [128, 2048], BF16) as PTS, \
         nc.sbuf_tensor("SST", [128, 16], F32) as SST, \
         nc.sbuf_tensor("OT1", [128, 128], F32) as OT1, \
         nc.sbuf_tensor("OT2", [128, 128], BF16) as OT2, \
         nc.psum_tensor("SC_ps", [128, 2048], F32) as SC_ps, \
         nc.psum_tensor("PT_ps", [128, 2048], BF16) as PT_ps, \
         nc.psum_tensor("O_ps", [128, 256], F32) as O_ps, \
         nc.psum_tensor("KB_ps", [128, 512], F32) as KB_ps:

        KRF = [KRF0, KRF1]
        QRP = [QRP0, QRP1]
        QCL = [QCL0, QCL1]
        krf_used, qcl_used = {}, {}
        kc_evac, v_evac = {}, {}
        exp_done, pts_evac, pv_done, ocomb, ofm_evac, pts_pe = {}, {}, {}, {}, {}, {}
        for h in range(H):
            hb = h % 2
            pj = h // 2
            if hb == 0:
                @block.sync
                def _(sync, pj=pj):
                    if pj >= 2:
                        pe.wait(sync, krf_used[pj - 2])
                    sy.dma(sync.dma_start_transpose(KRF[pj % 2][:],
                                                    kr_tm[:, pj * 128:(pj + 1) * 128]))
                    sy.dma(sync.dma_start(out=QRP[pj % 2][:], in_=qr_hbm[pj]))

            @block.sync
            def _(sync, h=h, hb=hb):
                if h >= 2:
                    pe.wait(sync, qcl_used[h - 2])
                sy.dma(sync.dma_start(out=QCL[hb][:], in_=qc_hbm[h]))
            ldq = sy.n

            for nt in range(8):
                @block.tensor
                def _(t, h=h, nt=nt):
                    if nt == 0:
                        if h > 0:
                            dv.wait(t, v_evac[(h - 1, 7)])
                    else:
                        ac.wait(t, kc_evac[(h, nt - 1)])
                    for rc in range(4):
                        ins = t.matmul(KB_ps[:], Wku[:, rc, h * 128:(h + 1) * 128],
                                       ckv_fm[:, rc, nt * 512:(nt + 1) * 512],
                                       start=(rc == 0), stop=(rc == 3))
                    pe.inc(ins)

                @block.scalar
                def _(s, h=h, nt=nt):
                    pe.wait(s)
                    ac.inc(s.copy(KC[:, nt * 512:(nt + 1) * 512], KB_ps[:]))
                    kc_evac[(h, nt)] = ac.n

            for g8 in range(8):
                @block.tensor
                def _(t, h=h, g8=g8):
                    if g8 == 0:
                        ac.wait(t, kc_evac[(h, 7)])
                    else:
                        dv.wait(t, v_evac[(h, g8 - 1)])
                    for kt in range(4):
                        for rc in range(4):
                            ins = t.matmul(KB_ps[:, kt * 128:(kt + 1) * 128],
                                           ckv_fm[:, rc, (4 * g8 + kt) * 128:(4 * g8 + kt + 1) * 128],
                                           Wvu[:, rc, h * 128:(h + 1) * 128],
                                           start=(rc == 0), stop=(rc == 3))
                    pe.inc(ins)

                @block.vector
                def _(v, h=h, g8=g8):
                    pe.wait(v)
                    dv.inc(v.tensor_copy(VSB[:, 4 * g8:4 * (g8 + 1)],
                                         KB_ps[:].rearrange("p (a c) -> p a c", c=128)))
                    v_evac[(h, g8)] = dv.n

            for qt in range(NTQ):
                for half in range(2):
                    @block.tensor
                    def _(t, h=h, hb=hb, pj=pj, qt=qt, half=half, ldq=ldq):
                        if qt == 0 and half == 0:
                            sy.wait(t, ldq)
                            dv.wait(t, v_evac[(h, 7)])
                        prev = exp_done.get((h, qt, half - 1),
                                            exp_done.get((h, qt - 1, 1),
                                                         exp_done.get((h - 1, NTQ - 1, 1), 0)))
                        ac.wait(t, prev)
                        for kc in range(4):
                            t.matmul(SC_ps[:, kc * 512:(kc + 1) * 512],
                                     QCL[hb][:, qt * 128:(qt + 1) * 128],
                                     KC[:, (half * 4 + kc) * 512:(half * 4 + kc + 1) * 512],
                                     start=True, stop=False)
                        qr_l = QRP[pj % 2][0:64, qt * 128:(qt + 1) * 128] if hb == 0 else \
                            QRP[pj % 2][64:128, qt * 128:(qt + 1) * 128]
                        for kc in range(4):
                            kr_r = KRF[pj % 2][0:64, (half * 4 + kc) * 512:(half * 4 + kc + 1) * 512] \
                                if hb == 0 else \
                                KRF[pj % 2][64:128, (half * 4 + kc) * 512:(half * 4 + kc + 1) * 512]
                            ins = t.matmul(SC_ps[:, kc * 512:(kc + 1) * 512], qr_l, kr_r,
                                           start=False, stop=True)
                        pe.inc(ins)
                        if qt == NTQ - 1 and half == 1:
                            qcl_used[h] = pe.n
                            if hb == 1:
                                krf_used[pj] = pe.n

                    @block.vector
                    def _(v, half=half):
                        pe.wait(v)
                        v.reduce_max(SST[:, half:half + 1], SC_ps[:], axis=AX, negate=True)
                        dv.inc(v.drain())

                    @block.scalar
                    def _(s, h=h, qt=qt, half=half):
                        dv.wait(s)
                        PH = PA if half == 0 else PBH
                        pe.wait(s, pts_pe.get(half, 0))
                        ac.inc(s.activation(PH[:], SC_ps[:], AF.Exp,
                                            bias=SST[:, half:half + 1],
                                            accum_out=SST[:, 12 + half:13 + half]))
                        exp_done[(h, qt, half)] = ac.n

                @block.vector
                def _(v, h=h, qt=qt):
                    ac.wait(v, exp_done[(h, qt, 1)])
                    v.tensor_tensor(SST[:, 2:3], SST[:, 0:1], SST[:, 1:2], ALU.min)
                    v.drain()
                    v.tensor_sub(SST[:, 3:4], SST[:, 2:3], SST[:, 0:1])
                    v.tensor_sub(SST[:, 4:5], SST[:, 2:3], SST[:, 1:2])
                    dv.inc(v.drain())

                @block.scalar
                def _(s):
                    dv.wait(s)
                    ac.inc(s.activation(SST[:, 5:7], SST[:, 3:5], AF.Exp))

                @block.vector
                def _(v):
                    ac.wait(v)
                    v.tensor_tensor(SST[:, 7:8], SST[:, 5:6], SST[:, 12:13], ALU.mult)
                    v.drain()
                    v.scalar_tensor_tensor(SST[:, 8:9], SST[:, 6:7], SST[:, 13:14],
                                           SST[:, 7:8], ALU.mult, ALU.add)
                    v.drain()
                    v.reciprocal(SST[:, 9:10], SST[:, 8:9])
                    v.drain()
                    v.tensor_tensor(SST[:, 10:12], SST[:, 5:7],
                                    SST[:, 9:10].broadcast_to((128, 2)), ALU.mult)
                    dv.inc(v.drain())
                stats_done = dv.n

                for half in range(2):
                    PH = PA if half == 0 else PBH

                    @block.tensor
                    def _(t, h=h, qt=qt, half=half, PH=PH):
                        ac.wait(t, exp_done[(h, qt, half)])
                        dv.wait(t, pts_evac.get("last", 0))
                        if half == 0:
                            dv.wait(t, ocomb.get("last", 0))
                        for i in range(16):
                            ins = t.transpose(PT_ps[:, i * 128:(i + 1) * 128],
                                              PH[:, i * 128:(i + 1) * 128], P_ident[:])
                        pe.inc(ins)
                        pts_pe[half] = pe.n

                    @block.vector
                    def _(v):
                        pe.wait(v)
                        dv.inc(v.tensor_copy(PTS[:], PT_ps[:]))
                        pts_evac["last"] = dv.n

                    @block.tensor
                    def _(t, h=h, qt=qt, half=half):
                        dv.wait(t, pts_evac["last"])
                        for kc in range(16):
                            ins = t.matmul(O_ps[:, half * 128:(half + 1) * 128],
                                           PTS[:, kc * 128:(kc + 1) * 128],
                                           VSB[:, half * 16 + kc],
                                           start=(kc == 0), stop=(kc == 15))
                        pe.inc(ins)
                        pv_done[half] = pe.n

                @block.vector
                def _(v, h=h, qt=qt, stats_done=stats_done):
                    pe.wait(v, pv_done[1])
                    dv.wait(v, stats_done)
                    v.tensor_scalar_mul(OT1[:], O_ps[:, 0:128], SST[:, 10:11])
                    v.drain()
                    v.scalar_tensor_tensor(OT2[:], O_ps[:, 128:256], SST[:, 11:12],
                                           OT1[:], ALU.mult, ALU.add)
                    dv.inc(v.drain())
                    ocomb["last"] = dv.n

                @block.tensor
                def _(t, h=h, qt=qt):
                    dv.wait(t, ocomb["last"])
                    ac.wait(t, ofm_evac.get("last", 0))
                    pe.inc(t.transpose(PT_ps[:, 0:128], OT2[:], P_ident[:]))

                @block.scalar
                def _(s, h=h, qt=qt):
                    pe.wait(s)
                    ac.inc(s.activation(o_fm[:, h, qt * 128:(qt + 1) * 128], PT_ps[:, 0:128],
                                        AF.Identity, bias=P_vb[:, h:h + 1]))
                    ofm_evac["last"] = ac.n
        barrier()

    stack_c.close()

    if debug:
        with nc.sbuf_tensor("DBG4", [128, 2048], F32) as DBG4:
            dump_tap(lambda a: o_fm[:].rearrange("p a c -> p (a c)")[:, a * 2048:(a + 1) * 2048],
                     taps["t_ofm"], 8, 2048, DBG4)

    with nc.sbuf_tensor("WOL", [128, 16, D], BF16) as WOL, \
         nc.sbuf_tensor("XRES", [128, D], F32) as XRES, \
         nc.sbuf_tensor("FRES", [128, D], F32) as FRES, \
         nc.sbuf_tensor("S1", [128, D], F32) as S1, \
         nc.sbuf_tensor("OUTB", [128, D], F32) as OUTB, \
         nc.psum_tensor("Y2", [128, 2048], F32) as Y2:

        @block.gpsimd
        def _(g):
            for a in range(4):
                gd.dma(g.dma_start(out=WOL[:, 4 * a:4 * (a + 1)],
                                   in_=wo[a * 512:(a + 1) * 512, :].rearrange("(a p) c -> p a c", p=128)))
        ldwo = gd.n

        res_used = {}
        for qt in range(NTQ):
            @block.sync
            def _(sync, qt=qt):
                if qt > 0:
                    dv.wait(sync, res_used[qt - 1])
                sy.dma(sync.dma_start(out=XRES[:], in_=xb[qt * 128:(qt + 1) * 128, :]))
                sy.dma(sync.dma_start(out=FRES[:], in_=ffn_y[qt * 128:(qt + 1) * 128, :]))
            ldr = sy.n

            @block.tensor
            def _(t, qt=qt):
                if qt == 0:
                    gd.wait(t, ldwo)
                else:
                    dv.wait(t, res_used[qt - 1])
                for fc in range(H):
                    for jj in range(4):
                        ins = t.matmul(Y2[:, jj * 512:(jj + 1) * 512],
                                       o_fm[:, fc, qt * 128:(qt + 1) * 128],
                                       WOL[:, fc, jj * 512:(jj + 1) * 512],
                                       start=(fc == 0), stop=(fc == H - 1))
                pe.inc(ins)

            @block.vector
            def _(v, qt=qt, ldr=ldr):
                sy.wait(v, ldr)
                v.tensor_tensor(S1[:], XRES[:], FRES[:], ALU.add)
                pe.wait(v)
                v.drain()
                v.tensor_tensor(OUTB[:], S1[:], Y2[:], ALU.add)
                dv.inc(v.drain())
                res_used[qt] = dv.n

            @block.sync
            def _(sync, qt=qt):
                dv.wait(sync, res_used[qt])
                sy.dma(sync.dma_start(out=y[qt * 128:(qt + 1) * 128, :], in_=OUTB[:]))

        @block.sync
        def _(sync):
            sy.wait(sync)

    stx.close()
    return nc, taps


def _prep_inputs(inputs):
    x = np.asarray(inputs["L0_attn_x"])
    f32 = np.float32
    wg_u8 = np.asarray(inputs["L0_ffn_wq_gate"]).view(np.uint8).reshape(HID, D, 4)[:, :, 0]
    wu_u8 = np.asarray(inputs["L0_ffn_wq_up"]).view(np.uint8).reshape(HID, D, 4)[:, :, 0]
    wd_u8 = np.asarray(inputs["L0_ffn_wq_down"]).view(np.uint8).reshape(D, HID, 4)[:, :, 0]
    shared = {
        "wqd": np.ascontiguousarray(inputs["L0_attn_Wq_down"], f32),
        "wqu": np.ascontiguousarray(inputs["L0_attn_Wq_up"], f32),
        "wqr": np.ascontiguousarray(inputs["L0_attn_Wq_rope"], f32),
        "wkd": np.ascontiguousarray(inputs["L0_attn_Wkv_down"], f32),
        "wku": np.ascontiguousarray(inputs["L0_attn_Wk_up"], f32),
        "wvu": np.ascontiguousarray(inputs["L0_attn_Wv_up"], f32),
        "wkr": np.ascontiguousarray(inputs["L0_attn_Wk_rope"], f32),
        "wo": np.ascontiguousarray(inputs["L0_attn_Wout"], f32),
        "qg2": np.ascontiguousarray(np.asarray(inputs["L0_attn_q_gamma"], f32).reshape(12, 128).T),
        "qb2": np.ascontiguousarray(np.asarray(inputs["L0_attn_q_beta"], f32).reshape(12, 128).T),
        "kg2": np.ascontiguousarray(np.asarray(inputs["L0_attn_kv_gamma"], f32).reshape(4, 128).T),
        "kb2": np.ascontiguousarray(np.asarray(inputs["L0_attn_kv_beta"], f32).reshape(4, 128).T),
        "fg": np.ascontiguousarray(wg_u8.reshape(NHC, 128, 16, 128).transpose(0, 3, 2, 1)),
        "fu": np.ascontiguousarray(wu_u8.reshape(NHC, 128, 16, 128).transpose(0, 3, 2, 1)),
        "fd": np.ascontiguousarray(wd_u8.reshape(D, NHC, 128).transpose(1, 2, 0)),
        "sg2": np.ascontiguousarray(np.asarray(inputs["L0_ffn_s_gate"], f32).reshape(NHC, 128).T),
        "su2": np.ascontiguousarray(np.asarray(inputs["L0_ffn_s_up"], f32).reshape(NHC, 128).T),
        "sdr": np.ascontiguousarray(np.asarray(inputs["L0_ffn_s_down"], f32).reshape(1, D)),
    }
    per_core = []
    for c in range(8):
        b, qs = c // 4, c % 4
        perm = np.concatenate([np.arange(qs * OWN, (qs + 1) * OWN),
                               np.arange(0, qs * OWN),
                               np.arange((qs + 1) * OWN, S)])
        m = dict(shared)
        m["xb"] = np.ascontiguousarray(x[b][perm], f32)
        m["posP"] = np.ascontiguousarray(perm.astype(f32).reshape(NT, 128).T)
        per_core.append(m)
    return per_core


def kernel(**inputs):
    per_core = _prep_inputs(inputs)
    nc, _ = build_program(debug=DEBUG)
    res = run_bass_kernel_spmd(nc, per_core, core_ids=list(range(8)))
    out = np.empty((B, S, D), np.float32)
    for c in range(8):
        b, qs = c // 4, c % 4
        out[b, qs * OWN:(qs + 1) * OWN] = res.results[c]["y"]
    return out


# version 02
# speedup vs baseline: 1.1725x; 1.1725x over previous
import contextlib
import numpy as np
import concourse.bass as bass
import concourse.mybir as mybir
from concourse.bass_utils import run_bass_kernel_spmd

F32 = mybir.dt.float32
BF16 = mybir.dt.bfloat16
U8 = mybir.dt.uint8
I32 = mybir.dt.int32
AF = mybir.ActivationFunctionType
ALU = mybir.AluOpType
AX = mybir.AxisListType.X

B, S, D = 2, 4096, 2048
H, DH, DR = 16, 128, 64
QR, KVR = 1536, 512
HID, NHC = 5632, 44
NT, NTQ = 32, 8
OWN = 1024
SCALE = float((DH + DR) ** -0.5)
EPS = 1e-5
TWO_PI = 2 * np.pi
CW0 = float(np.float32(6.28125))
CW1 = float(np.float32(TWO_PI - 6.28125))
CW2 = float(TWO_PI - np.float64(np.float32(6.28125)) - np.float64(np.float32(TWO_PI - 6.28125)))
MAGIC = float(np.float32(1.5 * 2 ** 23))

DEBUG = False


class Sem:
    __slots__ = ("h", "n")

    def __init__(self, h):
        self.h = h
        self.n = 0

    def inc(self, instr, k=1):
        instr.then_inc(self.h, k)
        self.n += k
        return self.n

    def dma(self, instr):
        return self.inc(instr, 16)

    def wait(self, eng, n=None):
        eng.wait_ge(self.h, self.n if n is None else n)


def build_program(debug=False):
    nc = bass.Bass("TRN2", target_bir_lowering=False)

    xb = nc.dram_tensor("xb", [S, D], F32, kind="ExternalInput").ap()
    wqd = nc.dram_tensor("wqd", [D, QR], F32, kind="ExternalInput").ap()
    wqu = nc.dram_tensor("wqu", [QR, D], F32, kind="ExternalInput").ap()
    wqr = nc.dram_tensor("wqr", [QR, H * DR], F32, kind="ExternalInput").ap()
    wkd = nc.dram_tensor("wkd", [D, KVR], F32, kind="ExternalInput").ap()
    wku = nc.dram_tensor("wku", [KVR, D], F32, kind="ExternalInput").ap()
    wvu = nc.dram_tensor("wvu", [KVR, D], F32, kind="ExternalInput").ap()
    wkr = nc.dram_tensor("wkr", [KVR, H * DR], F32, kind="ExternalInput").ap()
    wo = nc.dram_tensor("wo", [D, D], F32, kind="ExternalInput").ap()
    qg2 = nc.dram_tensor("qg2", [128, 12], F32, kind="ExternalInput").ap()
    qb2 = nc.dram_tensor("qb2", [128, 12], F32, kind="ExternalInput").ap()
    kg2 = nc.dram_tensor("kg2", [128, 4], F32, kind="ExternalInput").ap()
    kb2 = nc.dram_tensor("kb2", [128, 4], F32, kind="ExternalInput").ap()
    posP = nc.dram_tensor("posP", [128, NT], F32, kind="ExternalInput").ap()
    fg = nc.dram_tensor("fg", [NHC, 128, 16, 128], U8, kind="ExternalInput").ap()
    fu = nc.dram_tensor("fu", [NHC, 128, 16, 128], U8, kind="ExternalInput").ap()
    fd = nc.dram_tensor("fd", [NHC, 128, D], U8, kind="ExternalInput").ap()
    sg2 = nc.dram_tensor("sg2", [128, NHC], F32, kind="ExternalInput").ap()
    su2 = nc.dram_tensor("su2", [128, NHC], F32, kind="ExternalInput").ap()
    sdr = nc.dram_tensor("sdr", [1, D], F32, kind="ExternalInput").ap()

    y = nc.dram_tensor("y", [OWN, D], F32, kind="ExternalOutput").ap()

    kr_tm = nc.dram_tensor("kr_tm", [S, H * DR], BF16, kind="Internal").ap()
    qc_hbm = nc.dram_tensor("qc_hbm", [H, 128, OWN], BF16, kind="Internal").ap()
    qr_hbm = nc.dram_tensor("qr_hbm", [8, 128, OWN], BF16, kind="Internal").ap()
    ffn_y = nc.dram_tensor("ffn_y", [OWN, D], F32, kind="Internal").ap()
    bb = nc.dram_tensor("bb", [4, D], F32, kind="Internal").ap()

    taps = {}
    if debug:
        taps["t_xn"] = nc.dram_tensor("t_xn", [128, 16 * OWN], F32, kind="ExternalOutput").ap()
        taps["t_ckv"] = nc.dram_tensor("t_ckv", [128, 4 * S], F32, kind="ExternalOutput").ap()
        taps["t_cq"] = nc.dram_tensor("t_cq", [128, 12 * OWN], F32, kind="ExternalOutput").ap()
        taps["t_qr"] = nc.dram_tensor("t_qr", [128, 8 * OWN], F32, kind="ExternalOutput").ap()
        taps["t_ffn"] = nc.dram_tensor("t_ffn", [OWN, D], F32, kind="ExternalOutput").ap()
        taps["t_ofm"] = nc.dram_tensor("t_ofm", [128, H * OWN], F32, kind="ExternalOutput").ap()
        taps["t_cos"] = nc.dram_tensor("t_cos", [128, NT * 32], F32, kind="ExternalOutput").ap()
        taps["t_ifr"] = nc.dram_tensor("t_ifr", [1, 32], F32, kind="ExternalOutput").ap()

    stx = contextlib.ExitStack()
    E = stx.enter_context

    sy = Sem(E(nc.semaphore("sy")))
    gd = Sem(E(nc.semaphore("gd")))
    pe = Sem(E(nc.semaphore("pe")))
    dv = Sem(E(nc.semaphore("dv")))
    ac = Sem(E(nc.semaphore("ac")))
    gp = Sem(E(nc.semaphore("gp")))
    sems = (sy, gd, pe, dv, ac, gp)

    block = E(nc.Block())

    def barrier():
        snap = [(s.h, s.n) for s in sems]
        for name in ("sync", "vector", "scalar", "tensor", "gpsimd"):
            def f(e, snap=snap):
                for h, n in snap:
                    e.wait_ge(h, n)
            getattr(block, name)(f)

    def dump_tap(src_ap_fn, dst_ap, nchunks, chunk_cols, scratch):
        for a in range(nchunks):
            @block.scalar
            def _(s, a=a):
                ac.inc(s.copy(scratch[:, 0:chunk_cols], src_ap_fn(a)))

            @block.sync
            def _(sync, a=a):
                ac.wait(sync)
                sy.dma(sync.dma_start(out=dst_ap[:, a * chunk_cols:(a + 1) * chunk_cols],
                                      in_=scratch[:, 0:chunk_cols]))
            barrier()

    P_ident = E(nc.sbuf_tensor("P_ident", [128, 128], BF16))
    P_identf = E(nc.sbuf_tensor("P_identf", [128, 128], F32))
    P_ones = E(nc.sbuf_tensor("P_ones", [1, 128], F32))
    P_cos = E(nc.sbuf_tensor("P_cos", [128, NT, 32], F32))
    P_sin = E(nc.sbuf_tensor("P_sin", [128, NT, 32], F32))
    P_qg = E(nc.sbuf_tensor("P_qg", [128, 12], F32))
    P_qb = E(nc.sbuf_tensor("P_qb", [128, 12], F32))
    P_kg = E(nc.sbuf_tensor("P_kg", [128, 4], F32))
    P_kb = E(nc.sbuf_tensor("P_kb", [128, 4], F32))
    P_qbb = E(nc.sbuf_tensor("P_qbb", [128, 12], BF16))
    P_kbb = E(nc.sbuf_tensor("P_kbb", [128, 4], BF16))
    P_sg = E(nc.sbuf_tensor("P_sg", [128, NHC], F32))
    P_su = E(nc.sbuf_tensor("P_su", [128, NHC], F32))
    P_qcb = E(nc.sbuf_tensor("P_qcb", [128, 16], F32))
    P_vb = E(nc.sbuf_tensor("P_vb", [128, 16], F32))

    ckv_fm = E(nc.sbuf_tensor("ckv_fm", [128, 4, S], BF16))
    ST1 = E(nc.sbuf_tensor("ST1", [128, 16], F32))
    ST2 = E(nc.sbuf_tensor("ST2", [128, 8], F32))
    stack_a = contextlib.ExitStack()
    xn_fm = stack_a.enter_context(nc.sbuf_tensor("xn_fm", [128, 16, OWN], BF16))

    @block.sync
    def _(sync):
        sy.dma(sync.dma_start(out=P_qg[:], in_=qg2[:, :]))
        sy.dma(sync.dma_start(out=P_qb[:], in_=qb2[:, :]))
        sy.dma(sync.dma_start(out=P_kg[:], in_=kg2[:, :]))
        sy.dma(sync.dma_start(out=P_kb[:], in_=kb2[:, :]))
        sy.dma(sync.dma_start(out=P_sg[:], in_=sg2[:, :]))
        sy.dma(sync.dma_start(out=P_su[:], in_=su2[:, :]))
    ld0 = sy.n

    with nc.sbuf_tensor("T_pos", [128, NT], F32) as T_pos, \
         nc.sbuf_tensor("T_ioti", [1, 32], I32) as T_ioti, \
         nc.sbuf_tensor("T_iot", [1, 32], F32) as T_iot, \
         nc.sbuf_tensor("T_g", [1, 32], F32) as T_g, \
         nc.sbuf_tensor("T_m", [1, 32], F32) as T_m, \
         nc.sbuf_tensor("T_f", [1, 32], F32) as T_f, \
         nc.sbuf_tensor("T_ifb", [128, 32], F32) as T_ifb, \
         nc.sbuf_tensor("T_th", [128, NT, 32], F32) as T_th, \
         nc.sbuf_tensor("T_k", [128, NT, 32], F32) as T_k, \
         nc.sbuf_tensor("T_w", [128, NT, 32], F32) as T_w, \
         nc.psum_tensor("T_ps", [128, 512], F32) as T_ps:

        @block.sync
        def _(sync):
            sy.dma(sync.dma_start(out=T_pos[:], in_=posP[:, :]))
        ld1 = sy.n

        @block.gpsimd
        def _(g):
            g.memset(P_ident[:], 0.0)
            g.affine_select(out=P_ident[:], in_=P_ident[:], compare_op=ALU.not_equal,
                            fill=1.0, base=0, pattern=[[-1, 128]], channel_multiplier=1)
            g.memset(P_identf[:], 0.0)
            g.affine_select(out=P_identf[:], in_=P_identf[:], compare_op=ALU.not_equal,
                            fill=1.0, base=0, pattern=[[-1, 128]], channel_multiplier=1)
            g.memset(P_ones[:], 1.0)
            g.iota(T_ioti[:], pattern=[[1, 32]], base=0, channel_multiplier=0)
            g.tensor_copy(T_iot[:], T_ioti[:])
            gp.inc(g.memset(T_m[:], float(-np.log(10000.0) / 32.0)))

        @block.scalar
        def _(s):
            gp.wait(s)
            ac.inc(s.activation(T_g[:], T_m[:], AF.Exp))

        @block.vector
        def _(v):
            ac.wait(v)
            v.tensor_tensor(T_m[:], T_g[:], T_g[:], ALU.mult)
            for _sq in range(4):
                v.drain()
                v.tensor_tensor(T_m[:], T_m[:], T_m[:], ALU.mult)
            v.drain()
            v.tensor_scalar(T_m[:], T_m[:], -10000.0, 1.0, ALU.mult, ALU.add)
            v.drain()
            v.tensor_scalar(T_m[:], T_m[:], float(1.0 / 32.0), 1.0, ALU.mult, ALU.add)
            v.drain()
            v.tensor_tensor(T_g[:], T_g[:], T_m[:], ALU.mult)
            v.drain()
            v.memset(T_f[:], 1.0)
            for k in range(5):
                b = 2 ** k
                sl = T_f[0:1, :].rearrange("p (a b) -> p a b", b=2 * b)[:, :, b:2 * b]
                v.tensor_scalar_mul(sl, sl, T_g[0:1, 0:1])
                v.drain()
                if k < 4:
                    v.tensor_tensor(T_g[:], T_g[:], T_g[:], ALU.mult)
                    v.drain()
            dv.inc(v.drain())

        @block.tensor
        def _(t):
            dv.wait(t)
            pe.inc(t.matmul(T_ps[:, 0:32], P_ones[:], T_f[:], start=True, stop=True))

        @block.vector
        def _(v):
            pe.wait(v)
            v.tensor_copy(T_ifb[:], T_ps[:, 0:32])
            sy.wait(v, ld1)
            v.drain()
            for t in range(NT):
                v.tensor_scalar_mul(T_th[:, t], T_ifb[:], T_pos[:, t:t + 1])
            v.drain()
            TH = T_th[:].rearrange("p a c -> p (a c)")
            K_ = T_k[:].rearrange("p a c -> p (a c)")
            W_ = T_w[:].rearrange("p a c -> p (a c)")
            v.tensor_scalar(K_, TH, float(1.0 / TWO_PI), MAGIC, ALU.mult, ALU.add)
            v.drain()
            v.tensor_scalar_add(K_, K_, -MAGIC)
            v.drain()
            v.scalar_tensor_tensor(W_, K_, -CW0, TH, ALU.mult, ALU.add)
            v.drain()
            v.scalar_tensor_tensor(W_, K_, -CW1, W_, ALU.mult, ALU.add)
            v.drain()
            v.scalar_tensor_tensor(W_, K_, -CW2, W_, ALU.mult, ALU.add)
            v.drain()
            v.tensor_scalar_min(W_, W_, float(np.pi))
            v.drain()
            v.tensor_scalar_max(W_, W_, float(-np.pi))
            v.drain()
            v.tensor_scalar_add(K_, W_, float(np.pi / 2))
            v.drain()
            v.tensor_scalar(TH, K_, float(np.pi), None, ALU.is_gt)
            v.drain()
            v.scalar_tensor_tensor(K_, TH, float(-TWO_PI), K_, ALU.mult, ALU.add)
            dv.inc(v.drain())

        @block.scalar
        def _(s):
            dv.wait(s)
            s.activation(P_sin[:].rearrange("p a c -> p (a c)"),
                         T_w[:].rearrange("p a c -> p (a c)"), AF.Sin)
            ac.inc(s.activation(P_cos[:].rearrange("p a c -> p (a c)"),
                                T_k[:].rearrange("p a c -> p (a c)"), AF.Sin))

        if debug:
            @block.sync
            def _(sync):
                ac.wait(sync)
                dv.wait(sync)
                sy.dma(sync.dma_start(out=taps["t_cos"][:, :],
                                      in_=P_cos[:].rearrange("p a c -> p (a c)")))
                sy.dma(sync.dma_start(out=taps["t_ifr"][:, :], in_=T_f[:]))
        barrier()

    def emit_ln(src_ap, n_feat, sq_ap, stats, out_ap):
        @block.vector
        def _(v):
            v.reduce_sum(stats[:, 0:1], src_ap, axis=AX)
            v.drain()
            v.tensor_scalar_mul(stats[:, 1:2], stats[:, 0:1], float(-1.0 / n_feat))
            dv.inc(v.drain())

        @block.scalar
        def _(s):
            dv.wait(s)
            ac.inc(s.activation(sq_ap, src_ap, AF.Square, bias=stats[:, 1:2],
                                accum_out=stats[:, 2:3]))

        @block.vector
        def _(v):
            ac.wait(v)
            v.tensor_scalar(stats[:, 3:4], stats[:, 2:3], float(1.0 / n_feat), EPS,
                            ALU.mult, ALU.add)
            dv.inc(v.drain())

        @block.scalar
        def _(s):
            dv.wait(s)
            ac.inc(s.activation(stats[:, 4:5], stats[:, 3:4], AF.Sqrt))

        @block.vector
        def _(v):
            ac.wait(v)
            v.reciprocal(stats[:, 5:6], stats[:, 4:5])
            v.drain()
            v.tensor_tensor(stats[:, 6:7], stats[:, 1:2], stats[:, 5:6], ALU.mult)
            dv.inc(v.drain())

        @block.scalar
        def _(s):
            dv.wait(s)
            ac.inc(s.activation(out_ap, src_ap, AF.Identity,
                                bias=stats[:, 6:7], scale=stats[:, 5:6]))

    with nc.sbuf_tensor("XT0", [128, D], F32) as XT0, \
         nc.sbuf_tensor("XT1", [128, D], F32) as XT1, \
         nc.sbuf_tensor("XNB0", [128, D], BF16) as XNB0, \
         nc.sbuf_tensor("XNB1", [128, D], BF16) as XNB1, \
         nc.sbuf_tensor("SQ", [128, 2048], F32) as SQ, \
         nc.psum_tensor("TRP", [128, 2048], BF16) as TRP:

        XT, XNB = [XT0, XT1], [XNB0, XNB1]
        x_done, evac_done = {}, {}
        for t in range(NTQ):
            bi = t % 2

            @block.sync
            def _(sync, t=t, bi=bi):
                if t >= 2:
                    ac.wait(sync, x_done[t - 2])
                sy.dma(sync.dma_start(out=XT[bi][:], in_=xb[t * 128:(t + 1) * 128, :]))
            ld = sy.n

            @block.vector
            def _(v, ld=ld):
                sy.wait(v, ld)
            emit_ln(XT[bi][:], D, SQ[:], ST1[:, 8 * bi:8 * bi + 8], XNB[bi][:])
            x_done[t] = ac.n

            @block.tensor
            def _(t_, t=t, bi=bi):
                ac.wait(t_)
                if t >= 1:
                    dv.wait(t_, evac_done[t - 1])
                for fc in range(16):
                    ins = t_.transpose(TRP[:, fc * 128:(fc + 1) * 128],
                                       XNB[bi][:, fc * 128:(fc + 1) * 128], P_ident[:])
                pe.inc(ins)

            @block.vector
            def _(v, t=t):
                pe.wait(v)
                dv.inc(v.tensor_copy(xn_fm[:, :, t * 128:(t + 1) * 128],
                                     TRP[:].rearrange("p (a c) -> p a c", c=128)))
                evac_done[t] = dv.n
        barrier()

    if debug:
        with nc.sbuf_tensor("DBG0", [128, 2048], F32) as DBG0:
            dump_tap(lambda a: xn_fm[:].rearrange("p a c -> p (a c)")[:, a * 2048:(a + 1) * 2048],
                     taps["t_xn"], 8, 2048, DBG0)

    NQ4 = 4
    QHC = NHC // NQ4
    with nc.sbuf_tensor("SDB", [128, D], F32) as SDB, \
         nc.sbuf_tensor("SDROW", [1, D], F32) as SDROW, \
         nc.sbuf_tensor("G8", [128, 2, D], U8) as G8b, \
         nc.sbuf_tensor("U8T", [128, 2, D], U8) as U8b, \
         nc.sbuf_tensor("D8", [128, 2, D], U8) as D8b, \
         nc.sbuf_tensor("GW", [128, 2, 16, 128], BF16) as GWb, \
         nc.sbuf_tensor("UW", [128, 2, 16, 128], BF16) as UWb, \
         nc.sbuf_tensor("SILU", [128, 2, OWN], BF16) as SILUb, \
         nc.sbuf_tensor("MQ", [128, QHC, OWN], BF16) as MQ, \
         nc.sbuf_tensor("DWQ", [128, QHC, D], BF16) as DWQ, \
         nc.sbuf_tensor("YSB", [128, D], F32) as YSB, \
         nc.sbuf_tensor("YOLD", [128, D], F32) as YOLD, \
         nc.psum_tensor("G_ps", [128, 1024], F32) as G_ps, \
         nc.psum_tensor("U_ps", [128, 1024], F32) as U_ps, \
         nc.psum_tensor("Y_ps", [128, 2048], F32) as Y_ps:

        @block.sync
        def _(sync):
            sy.dma(sync.dma_start(out=SDROW[:], in_=sdr[:, :]))
        ldsd = sy.n

        @block.tensor
        def _(t):
            sy.wait(t, ldsd)
            for j in range(4):
                ins = t.matmul(Y_ps[:, j * 512:(j + 1) * 512], P_ones[:],
                               SDROW[:, j * 512:(j + 1) * 512], start=True, stop=True)
            pe.inc(ins)

        @block.vector
        def _(v):
            pe.wait(v)
            dv.inc(v.tensor_copy(SDB[:], Y_ps[:]))
        sdb_done = dv.n

        fold_done, gu_done, silu_done, m_done = {}, {}, {}, {}
        down_done, yevac_done, dfold_done = {}, {}, {}
        yadd_done, yst_done = {}, {}
        for q4 in range(NQ4):
            for i in range(QHC):
                hc = q4 * QHC + i
                par = hc % 2
                G8, U8T, D8 = G8b[:, par], U8b[:, par], D8b[:, par]
                GW, UW, SILU = GWb[:, par], UWb[:, par], SILUb[:, par]

                @block.gpsimd
                def _(g, hc=hc, i=i, q4=q4, G8=G8, U8T=U8T, D8=D8, GW=GW, UW=UW):
                    if hc == 0:
                        dv.wait(g, sdb_done)
                    if hc >= 2:
                        dv.wait(g, dfold_done[hc - 2])
                        pe.wait(g, gu_done[hc - 2])
                    if i == 0 and q4 > 0:
                        pe.wait(g, down_done[q4 - 1])
                    gd.dma(g.dma_start(out=G8, in_=fg[hc].rearrange("p a c -> p (a c)")))
                    gd.dma(g.dma_start(out=U8T, in_=fu[hc].rearrange("p a c -> p (a c)")))
                    gd.dma(g.dma_start(out=D8, in_=fd[hc]))
                    g.wait_ge(gd.h, gd.n)
                    g.tensor_scalar_add(GW.rearrange("p a c -> p (a c)"), G8, -128.0)
                    gp.inc(g.tensor_scalar_add(UW.rearrange("p a c -> p (a c)"), U8T, -128.0))
                    fold_done[hc] = gp.n

                @block.vector
                def _(v, hc=hc, i=i, D8=D8):
                    gp.wait(v, fold_done[hc])
                    dv.inc(v.scalar_tensor_tensor(DWQ[:, i], D8, -128.0, SDB[:],
                                                  ALU.add, ALU.mult))
                    dfold_done[hc] = dv.n

                @block.tensor
                def _(t, hc=hc, GW=GW, UW=UW):
                    gp.wait(t, fold_done[hc])
                    if hc > 0:
                        ac.wait(t, silu_done[hc - 1])
                        dv.wait(t, m_done[hc - 1])
                    for dc in range(16):
                        for j in range(2):
                            t.matmul(G_ps[:, j * 512:(j + 1) * 512], GW[:, dc],
                                     xn_fm[:, dc, j * 512:(j + 1) * 512],
                                     start=(dc == 0), stop=(dc == 15))
                    for dc in range(16):
                        for j in range(2):
                            ins = t.matmul(U_ps[:, j * 512:(j + 1) * 512], UW[:, dc],
                                           xn_fm[:, dc, j * 512:(j + 1) * 512],
                                           start=(dc == 0), stop=(dc == 15))
                    pe.inc(ins)
                    gu_done[hc] = pe.n

                @block.scalar
                def _(s, hc=hc, SILU=SILU):
                    pe.wait(s, gu_done[hc])
                    if hc >= 2:
                        dv.wait(s, m_done[hc - 2])
                    ac.inc(s.activation(SILU, G_ps[:], AF.Silu, scale=P_sg[:, hc:hc + 1]))
                    silu_done[hc] = ac.n

                @block.vector
                def _(v, hc=hc, i=i, SILU=SILU):
                    ac.wait(v, silu_done[hc])
                    dv.inc(v.scalar_tensor_tensor(MQ[:, i], U_ps[:], P_su[:, hc:hc + 1],
                                                  SILU, ALU.mult, ALU.mult))
                    m_done[hc] = dv.n

            for tt in range(NTQ):
                if q4 > 0:
                    @block.sync
                    def _(sync, q4=q4, tt=tt):
                        if not (q4 == 1 and tt == 0):
                            dv.wait(sync, yadd_done.get(tt - 1, yadd_done.get(NTQ - 1, 0)))
                        sy.dma(sync.dma_start(out=YOLD[:], in_=ffn_y[tt * 128:(tt + 1) * 128, :]))
                    ldy = sy.n
                for ys in range(2):
                    YH = Y_ps[:, ys * 1024:(ys + 1) * 1024]

                    @block.tensor
                    def _(t, q4=q4, tt=tt, ys=ys, YH=YH):
                        if tt == 0 and ys == 0:
                            dv.wait(t, m_done[q4 * QHC + QHC - 1])
                            dv.wait(t, dfold_done[q4 * QHC + QHC - 1])
                        pk = (q4, tt - 1, ys)
                        pv_ = yevac_done.get(pk, yevac_done.get((q4 - 1, NTQ - 1, ys), None))
                        if pv_ is not None:
                            s_, n_ = pv_
                            s_.wait(t, n_)
                        for i in range(QHC):
                            for j in range(2):
                                ins = t.matmul(YH[:, j * 512:(j + 1) * 512],
                                               MQ[:, i, tt * 128:(tt + 1) * 128],
                                               DWQ[:, i, (2 * ys + j) * 512:(2 * ys + j + 1) * 512],
                                               start=(i == 0), stop=(i == QHC - 1))
                        pe.inc(ins)
                        if tt == NTQ - 1 and ys == 1:
                            down_done[q4] = pe.n

                    if q4 == 0:
                        @block.scalar
                        def _(s, q4=q4, tt=tt, ys=ys, YH=YH):
                            pe.wait(s)
                            sy.wait(s, yst_done.get((q4, tt - 1), yst_done.get((q4 - 1, NTQ - 1), 0)))
                            ac.inc(s.copy(YSB[:, ys * 1024:(ys + 1) * 1024], YH))
                            yevac_done[(0, tt, ys)] = (ac, ac.n)
                    else:
                        @block.vector
                        def _(v, ldy=ldy, q4=q4, tt=tt, ys=ys, YH=YH):
                            sy.wait(v, ldy)
                            sy.wait(v, yst_done.get((q4, tt - 1), yst_done.get((q4 - 1, NTQ - 1), 0)))
                            pe.wait(v)
                            dv.inc(v.tensor_tensor(YSB[:, ys * 1024:(ys + 1) * 1024], YH,
                                                   YOLD[:, ys * 1024:(ys + 1) * 1024], ALU.add))
                            yevac_done[(q4, tt, ys)] = (dv, dv.n)
                            if ys == 1:
                                yadd_done[tt] = dv.n

                @block.sync
                def _(sync, q4=q4, tt=tt):
                    s_, n_ = yevac_done[(q4, tt, 1)]
                    s_.wait(sync, n_)
                    s_, n_ = yevac_done[(q4, tt, 0)]
                    s_.wait(sync, n_)
                    sy.dma(sync.dma_start(out=ffn_y[tt * 128:(tt + 1) * 128, :], in_=YSB[:]))
                    yst_done[(q4, tt)] = sy.n
        barrier()

    if debug:
        with nc.sbuf_tensor("DBGF", [128, D], F32) as DBGF:
            for tt in range(NTQ):
                @block.sync
                def _(sync, tt=tt):
                    sy.dma(sync.dma_start(out=DBGF[:], in_=ffn_y[tt * 128:(tt + 1) * 128, :]))
                    sy.wait(sync)
                    sy.dma(sync.dma_start(out=taps["t_ffn"][tt * 128:(tt + 1) * 128, :], in_=DBGF[:]))
                barrier()

    stack_b = contextlib.ExitStack()
    cq_fm = stack_b.enter_context(nc.sbuf_tensor("cq_fm", [128, 12, OWN], BF16))
    with nc.sbuf_tensor("WQD", [128, 16, QR], BF16) as WQD, \
         nc.sbuf_tensor("SQc", [128, QR], F32) as SQ, \
         nc.sbuf_tensor("CQT", [128, QR], BF16) as CQT, \
         nc.psum_tensor("CQ_ps", [128, QR], F32) as CQ_ps, \
         nc.psum_tensor("TR3", [128, QR], BF16) as TR3:

        @block.gpsimd
        def _(g):
            for a in range(4):
                gd.dma(g.dma_start(out=WQD[:, 4 * a:4 * (a + 1)],
                                   in_=wqd[a * 512:(a + 1) * 512, :].rearrange("(a p) c -> p a c", p=128)))
        ldw = gd.n

        cq_evac, cqt_done = {}, {}
        for t in range(NTQ):
            @block.tensor
            def _(t_, t=t):
                if t == 0:
                    gd.wait(t_, ldw)
                else:
                    ac.wait(t_, cqt_done[t - 1])
                for dc in range(16):
                    for j in range(3):
                        ins = t_.matmul(CQ_ps[:, j * 512:(j + 1) * 512],
                                        xn_fm[:, dc, t * 128:(t + 1) * 128],
                                        WQD[:, dc, j * 512:(j + 1) * 512],
                                        start=(dc == 0), stop=(dc == 15))
                pe.inc(ins)

            @block.vector
            def _(v):
                pe.wait(v)
            emit_ln(CQ_ps[:], QR, SQ[:, 0:QR], ST2, CQT[:])
            cqt_done[t] = ac.n

            @block.tensor
            def _(t_, t=t):
                ac.wait(t_, cqt_done[t])
                if t >= 1:
                    dv.wait(t_, cq_evac[t - 1])
                for rc in range(12):
                    ins = t_.transpose(TR3[:, rc * 128:(rc + 1) * 128],
                                       CQT[:, rc * 128:(rc + 1) * 128], P_ident[:])
                pe.inc(ins)

            @block.vector
            def _(v, t=t):
                pe.wait(v)
                dv.inc(v.tensor_copy(cq_fm[:, :, t * 128:(t + 1) * 128],
                                     TR3[:].rearrange("p (a c) -> p a c", c=128)))
                cq_evac[t] = dv.n
        barrier()

    if debug:
        with nc.sbuf_tensor("DBG1", [128, 2048], F32) as DBG1:
            dump_tap(lambda a: cq_fm[:].rearrange("p a c -> p (a c)")[:, a * 2048:(a + 1) * 2048],
                     taps["t_cq"], 6, 2048, DBG1)

    with nc.sbuf_tensor("WQU", [128, 12, D], BF16) as WQU, \
         nc.sbuf_tensor("BROW", [1, D], F32) as BROW, \
         nc.sbuf_tensor("B16", [16, 128], F32) as B16, \
         nc.sbuf_tensor("QCS", [128, OWN], BF16) as QCS, \
         nc.psum_tensor("QP", [128, 1024], F32) as QP, \
         nc.psum_tensor("BP", [128, 2048], F32) as BP:

        @block.gpsimd
        def _(g):
            for a in range(3):
                gd.dma(g.dma_start(out=WQU[:, 4 * a:4 * (a + 1)],
                                   in_=wqu[a * 512:(a + 1) * 512, :].rearrange("(a p) c -> p a c", p=128)))
        ldw3 = gd.n

        @block.vector
        def _(v):
            sy.wait(v, ld0)
            gd.wait(v, ldw3)
            v.tensor_copy(P_qbb[:], P_qb[:])
            dv.inc(v.tensor_copy(P_kbb[:], P_kb[:]))
        beta_done = dv.n

        @block.tensor
        def _(t):
            dv.wait(t, beta_done)
            for rc in range(12):
                for j in range(4):
                    ins = t.matmul(BP[0:1, j * 512:(j + 1) * 512],
                                   P_qbb[:, rc:rc + 1], WQU[:, rc, j * 512:(j + 1) * 512],
                                   start=(rc == 0), stop=(rc == 11))
            pe.inc(ins)

        @block.scalar
        def _(s):
            pe.wait(s)
            ac.inc(s.activation(BROW[:], BP[0:1, :], AF.Identity, scale=SCALE))

        @block.sync
        def _(sync):
            ac.wait(sync)
            sy.dma(sync.dma_start(out=bb[0:1, :], in_=BROW[:]))
            sy.wait(sync)
            sy.dma(sync.dma_start(out=B16[:], in_=bb[0, :].rearrange("(a c) -> a c", c=128)))
        ldb = sy.n

        @block.tensor
        def _(t):
            sy.wait(t, ldb)
            pe.inc(t.transpose(BP[:, 0:16], B16[:], P_identf[0:16, 0:16]))

        @block.vector
        def _(v):
            pe.wait(v)
            dv.inc(v.tensor_copy(P_qcb[:], BP[:, 0:16]))
        qcb_done = dv.n

        @block.vector
        def _(v):
            dv.wait(v, qcb_done)
            dv.inc(v.tensor_scalar_mul(ST1[:, 0:12], P_qg[:], SCALE))
        qgs_done = dv.n

        @block.gpsimd
        def _(g):
            dv.wait(g, qgs_done)
            for rc in range(12):
                ins = g.tensor_scalar_mul(WQU[:, rc], WQU[:, rc], ST1[:, rc:rc + 1])
            gp.inc(ins)
        fold_q = gp.n

        qc_done, qcs_st = {}, {}
        for h in range(H):
            @block.tensor
            def _(t, h=h):
                gp.wait(t, fold_q)
                if h > 0:
                    ac.wait(t, qc_done[h - 1])
                for rc in range(12):
                    for j in range(2):
                        ins = t.matmul(QP[:, j * 512:(j + 1) * 512],
                                       WQU[:, rc, h * 128:(h + 1) * 128],
                                       cq_fm[:, rc, j * 512:(j + 1) * 512],
                                       start=(rc == 0), stop=(rc == 11))
                pe.inc(ins)

            @block.scalar
            def _(s, h=h):
                pe.wait(s)
                if h > 0:
                    sy.wait(s, qcs_st[h - 1])
                ac.inc(s.activation(QCS[:], QP[:], AF.Identity, bias=P_qcb[:, h:h + 1]))
                qc_done[h] = ac.n

            @block.sync
            def _(sync, h=h):
                ac.wait(sync, qc_done[h])
                sy.dma(sync.dma_start(out=qc_hbm[h], in_=QCS[:]))
                qcs_st[h] = sy.n

        barrier()

    with nc.sbuf_tensor("WQR", [128, 12, H * DR], BF16) as WQR, \
         nc.sbuf_tensor("P_qrb", [128, H * DR], F32) as P_qrb, \
         nc.sbuf_tensor("BROWr", [1, H * DR], F32) as BROWr, \
         nc.sbuf_tensor("QRT", [128, OWN], F32) as QRT, \
         nc.sbuf_tensor("RTMP", [128, OWN], F32) as RTMP, \
         nc.sbuf_tensor("RT2", [128, 512], F32) as RT2, \
         nc.sbuf_tensor("QRB", [128, OWN], BF16) as QRB, \
         nc.sbuf_tensor("QRE", [128, OWN], BF16) as QRE, \
         nc.psum_tensor("QPr", [128, 1024], F32) as QP, \
         nc.psum_tensor("BPr", [128, 1024], F32) as BPr, \
         nc.psum_tensor("TR4r", [128, 1024], BF16) as TR4:

        @block.gpsimd
        def _(g):
            for a in range(3):
                gd.dma(g.dma_start(out=WQR[:, 4 * a:4 * (a + 1)],
                                   in_=wqr[a * 512:(a + 1) * 512, :].rearrange("(a p) c -> p a c", p=128)))
        ldw3r = gd.n

        @block.tensor
        def _(t):
            gd.wait(t, ldw3r)
            for rc in range(12):
                for j in range(2):
                    ins = t.matmul(BPr[0:1, j * 512:(j + 1) * 512],
                                   P_qbb[:, rc:rc + 1], WQR[:, rc, j * 512:(j + 1) * 512],
                                   start=(rc == 0), stop=(rc == 11))
            pe.inc(ins)

        @block.scalar
        def _(s):
            pe.wait(s)
            ac.inc(s.activation(BROWr[:], BPr[0:1, :], AF.Identity, scale=SCALE))

        @block.tensor
        def _(t):
            ac.wait(t)
            for j in range(2):
                ins = t.matmul(BPr[:, j * 512:(j + 1) * 512], P_ones[:],
                               BROWr[:, j * 512:(j + 1) * 512], start=True, stop=True)
            pe.inc(ins)

        @block.vector
        def _(v):
            pe.wait(v)
            dv.inc(v.tensor_copy(P_qrb[:], BPr[:]))
        qrb_done = dv.n

        @block.gpsimd
        def _(g):
            dv.wait(g, qrb_done)
            for rc in range(12):
                ins = g.tensor_scalar_mul(WQR[:, rc], WQR[:, rc], ST1[:, rc:rc + 1])
            gp.inc(ins)
        fold_qr = gp.n

        qr_evac, qre_st = {}, {}
        for tt in range(NTQ):
            @block.tensor
            def _(t, tt=tt):
                gp.wait(t, fold_qr)
                if tt > 0:
                    dv.wait(t, qr_evac[tt - 1])
                for rc in range(12):
                    for j in range(2):
                        ins = t.matmul(QP[:, j * 512:(j + 1) * 512],
                                       cq_fm[:, rc, tt * 128:(tt + 1) * 128],
                                       WQR[:, rc, j * 512:(j + 1) * 512],
                                       start=(rc == 0), stop=(rc == 11))
                pe.inc(ins)

            @block.vector
            def _(v, tt=tt):
                pe.wait(v)
                v.tensor_tensor(QRT[:], QP[:], P_qrb[:], ALU.add)
                v.drain()
                q4 = QRT[:].rearrange("p (h d f) -> p h d f", h=16, d=2)
                t4 = RTMP[:].rearrange("p (h d f) -> p h d f", h=16, d=2)
                o4 = QRB[:].rearrange("p (h d f) -> p h d f", h=16, d=2)
                cosb = P_cos[:, tt, None, None, :].broadcast_to((128, 16, 2, 32))
                sinh = P_sin[:, tt, None, :].broadcast_to((128, 16, 32))
                r2 = RT2[:].rearrange("p (h f) -> p h f", h=16)
                v.tensor_tensor(t4, q4, cosb, ALU.mult)
                v.scalar_tensor_tensor(r2, q4[:, :, 1, :], -1.0, sinh, ALU.mult, ALU.mult)
                v.drain()
                v.tensor_tensor(o4[:, :, 0, :], t4[:, :, 0, :], r2, ALU.add)
                v.drain()
                v.tensor_tensor(r2, q4[:, :, 0, :], sinh, ALU.mult)
                v.drain()
                v.tensor_tensor(o4[:, :, 1, :], t4[:, :, 1, :], r2, ALU.add)
                dv.inc(v.drain())

            @block.tensor
            def _(t, tt=tt):
                dv.wait(t)
                if tt >= 1:
                    dv.wait(t, qr_evac[tt - 1])
                for j in range(8):
                    ins = t.transpose(TR4[:, j * 128:(j + 1) * 128],
                                      QRB[:, j * 128:(j + 1) * 128], P_ident[:])
                pe.inc(ins)

            @block.vector
            def _(v, tt=tt):
                pe.wait(v)
                if tt > 0:
                    sy.wait(v, qre_st[tt - 1])
                dv.inc(v.tensor_copy(QRE[:], TR4[:]))
                qr_evac[tt] = dv.n

            @block.sync
            def _(sync, tt=tt):
                dv.wait(sync, qr_evac[tt])
                sy.dma(sync.dma_start(out=qr_hbm[:, :, tt * 128:(tt + 1) * 128].rearrange("a p c -> p a c"),
                                      in_=QRE[:].rearrange("p (a c) -> p a c", c=128)))
                qre_st[tt] = sy.n
        barrier()

    stack_b.close()

    if debug:
        with nc.sbuf_tensor("DBG3", [128, 2048], F32) as DBG3:
            for a in range(4):
                @block.gpsimd
                def _(g, a=a):
                    gd.dma(g.dma_start(out=DBG3[:].rearrange("p (a c) -> p a c", c=OWN), in_=qr_hbm[2 * a:2 * a + 2].rearrange("a p c -> p a c")))

                @block.sync
                def _(sync, a=a):
                    gd.wait(sync)
                    sy.dma(sync.dma_start(out=taps["t_qr"][:, a * 2048:(a + 1) * 2048], in_=DBG3[:]))
                barrier()

    with nc.sbuf_tensor("WKD", [128, 16, KVR], BF16) as WKD, \
         nc.sbuf_tensor("SQk", [128, 2048], F32) as SQ, \
         nc.sbuf_tensor("XT0b", [128, D], F32) as XT0b, \
         nc.sbuf_tensor("XT1b", [128, D], F32) as XT1b, \
         nc.sbuf_tensor("XNB0b", [128, D], BF16) as XNB0b, \
         nc.sbuf_tensor("XNB1b", [128, D], BF16) as XNB1b, \
         nc.sbuf_tensor("XFM0", [128, 16, 128], BF16) as XFM0, \
         nc.sbuf_tensor("XFM1", [128, 16, 128], BF16) as XFM1, \
         nc.sbuf_tensor("CKVT", [128, KVR], BF16) as CKVT, \
         nc.psum_tensor("TRPb", [128, 2048], BF16) as TRPb, \
         nc.psum_tensor("CKV_ps", [128, KVR], F32) as CKV_ps, \
         nc.psum_tensor("TR2", [128, KVR], BF16) as TR2:

        @block.gpsimd
        def _(g):
            for a in range(4):
                gd.dma(g.dma_start(out=WKD[:, 4 * a:4 * (a + 1)],
                                   in_=wkd[a * 512:(a + 1) * 512, :].rearrange("(a p) c -> p a c", p=128)))
        ldw2 = gd.n

        XTb, XNBb, XFM = [XT0b, XT1b], [XNB0b, XNB1b], [XFM0, XFM1]
        xb_done, xfm_done, ckvt_done, kv_evac = {}, {}, {}, {}
        for t in range(NT):
            bi = t % 2
            if t >= NTQ:
                @block.sync
                def _(sync, t=t, bi=bi):
                    if t >= NTQ + 2:
                        ac.wait(sync, xb_done[t - 2])
                    sy.dma(sync.dma_start(out=XTb[bi][:], in_=xb[t * 128:(t + 1) * 128, :]))
                ld = sy.n

                @block.vector
                def _(v, ld=ld):
                    sy.wait(v, ld)
                emit_ln(XTb[bi][:], D, SQ[:], ST1[:, 8 * bi:8 * bi + 8], XNBb[bi][:])
                xb_done[t] = ac.n

                @block.tensor
                def _(t_, t=t, bi=bi):
                    ac.wait(t_, xb_done[t])
                    if t >= NTQ + 2:
                        dv.wait(t_, xfm_done[t - 2])
                    for fc in range(16):
                        ins = t_.transpose(TRPb[:, fc * 128:(fc + 1) * 128],
                                           XNBb[bi][:, fc * 128:(fc + 1) * 128], P_ident[:])
                    pe.inc(ins)

                @block.vector
                def _(v, t=t, bi=bi):
                    pe.wait(v)
                    dv.inc(v.tensor_copy(XFM[bi][:], TRPb[:].rearrange("p (a c) -> p a c", c=128)))
                    xfm_done[t] = dv.n
                xsrc = XFM[bi]
            else:
                xsrc = None

            @block.tensor
            def _(t_, t=t, xsrc=xsrc):
                if t == 0:
                    gd.wait(t_, ldw2)
                else:
                    ac.wait(t_, ckvt_done[t - 1])
                if xsrc is not None:
                    dv.wait(t_, xfm_done[t])
                for dc in range(16):
                    src = xsrc[:, dc] if xsrc is not None else xn_fm[:, dc, t * 128:(t + 1) * 128]
                    ins = t_.matmul(CKV_ps[:], src, WKD[:, dc], start=(dc == 0), stop=(dc == 15))
                pe.inc(ins)

            @block.vector
            def _(v):
                pe.wait(v)
            emit_ln(CKV_ps[:], KVR, SQ[:, 0:KVR], ST2, CKVT[:])
            ckvt_done[t] = ac.n

            @block.tensor
            def _(t_, t=t):
                ac.wait(t_, ckvt_done[t])
                if t >= 1:
                    dv.wait(t_, kv_evac[t - 1])
                for rc in range(4):
                    ins = t_.transpose(TR2[:, rc * 128:(rc + 1) * 128],
                                       CKVT[:, rc * 128:(rc + 1) * 128], P_ident[:])
                pe.inc(ins)

            @block.vector
            def _(v, t=t):
                pe.wait(v)
                dv.inc(v.tensor_copy(ckv_fm[:, :, t * 128:(t + 1) * 128],
                                     TR2[:].rearrange("p (a c) -> p a c", c=128)))
                kv_evac[t] = dv.n
        barrier()

    stack_a.close()

    if debug:
        with nc.sbuf_tensor("DBG2", [128, 2048], F32) as DBG2:
            dump_tap(lambda a: ckv_fm[:].rearrange("p a c -> p (a c)")[:, a * 2048:(a + 1) * 2048],
                     taps["t_ckv"], 8, 2048, DBG2)

    o_fm = E(nc.sbuf_tensor("o_fm", [128, H, OWN], BF16))
    stack_c = contextlib.ExitStack()
    Wku = stack_c.enter_context(nc.sbuf_tensor("Wku", [128, 4, D], BF16))
    Wvu = stack_c.enter_context(nc.sbuf_tensor("Wvu", [128, 4, D], BF16))
    with nc.sbuf_tensor("WKR", [128, 4, H * DR], BF16) as WKR, \
         nc.sbuf_tensor("BROW2", [1, D], F32) as BROW2, \
         nc.sbuf_tensor("B16b", [16, 128], F32) as B16b, \
         nc.sbuf_tensor("P_krb", [128, H * DR], F32) as P_krb, \
         nc.sbuf_tensor("KRT", [128, 1024], F32) as KRT, \
         nc.sbuf_tensor("RTMPb", [128, 1024], F32) as RTMPb, \
         nc.sbuf_tensor("RT2b", [128, 512], F32) as RT2b, \
         nc.sbuf_tensor("KRB", [128, 1024], BF16) as KRB, \
         nc.psum_tensor("KP", [128, 1024], F32) as KP, \
         nc.psum_tensor("BPb", [128, 1024], F32) as BPb:

        @block.gpsimd
        def _(g):
            gd.dma(g.dma_start(out=Wku[:], in_=wku[:, :].rearrange("(a p) c -> p a c", p=128)))
            gd.dma(g.dma_start(out=Wvu[:], in_=wvu[:, :].rearrange("(a p) c -> p a c", p=128)))
            gd.dma(g.dma_start(out=WKR[:], in_=wkr[:, :].rearrange("(a p) c -> p a c", p=128)))
        ldw4 = gd.n

        vb_done = {}
        for half in range(2):
            @block.tensor
            def _(t, half=half):
                gd.wait(t, ldw4)
                if half == 1:
                    ac.wait(t, vb_done[0])
                for rc in range(4):
                    for j in range(2):
                        ins = t.matmul(BPb[0:1, j * 512:(j + 1) * 512],
                                       P_kbb[:, rc:rc + 1],
                                       Wvu[:, rc, (half * 2 + j) * 512:(half * 2 + j + 1) * 512],
                                       start=(rc == 0), stop=(rc == 3))
                pe.inc(ins)

            @block.scalar
            def _(s, half=half):
                pe.wait(s)
                ac.inc(s.copy(BROW2[:, half * 1024:(half + 1) * 1024], BPb[0:1, :]))
                vb_done[half] = ac.n

        @block.sync
        def _(sync):
            ac.wait(sync, vb_done[1])
            sy.dma(sync.dma_start(out=bb[1:2, :], in_=BROW2[:]))
            sy.wait(sync)
            sy.dma(sync.dma_start(out=B16b[:], in_=bb[1, :].rearrange("(a c) -> a c", c=128)))
        ldb2 = sy.n

        @block.tensor
        def _(t):
            sy.wait(t, ldb2)
            pe.inc(t.transpose(BPb[:, 0:16], B16b[:], P_identf[0:16, 0:16]))

        @block.vector
        def _(v):
            pe.wait(v)
            dv.inc(v.tensor_copy(P_vb[:], BPb[:, 0:16]))
        pvb_done = dv.n

        @block.tensor
        def _(t):
            dv.wait(t, pvb_done)
            for rc in range(4):
                for j in range(2):
                    ins = t.matmul(BPb[0:1, j * 512:(j + 1) * 512],
                                   P_kbb[:, rc:rc + 1], WKR[:, rc, j * 512:(j + 1) * 512],
                                   start=(rc == 0), stop=(rc == 3))
            pe.inc(ins)

        @block.scalar
        def _(s):
            pe.wait(s)
            ac.inc(s.copy(BROW2[:, 0:1024], BPb[0:1, :]))

        @block.tensor
        def _(t):
            ac.wait(t)
            for j in range(2):
                ins = t.matmul(BPb[:, j * 512:(j + 1) * 512], P_ones[:],
                               BROW2[:, j * 512:(j + 1) * 512], start=True, stop=True)
            pe.inc(ins)

        @block.vector
        def _(v):
            pe.wait(v)
            dv.inc(v.tensor_copy(P_krb[:], BPb[:]))
        krb_done = dv.n

        @block.gpsimd
        def _(g):
            dv.wait(g, krb_done)
            for rc in range(4):
                g.tensor_scalar_mul(Wku[:, rc], Wku[:, rc], P_kg[:, rc:rc + 1])
            for rc in range(4):
                g.tensor_scalar_mul(Wvu[:, rc], Wvu[:, rc], P_kg[:, rc:rc + 1])
            for rc in range(4):
                ins = g.tensor_scalar_mul(WKR[:, rc], WKR[:, rc], P_kg[:, rc:rc + 1])
            gp.inc(ins)
        fold_k = gp.n

        kr_done, kr_st = {}, {}
        for tt in range(NT):
            @block.tensor
            def _(t, tt=tt):
                gp.wait(t, fold_k)
                if tt > 0:
                    dv.wait(t, kr_done[tt - 1])
                for rc in range(4):
                    for j in range(2):
                        ins = t.matmul(KP[:, j * 512:(j + 1) * 512],
                                       ckv_fm[:, rc, tt * 128:(tt + 1) * 128],
                                       WKR[:, rc, j * 512:(j + 1) * 512],
                                       start=(rc == 0), stop=(rc == 3))
                pe.inc(ins)

            @block.vector
            def _(v, tt=tt):
                pe.wait(v)
                if tt > 0:
                    sy.wait(v, kr_st[tt - 1])
                v.tensor_tensor(KRT[:], KP[:], P_krb[:], ALU.add)
                v.drain()
                q4 = KRT[:].rearrange("p (h d f) -> p h d f", h=16, d=2)
                t4 = RTMPb[:].rearrange("p (h d f) -> p h d f", h=16, d=2)
                o4 = KRB[:].rearrange("p (h d f) -> p h d f", h=16, d=2)
                cosb = P_cos[:, tt, None, None, :].broadcast_to((128, 16, 2, 32))
                sinh = P_sin[:, tt, None, :].broadcast_to((128, 16, 32))
                r2 = RT2b[:].rearrange("p (h f) -> p h f", h=16)
                v.tensor_tensor(t4, q4, cosb, ALU.mult)
                v.scalar_tensor_tensor(r2, q4[:, :, 1, :], -1.0, sinh, ALU.mult, ALU.mult)
                v.drain()
                v.tensor_tensor(o4[:, :, 0, :], t4[:, :, 0, :], r2, ALU.add)
                v.drain()
                v.tensor_tensor(r2, q4[:, :, 0, :], sinh, ALU.mult)
                v.drain()
                v.tensor_tensor(o4[:, :, 1, :], t4[:, :, 1, :], r2, ALU.add)
                dv.inc(v.drain())
                kr_done[tt] = dv.n

            @block.sync
            def _(sync, tt=tt):
                dv.wait(sync, kr_done[tt])
                sy.dma(sync.dma_start(out=kr_tm[tt * 128:(tt + 1) * 128, :], in_=KRB[:]))
                kr_st[tt] = sy.n
        barrier()

    with contextlib.ExitStack() as _attn_stack:
        _AE = _attn_stack.enter_context
        KRF0 = _AE(nc.sbuf_tensor("KRF0", [128, S], BF16))
        KRF1 = _AE(nc.sbuf_tensor("KRF1", [128, S], BF16))
        QRP0 = _AE(nc.sbuf_tensor("QRP0", [128, OWN], BF16))
        QRP1 = _AE(nc.sbuf_tensor("QRP1", [128, OWN], BF16))
        QCL0 = _AE(nc.sbuf_tensor("QCL0", [128, OWN], BF16))
        QCL1 = _AE(nc.sbuf_tensor("QCL1", [128, OWN], BF16))
        KC = _AE(nc.sbuf_tensor("KC", [128, S], BF16))
        VSB = _AE(nc.sbuf_tensor("VSB", [128, 32, 128], BF16))
        PA0 = _AE(nc.sbuf_tensor("PA0", [128, 2048], BF16))
        PA1 = _AE(nc.sbuf_tensor("PA1", [128, 2048], BF16))
        PB0 = _AE(nc.sbuf_tensor("PB0", [128, 2048], BF16))
        PB1 = _AE(nc.sbuf_tensor("PB1", [128, 2048], BF16))
        PTS = _AE(nc.sbuf_tensor("PTS", [128, 2048], BF16))
        SSTT = _AE(nc.sbuf_tensor("SSTT", [128, 48], F32))
        OT1 = _AE(nc.sbuf_tensor("OT1", [128, 128], F32))
        OT2 = _AE(nc.sbuf_tensor("OT2", [128, 128], BF16))
        SC_ps = _AE(nc.psum_tensor("SC_ps", [128, 2048], F32))
        PT_ps = _AE(nc.psum_tensor("PT_ps", [128, 2048], BF16))
        O_ps = _AE(nc.psum_tensor("O_ps", [128, 256], F32))
        KB_ps = _AE(nc.psum_tensor("KB_ps", [128, 512], F32))
        KRF = [KRF0, KRF1]
        QRP = [QRP0, QRP1]
        QCL = [QCL0, QCL1]
        krf_used, qcl_used = {}, {}
        kc_evac, v_evac = {}, {}
        exp_done, pts_evac, pv_done, ocomb, ofm_evac, pts_pe = {}, {}, {}, {}, {}, {}
        stats_done = {0: 0, 1: 0}
        for h in range(H):
            hb = h % 2
            pj = h // 2
            if hb == 0:
                @block.sync
                def _(sync, pj=pj):
                    if pj >= 2:
                        pe.wait(sync, krf_used[pj - 2])
                    sy.dma(sync.dma_start_transpose(KRF[pj % 2][:],
                                                    kr_tm[:, pj * 128:(pj + 1) * 128]))
                    sy.dma(sync.dma_start(out=QRP[pj % 2][:], in_=qr_hbm[pj]))

            @block.sync
            def _(sync, h=h, hb=hb):
                if h >= 2:
                    pe.wait(sync, qcl_used[h - 2])
                sy.dma(sync.dma_start(out=QCL[hb][:], in_=qc_hbm[h]))
            ldq = sy.n

            for nt in range(8):
                @block.tensor
                def _(t, h=h, nt=nt):
                    if nt == 0:
                        if h > 0:
                            dv.wait(t, v_evac[(h - 1, 7)])
                    else:
                        ac.wait(t, kc_evac[(h, nt - 1)])
                    for rc in range(4):
                        ins = t.matmul(KB_ps[:], Wku[:, rc, h * 128:(h + 1) * 128],
                                       ckv_fm[:, rc, nt * 512:(nt + 1) * 512],
                                       start=(rc == 0), stop=(rc == 3))
                    pe.inc(ins)

                @block.scalar
                def _(s, h=h, nt=nt):
                    pe.wait(s)
                    ac.inc(s.copy(KC[:, nt * 512:(nt + 1) * 512], KB_ps[:]))
                    kc_evac[(h, nt)] = ac.n

            for g8 in range(8):
                @block.tensor
                def _(t, h=h, g8=g8):
                    if g8 == 0:
                        ac.wait(t, kc_evac[(h, 7)])
                    else:
                        dv.wait(t, v_evac[(h, g8 - 1)])
                    for kt in range(4):
                        for rc in range(4):
                            ins = t.matmul(KB_ps[:, kt * 128:(kt + 1) * 128],
                                           ckv_fm[:, rc, (4 * g8 + kt) * 128:(4 * g8 + kt + 1) * 128],
                                           Wvu[:, rc, h * 128:(h + 1) * 128],
                                           start=(rc == 0), stop=(rc == 3))
                    pe.inc(ins)

                @block.vector
                def _(v, h=h, g8=g8):
                    pe.wait(v)
                    dv.inc(v.tensor_copy(VSB[:, 4 * g8:4 * (g8 + 1)],
                                         KB_ps[:].rearrange("p (a c) -> p a c", c=128)))
                    v_evac[(h, g8)] = dv.n

            PBUF = [[PA0, PA1], [PB0, PB1]]

            def emit_scores_half(qt, half, h=h, hb=hb, pj=pj, ldq=ldq):
                par = qt % 2
                SST = SSTT[:, 24 * par:24 * par + 24]
                chunk_pe = {}

                @block.tensor
                def _(t, qt=qt, half=half):
                    if qt == 0 and half == 0:
                        sy.wait(t, ldq)
                        dv.wait(t, v_evac[(h, 7)])
                    prev = exp_done.get((h, qt, half - 1),
                                        exp_done.get((h, qt - 1, 1),
                                                     exp_done.get((h - 1, NTQ - 1, 1), 0)))
                    ac.wait(t, prev)
                    for kc in range(4):
                        t.matmul(SC_ps[:, kc * 512:(kc + 1) * 512],
                                 QCL[hb][:, qt * 128:(qt + 1) * 128],
                                 KC[:, (half * 4 + kc) * 512:(half * 4 + kc + 1) * 512],
                                 start=True, stop=False)
                    qr_l = QRP[pj % 2][0:64, qt * 128:(qt + 1) * 128] if hb == 0 else \
                        QRP[pj % 2][64:128, qt * 128:(qt + 1) * 128]
                    for kc in range(4):
                        kr_r = KRF[pj % 2][0:64, (half * 4 + kc) * 512:(half * 4 + kc + 1) * 512] \
                            if hb == 0 else \
                            KRF[pj % 2][64:128, (half * 4 + kc) * 512:(half * 4 + kc + 1) * 512]
                        ins = t.matmul(SC_ps[:, kc * 512:(kc + 1) * 512], qr_l, kr_r,
                                       start=False, stop=True)
                        pe.inc(ins)
                        chunk_pe[kc] = pe.n
                    if qt == NTQ - 1 and half == 1:
                        qcl_used[h] = pe.n
                        if hb == 1:
                            krf_used[pj] = pe.n

                @block.vector
                def _(v, qt=qt, half=half, SST=SST, chunk_pe=chunk_pe):
                    base = 4 * half
                    for kc in range(4):
                        pe.wait(v, chunk_pe[kc])
                        v.reduce_max(SST[:, base + kc:base + kc + 1],
                                     SC_ps[:, kc * 512:(kc + 1) * 512], axis=AX, negate=True)
                    v.drain()
                    v.tensor_tensor(SST[:, 22:23], SST[:, base:base + 1],
                                    SST[:, base + 1:base + 2], ALU.min)
                    v.tensor_tensor(SST[:, 23:24], SST[:, base + 2:base + 3],
                                    SST[:, base + 3:base + 4], ALU.min)
                    v.drain()
                    v.tensor_tensor(SST[:, 8 + half:9 + half], SST[:, 22:23],
                                    SST[:, 23:24], ALU.min)
                    dv.inc(v.drain())

                @block.scalar
                def _(s, qt=qt, half=half, SST=SST, par=par):
                    dv.wait(s)
                    PH = PBUF[half][par]
                    pe.wait(s, pts_pe.get((half, par), 0))
                    ac.inc(s.activation(PH[:], SC_ps[:], AF.Exp,
                                        bias=SST[:, 8 + half:9 + half],
                                        accum_out=SST[:, 20 + half:21 + half]))
                    exp_done[(h, qt, half)] = ac.n

            def emit_stats(qt, h=h):
                par = qt % 2
                SST = SSTT[:, 24 * par:24 * par + 24]

                @block.vector
                def _(v, qt=qt, SST=SST):
                    ac.wait(v, exp_done[(h, qt, 1)])
                    v.tensor_tensor(SST[:, 10:11], SST[:, 8:9], SST[:, 9:10], ALU.min)
                    v.drain()
                    v.tensor_sub(SST[:, 11:12], SST[:, 10:11], SST[:, 8:9])
                    v.tensor_sub(SST[:, 12:13], SST[:, 10:11], SST[:, 9:10])
                    dv.inc(v.drain())

                @block.scalar
                def _(s, SST=SST):
                    dv.wait(s)
                    ac.inc(s.activation(SST[:, 13:15], SST[:, 11:13], AF.Exp))

                @block.vector
                def _(v, qt=qt, SST=SST):
                    ac.wait(v)
                    v.tensor_tensor(SST[:, 15:16], SST[:, 13:14], SST[:, 20:21], ALU.mult)
                    v.drain()
                    v.scalar_tensor_tensor(SST[:, 16:17], SST[:, 14:15], SST[:, 21:22],
                                           SST[:, 15:16], ALU.mult, ALU.add)
                    v.drain()
                    v.reciprocal(SST[:, 17:18], SST[:, 16:17])
                    v.drain()
                    v.tensor_tensor(SST[:, 18:20], SST[:, 13:15],
                                    SST[:, 17:18].broadcast_to((128, 2)), ALU.mult)
                    dv.inc(v.drain())
                stats_done[par] = dv.n

            def emit_pv_half(qt, half, h=h):
                par = qt % 2
                PH = PBUF[half][par]

                @block.tensor
                def _(t, qt=qt, half=half, PH=PH, par=par):
                    ac.wait(t, exp_done[(h, qt, half)])
                    dv.wait(t, pts_evac.get("last", 0))
                    ac.wait(t, ofm_evac.get("last", 0))
                    if half == 0:
                        dv.wait(t, ocomb.get("last", 0))
                    for i in range(16):
                        ins = t.transpose(PT_ps[:, i * 128:(i + 1) * 128],
                                          PH[:, i * 128:(i + 1) * 128], P_ident[:])
                    pe.inc(ins)
                    pts_pe[(half, par)] = pe.n

                @block.vector
                def _(v):
                    pe.wait(v)
                    dv.inc(v.tensor_copy(PTS[:], PT_ps[:]))
                    pts_evac["last"] = dv.n

                @block.tensor
                def _(t, qt=qt, half=half):
                    dv.wait(t, pts_evac["last"])
                    for kc in range(16):
                        ins = t.matmul(O_ps[:, half * 128:(half + 1) * 128],
                                       PTS[:, kc * 128:(kc + 1) * 128],
                                       VSB[:, half * 16 + kc],
                                       start=(kc == 0), stop=(kc == 15))
                    pe.inc(ins)
                    pv_done[half] = pe.n

            def emit_ocombine(qt, h=h):
                par = qt % 2
                SST = SSTT[:, 24 * par:24 * par + 24]

                @block.vector
                def _(v, qt=qt, SST=SST, par=par):
                    pe.wait(v, pv_done[1])
                    dv.wait(v, stats_done[par])
                    v.tensor_scalar_mul(OT1[:], O_ps[:, 0:128], SST[:, 18:19])
                    v.drain()
                    v.scalar_tensor_tensor(OT2[:], O_ps[:, 128:256], SST[:, 19:20],
                                           OT1[:], ALU.mult, ALU.add)
                    dv.inc(v.drain())
                    ocomb["last"] = dv.n

                @block.tensor
                def _(t, qt=qt):
                    dv.wait(t, ocomb["last"])
                    ac.wait(t, ofm_evac.get("last", 0))
                    pe.inc(t.transpose(PT_ps[:, 0:128], OT2[:], P_ident[:]))

                @block.scalar
                def _(s, qt=qt):
                    pe.wait(s)
                    ac.inc(s.activation(o_fm[:, h, qt * 128:(qt + 1) * 128], PT_ps[:, 0:128],
                                        AF.Identity, bias=P_vb[:, h:h + 1]))
                    ofm_evac["last"] = ac.n

            emit_scores_half(0, 0)
            emit_scores_half(0, 1)
            emit_stats(0)
            for qt in range(1, NTQ):
                emit_scores_half(qt, 0)
                emit_pv_half(qt - 1, 0)
                emit_scores_half(qt, 1)
                emit_pv_half(qt - 1, 1)
                emit_stats(qt)
                emit_ocombine(qt - 1)
            emit_pv_half(NTQ - 1, 0)
            emit_pv_half(NTQ - 1, 1)
            emit_ocombine(NTQ - 1)
        barrier()

    stack_c.close()

    if debug:
        with nc.sbuf_tensor("DBG4", [128, 2048], F32) as DBG4:
            dump_tap(lambda a: o_fm[:].rearrange("p a c -> p (a c)")[:, a * 2048:(a + 1) * 2048],
                     taps["t_ofm"], 8, 2048, DBG4)

    with nc.sbuf_tensor("WOL", [128, 16, D], BF16) as WOL, \
         nc.sbuf_tensor("XRES", [128, D], F32) as XRES, \
         nc.sbuf_tensor("FRES", [128, D], F32) as FRES, \
         nc.sbuf_tensor("S1", [128, D], F32) as S1, \
         nc.sbuf_tensor("OUTB", [128, D], F32) as OUTB, \
         nc.psum_tensor("Y2", [128, 2048], F32) as Y2:

        @block.gpsimd
        def _(g):
            for a in range(4):
                gd.dma(g.dma_start(out=WOL[:, 4 * a:4 * (a + 1)],
                                   in_=wo[a * 512:(a + 1) * 512, :].rearrange("(a p) c -> p a c", p=128)))
        ldwo = gd.n

        res_used = {}
        for qt in range(NTQ):
            @block.sync
            def _(sync, qt=qt):
                if qt > 0:
                    dv.wait(sync, res_used[qt - 1])
                sy.dma(sync.dma_start(out=XRES[:], in_=xb[qt * 128:(qt + 1) * 128, :]))
                sy.dma(sync.dma_start(out=FRES[:], in_=ffn_y[qt * 128:(qt + 1) * 128, :]))
            ldr = sy.n

            @block.tensor
            def _(t, qt=qt):
                if qt == 0:
                    gd.wait(t, ldwo)
                else:
                    dv.wait(t, res_used[qt - 1])
                for fc in range(H):
                    for jj in range(4):
                        ins = t.matmul(Y2[:, jj * 512:(jj + 1) * 512],
                                       o_fm[:, fc, qt * 128:(qt + 1) * 128],
                                       WOL[:, fc, jj * 512:(jj + 1) * 512],
                                       start=(fc == 0), stop=(fc == H - 1))
                pe.inc(ins)

            @block.vector
            def _(v, qt=qt, ldr=ldr):
                sy.wait(v, ldr)
                v.tensor_tensor(S1[:], XRES[:], FRES[:], ALU.add)
                pe.wait(v)
                v.drain()
                v.tensor_tensor(OUTB[:], S1[:], Y2[:], ALU.add)
                dv.inc(v.drain())
                res_used[qt] = dv.n

            @block.sync
            def _(sync, qt=qt):
                dv.wait(sync, res_used[qt])
                sy.dma(sync.dma_start(out=y[qt * 128:(qt + 1) * 128, :], in_=OUTB[:]))

        @block.sync
        def _(sync):
            sy.wait(sync)

    stx.close()
    return nc, taps


def _prep_inputs(inputs):
    x = np.asarray(inputs["L0_attn_x"])
    f32 = np.float32
    wg_u8 = np.asarray(inputs["L0_ffn_wq_gate"]).view(np.uint8).reshape(HID, D, 4)[:, :, 0]
    wu_u8 = np.asarray(inputs["L0_ffn_wq_up"]).view(np.uint8).reshape(HID, D, 4)[:, :, 0]
    wd_u8 = np.asarray(inputs["L0_ffn_wq_down"]).view(np.uint8).reshape(D, HID, 4)[:, :, 0]
    shared = {
        "wqd": np.ascontiguousarray(inputs["L0_attn_Wq_down"], f32),
        "wqu": np.ascontiguousarray(inputs["L0_attn_Wq_up"], f32),
        "wqr": np.ascontiguousarray(inputs["L0_attn_Wq_rope"], f32),
        "wkd": np.ascontiguousarray(inputs["L0_attn_Wkv_down"], f32),
        "wku": np.ascontiguousarray(inputs["L0_attn_Wk_up"], f32),
        "wvu": np.ascontiguousarray(inputs["L0_attn_Wv_up"], f32),
        "wkr": np.ascontiguousarray(inputs["L0_attn_Wk_rope"], f32),
        "wo": np.ascontiguousarray(inputs["L0_attn_Wout"], f32),
        "qg2": np.ascontiguousarray(np.asarray(inputs["L0_attn_q_gamma"], f32).reshape(12, 128).T),
        "qb2": np.ascontiguousarray(np.asarray(inputs["L0_attn_q_beta"], f32).reshape(12, 128).T),
        "kg2": np.ascontiguousarray(np.asarray(inputs["L0_attn_kv_gamma"], f32).reshape(4, 128).T),
        "kb2": np.ascontiguousarray(np.asarray(inputs["L0_attn_kv_beta"], f32).reshape(4, 128).T),
        "fg": np.ascontiguousarray(wg_u8.reshape(NHC, 128, 16, 128).transpose(0, 3, 2, 1)),
        "fu": np.ascontiguousarray(wu_u8.reshape(NHC, 128, 16, 128).transpose(0, 3, 2, 1)),
        "fd": np.ascontiguousarray(wd_u8.reshape(D, NHC, 128).transpose(1, 2, 0)),
        "sg2": np.ascontiguousarray(np.asarray(inputs["L0_ffn_s_gate"], f32).reshape(NHC, 128).T),
        "su2": np.ascontiguousarray(np.asarray(inputs["L0_ffn_s_up"], f32).reshape(NHC, 128).T),
        "sdr": np.ascontiguousarray(np.asarray(inputs["L0_ffn_s_down"], f32).reshape(1, D)),
    }
    per_core = []
    for c in range(8):
        b, qs = c // 4, c % 4
        perm = np.concatenate([np.arange(qs * OWN, (qs + 1) * OWN),
                               np.arange(0, qs * OWN),
                               np.arange((qs + 1) * OWN, S)])
        m = dict(shared)
        m["xb"] = np.ascontiguousarray(x[b][perm], f32)
        m["posP"] = np.ascontiguousarray(perm.astype(f32).reshape(NT, 128).T)
        per_core.append(m)
    return per_core


def kernel(**inputs):
    per_core = _prep_inputs(inputs)
    nc, _ = build_program(debug=DEBUG)
    res = run_bass_kernel_spmd(nc, per_core, core_ids=list(range(8)))
    out = np.empty((B, S, D), np.float32)
    for c in range(8):
        b, qs = c // 4, c % 4
        out[b, qs * OWN:(qs + 1) * OWN] = res.results[c]["y"]
    return out


# version 04
# speedup vs baseline: 1.5227x; 1.2987x over previous
import contextlib
import numpy as np
import concourse.bass as bass
import concourse.mybir as mybir
from concourse.bass_utils import run_bass_kernel_spmd

F32 = mybir.dt.float32
BF16 = mybir.dt.bfloat16
U8 = mybir.dt.uint8
I32 = mybir.dt.int32
AF = mybir.ActivationFunctionType
ALU = mybir.AluOpType
AX = mybir.AxisListType.X

B, S, D = 2, 4096, 2048
H, DH, DR = 16, 128, 64
QR, KVR = 1536, 512
HID, NHC = 5632, 44
NT, NTQ = 32, 8
OWN = 1024
SCALE = float((DH + DR) ** -0.5)
EPS = 1e-5
TWO_PI = 2 * np.pi
CW0 = float(np.float32(6.28125))
CW1 = float(np.float32(TWO_PI - 6.28125))
CW2 = float(TWO_PI - np.float64(np.float32(6.28125)) - np.float64(np.float32(TWO_PI - 6.28125)))
MAGIC = float(np.float32(1.5 * 2 ** 23))

DEBUG = False


class Sem:
    __slots__ = ("h", "n")

    def __init__(self, h):
        self.h = h
        self.n = 0

    def inc(self, instr, k=1):
        instr.then_inc(self.h, k)
        self.n += k
        return self.n

    def dma(self, instr):
        return self.inc(instr, 16)

    def wait(self, eng, n=None):
        eng.wait_ge(self.h, self.n if n is None else n)


def build_program(debug=False):
    nc = bass.Bass("TRN2", target_bir_lowering=False)

    xb = nc.dram_tensor("xb", [S, D], F32, kind="ExternalInput").ap()
    wqd = nc.dram_tensor("wqd", [D, QR], F32, kind="ExternalInput").ap()
    wqu = nc.dram_tensor("wqu", [QR, D], F32, kind="ExternalInput").ap()
    wqr = nc.dram_tensor("wqr", [QR, H * DR], F32, kind="ExternalInput").ap()
    wkd = nc.dram_tensor("wkd", [D, KVR], F32, kind="ExternalInput").ap()
    wku = nc.dram_tensor("wku", [KVR, D], F32, kind="ExternalInput").ap()
    wvu = nc.dram_tensor("wvu", [KVR, D], F32, kind="ExternalInput").ap()
    wkr = nc.dram_tensor("wkr", [KVR, H * DR], F32, kind="ExternalInput").ap()
    wo = nc.dram_tensor("wo", [D, D], F32, kind="ExternalInput").ap()
    qg2 = nc.dram_tensor("qg2", [128, 12], F32, kind="ExternalInput").ap()
    qb2 = nc.dram_tensor("qb2", [128, 12], F32, kind="ExternalInput").ap()
    kg2 = nc.dram_tensor("kg2", [128, 4], F32, kind="ExternalInput").ap()
    kb2 = nc.dram_tensor("kb2", [128, 4], F32, kind="ExternalInput").ap()
    posP = nc.dram_tensor("posP", [128, NT], F32, kind="ExternalInput").ap()
    fg = nc.dram_tensor("fg", [NHC, 128, 16, 128], U8, kind="ExternalInput").ap()
    fu = nc.dram_tensor("fu", [NHC, 128, 16, 128], U8, kind="ExternalInput").ap()
    fd = nc.dram_tensor("fd", [NHC, 128, D], U8, kind="ExternalInput").ap()
    sg2 = nc.dram_tensor("sg2", [128, NHC], F32, kind="ExternalInput").ap()
    su2 = nc.dram_tensor("su2", [128, NHC], F32, kind="ExternalInput").ap()
    sdr = nc.dram_tensor("sdr", [1, D], F32, kind="ExternalInput").ap()

    y = nc.dram_tensor("y", [OWN, D], F32, kind="ExternalOutput").ap()

    kr_tm = nc.dram_tensor("kr_tm", [S, H * DR], BF16, kind="Internal").ap()
    qc_hbm = nc.dram_tensor("qc_hbm", [H, 128, OWN], BF16, kind="Internal").ap()
    qr_hbm = nc.dram_tensor("qr_hbm", [8, 128, OWN], BF16, kind="Internal").ap()
    ffn_y = nc.dram_tensor("ffn_y", [OWN, D], F32, kind="Internal").ap()
    bb = nc.dram_tensor("bb", [4, D], F32, kind="Internal").ap()

    taps = {}
    if debug:
        taps["t_xn"] = nc.dram_tensor("t_xn", [128, 16 * OWN], F32, kind="ExternalOutput").ap()
        taps["t_ckv"] = nc.dram_tensor("t_ckv", [128, 4 * S], F32, kind="ExternalOutput").ap()
        taps["t_cq"] = nc.dram_tensor("t_cq", [128, 12 * OWN], F32, kind="ExternalOutput").ap()
        taps["t_qr"] = nc.dram_tensor("t_qr", [128, 8 * OWN], F32, kind="ExternalOutput").ap()
        taps["t_ffn"] = nc.dram_tensor("t_ffn", [OWN, D], F32, kind="ExternalOutput").ap()
        taps["t_ofm"] = nc.dram_tensor("t_ofm", [128, H * OWN], F32, kind="ExternalOutput").ap()
        taps["t_cos"] = nc.dram_tensor("t_cos", [128, NT * 32], F32, kind="ExternalOutput").ap()
        taps["t_ifr"] = nc.dram_tensor("t_ifr", [1, 32], F32, kind="ExternalOutput").ap()

    stx = contextlib.ExitStack()
    E = stx.enter_context

    sy = Sem(E(nc.semaphore("sy")))
    gd = Sem(E(nc.semaphore("gd")))
    pe = Sem(E(nc.semaphore("pe")))
    dv = Sem(E(nc.semaphore("dv")))
    ac = Sem(E(nc.semaphore("ac")))
    gp = Sem(E(nc.semaphore("gp")))
    sems = (sy, gd, pe, dv, ac, gp)

    block = E(nc.Block())

    def barrier():
        snap = [(s.h, s.n) for s in sems]
        for name in ("sync", "vector", "scalar", "tensor", "gpsimd"):
            def f(e, snap=snap):
                for h, n in snap:
                    e.wait_ge(h, n)
            getattr(block, name)(f)

    def dump_tap(src_ap_fn, dst_ap, nchunks, chunk_cols, scratch):
        for a in range(nchunks):
            @block.scalar
            def _(s, a=a):
                ac.inc(s.copy(scratch[:, 0:chunk_cols], src_ap_fn(a)))

            @block.sync
            def _(sync, a=a):
                ac.wait(sync)
                sy.dma(sync.dma_start(out=dst_ap[:, a * chunk_cols:(a + 1) * chunk_cols],
                                      in_=scratch[:, 0:chunk_cols]))
            barrier()

    P_ident = E(nc.sbuf_tensor("P_ident", [128, 128], BF16))
    P_identf = E(nc.sbuf_tensor("P_identf", [128, 128], F32))
    P_ones = E(nc.sbuf_tensor("P_ones", [1, 128], F32))
    P_cos = E(nc.sbuf_tensor("P_cos", [128, NT, 32], F32))
    P_sin = E(nc.sbuf_tensor("P_sin", [128, NT, 32], F32))
    P_qg = E(nc.sbuf_tensor("P_qg", [128, 12], F32))
    P_qb = E(nc.sbuf_tensor("P_qb", [128, 12], F32))
    P_kg = E(nc.sbuf_tensor("P_kg", [128, 4], F32))
    P_kb = E(nc.sbuf_tensor("P_kb", [128, 4], F32))
    P_qbb = E(nc.sbuf_tensor("P_qbb", [128, 12], BF16))
    P_kbb = E(nc.sbuf_tensor("P_kbb", [128, 4], BF16))
    P_sg = E(nc.sbuf_tensor("P_sg", [128, NHC], F32))
    P_su = E(nc.sbuf_tensor("P_su", [128, NHC], F32))
    P_qcb = E(nc.sbuf_tensor("P_qcb", [128, 16], F32))
    P_n128 = E(nc.sbuf_tensor("P_n128", [128, 1], F32))
    P_vb = E(nc.sbuf_tensor("P_vb", [128, 16], F32))

    ckv_fm = E(nc.sbuf_tensor("ckv_fm", [128, 4, S], BF16))
    ST1 = E(nc.sbuf_tensor("ST1", [128, 16], F32))
    ST2 = E(nc.sbuf_tensor("ST2", [128, 8], F32))
    stack_a = contextlib.ExitStack()
    xn_fm = stack_a.enter_context(nc.sbuf_tensor("xn_fm", [128, 16, OWN], BF16))

    @block.sync
    def _(sync):
        sy.dma(sync.dma_start(out=P_qg[:], in_=qg2[:, :]))
        sy.dma(sync.dma_start(out=P_qb[:], in_=qb2[:, :]))
        sy.dma(sync.dma_start(out=P_kg[:], in_=kg2[:, :]))
        sy.dma(sync.dma_start(out=P_kb[:], in_=kb2[:, :]))
        sy.dma(sync.dma_start(out=P_sg[:], in_=sg2[:, :]))
        sy.dma(sync.dma_start(out=P_su[:], in_=su2[:, :]))
    ld0 = sy.n

    with nc.sbuf_tensor("T_pos", [128, NT], F32) as T_pos, \
         nc.sbuf_tensor("T_ioti", [1, 32], I32) as T_ioti, \
         nc.sbuf_tensor("T_iot", [1, 32], F32) as T_iot, \
         nc.sbuf_tensor("T_g", [1, 32], F32) as T_g, \
         nc.sbuf_tensor("T_m", [1, 32], F32) as T_m, \
         nc.sbuf_tensor("T_f", [1, 32], F32) as T_f, \
         nc.sbuf_tensor("T_ifb", [128, 32], F32) as T_ifb, \
         nc.sbuf_tensor("T_th", [128, NT, 32], F32) as T_th, \
         nc.sbuf_tensor("T_k", [128, NT, 32], F32) as T_k, \
         nc.sbuf_tensor("T_w", [128, NT, 32], F32) as T_w, \
         nc.psum_tensor("T_ps", [128, 512], F32) as T_ps:

        @block.sync
        def _(sync):
            sy.dma(sync.dma_start(out=T_pos[:], in_=posP[:, :]))
        ld1 = sy.n

        @block.gpsimd
        def _(g):
            g.memset(P_ident[:], 0.0)
            g.affine_select(out=P_ident[:], in_=P_ident[:], compare_op=ALU.not_equal,
                            fill=1.0, base=0, pattern=[[-1, 128]], channel_multiplier=1)
            g.memset(P_identf[:], 0.0)
            g.affine_select(out=P_identf[:], in_=P_identf[:], compare_op=ALU.not_equal,
                            fill=1.0, base=0, pattern=[[-1, 128]], channel_multiplier=1)
            g.memset(P_ones[:], 1.0)
            g.memset(P_n128[:], -128.0)
            g.iota(T_ioti[:], pattern=[[1, 32]], base=0, channel_multiplier=0)
            g.tensor_copy(T_iot[:], T_ioti[:])
            gp.inc(g.memset(T_m[:], float(-np.log(10000.0) / 32.0)))

        @block.scalar
        def _(s):
            gp.wait(s)
            ac.inc(s.activation(T_g[:], T_m[:], AF.Exp))

        @block.vector
        def _(v):
            ac.wait(v)
            v.tensor_tensor(T_m[:], T_g[:], T_g[:], ALU.mult)
            for _sq in range(4):
                v.drain()
                v.tensor_tensor(T_m[:], T_m[:], T_m[:], ALU.mult)
            v.drain()
            v.tensor_scalar(T_m[:], T_m[:], -10000.0, 1.0, ALU.mult, ALU.add)
            v.drain()
            v.tensor_scalar(T_m[:], T_m[:], float(1.0 / 32.0), 1.0, ALU.mult, ALU.add)
            v.drain()
            v.tensor_tensor(T_g[:], T_g[:], T_m[:], ALU.mult)
            v.drain()
            v.memset(T_f[:], 1.0)
            for k in range(5):
                b = 2 ** k
                sl = T_f[0:1, :].rearrange("p (a b) -> p a b", b=2 * b)[:, :, b:2 * b]
                v.tensor_scalar_mul(sl, sl, T_g[0:1, 0:1])
                v.drain()
                if k < 4:
                    v.tensor_tensor(T_g[:], T_g[:], T_g[:], ALU.mult)
                    v.drain()
            dv.inc(v.drain())

        @block.tensor
        def _(t):
            dv.wait(t)
            pe.inc(t.matmul(T_ps[:, 0:32], P_ones[:], T_f[:], start=True, stop=True))

        @block.vector
        def _(v):
            pe.wait(v)
            v.tensor_copy(T_ifb[:], T_ps[:, 0:32])
            sy.wait(v, ld1)
            v.drain()
            for t in range(NT):
                v.tensor_scalar_mul(T_th[:, t], T_ifb[:], T_pos[:, t:t + 1])
            v.drain()
            TH = T_th[:].rearrange("p a c -> p (a c)")
            K_ = T_k[:].rearrange("p a c -> p (a c)")
            W_ = T_w[:].rearrange("p a c -> p (a c)")
            v.tensor_scalar(K_, TH, float(1.0 / TWO_PI), MAGIC, ALU.mult, ALU.add)
            v.drain()
            v.tensor_scalar_add(K_, K_, -MAGIC)
            v.drain()
            v.scalar_tensor_tensor(W_, K_, -CW0, TH, ALU.mult, ALU.add)
            v.drain()
            v.scalar_tensor_tensor(W_, K_, -CW1, W_, ALU.mult, ALU.add)
            v.drain()
            v.scalar_tensor_tensor(W_, K_, -CW2, W_, ALU.mult, ALU.add)
            v.drain()
            v.tensor_scalar_min(W_, W_, float(np.pi))
            v.drain()
            v.tensor_scalar_max(W_, W_, float(-np.pi))
            v.drain()
            v.tensor_scalar_add(K_, W_, float(np.pi / 2))
            v.drain()
            v.tensor_scalar(TH, K_, float(np.pi), None, ALU.is_gt)
            v.drain()
            v.scalar_tensor_tensor(K_, TH, float(-TWO_PI), K_, ALU.mult, ALU.add)
            dv.inc(v.drain())

        @block.scalar
        def _(s):
            dv.wait(s)
            s.activation(P_sin[:].rearrange("p a c -> p (a c)"),
                         T_w[:].rearrange("p a c -> p (a c)"), AF.Sin)
            ac.inc(s.activation(P_cos[:].rearrange("p a c -> p (a c)"),
                                T_k[:].rearrange("p a c -> p (a c)"), AF.Sin))

        if debug:
            @block.sync
            def _(sync):
                ac.wait(sync)
                dv.wait(sync)
                sy.dma(sync.dma_start(out=taps["t_cos"][:, :],
                                      in_=P_cos[:].rearrange("p a c -> p (a c)")))
                sy.dma(sync.dma_start(out=taps["t_ifr"][:, :], in_=T_f[:]))
        barrier()

    def emit_ln(src_ap, n_feat, sq_ap, stats, out_ap):
        @block.vector
        def _(v):
            v.reduce_sum(stats[:, 0:1], src_ap, axis=AX)
            v.drain()
            v.tensor_scalar_mul(stats[:, 1:2], stats[:, 0:1], float(-1.0 / n_feat))
            dv.inc(v.drain())

        @block.scalar
        def _(s):
            dv.wait(s)
            ac.inc(s.activation(sq_ap, src_ap, AF.Square, bias=stats[:, 1:2],
                                accum_out=stats[:, 2:3]))

        @block.vector
        def _(v):
            ac.wait(v)
            v.tensor_scalar(stats[:, 3:4], stats[:, 2:3], float(1.0 / n_feat), EPS,
                            ALU.mult, ALU.add)
            dv.inc(v.drain())

        @block.scalar
        def _(s):
            dv.wait(s)
            ac.inc(s.activation(stats[:, 4:5], stats[:, 3:4], AF.Sqrt))

        @block.vector
        def _(v):
            ac.wait(v)
            v.reciprocal(stats[:, 5:6], stats[:, 4:5])
            v.drain()
            v.tensor_tensor(stats[:, 6:7], stats[:, 1:2], stats[:, 5:6], ALU.mult)
            v.drain()
            v.tensor_scalar(out_ap, src_ap, stats[:, 5:6], stats[:, 6:7],
                            ALU.mult, ALU.add)
            dv.inc(v.drain())

    with nc.sbuf_tensor("XT0", [128, D], F32) as XT0, \
         nc.sbuf_tensor("XT1", [128, D], F32) as XT1, \
         nc.sbuf_tensor("XNB0", [128, D], BF16) as XNB0, \
         nc.sbuf_tensor("XNB1", [128, D], BF16) as XNB1, \
         nc.sbuf_tensor("SQ", [128, 2048], F32) as SQ, \
         nc.psum_tensor("TRP", [128, 2048], BF16) as TRP:

        XT, XNB = [XT0, XT1], [XNB0, XNB1]
        x_done, evac_done = {}, {}
        for t in range(NTQ):
            bi = t % 2

            @block.sync
            def _(sync, t=t, bi=bi):
                if t >= 2:
                    dv.wait(sync, x_done[t - 2])
                sy.dma(sync.dma_start(out=XT[bi][:], in_=xb[t * 128:(t + 1) * 128, :]))
            ld = sy.n

            @block.vector
            def _(v, ld=ld):
                sy.wait(v, ld)
            emit_ln(XT[bi][:], D, SQ[:], ST1[:, 8 * bi:8 * bi + 8], XNB[bi][:])
            x_done[t] = dv.n

            @block.tensor
            def _(t_, t=t, bi=bi):
                dv.wait(t_)
                if t >= 1:
                    dv.wait(t_, evac_done[t - 1])
                for fc in range(16):
                    ins = t_.transpose(TRP[:, fc * 128:(fc + 1) * 128],
                                       XNB[bi][:, fc * 128:(fc + 1) * 128], P_ident[:])
                pe.inc(ins)

            @block.vector
            def _(v, t=t):
                pe.wait(v)
                dv.inc(v.tensor_copy(xn_fm[:, :, t * 128:(t + 1) * 128],
                                     TRP[:].rearrange("p (a c) -> p a c", c=128)))
                evac_done[t] = dv.n
        barrier()

    if debug:
        with nc.sbuf_tensor("DBG0", [128, 2048], F32) as DBG0:
            dump_tap(lambda a: xn_fm[:].rearrange("p a c -> p (a c)")[:, a * 2048:(a + 1) * 2048],
                     taps["t_xn"], 8, 2048, DBG0)

    NQ4 = 4
    QHC = NHC // NQ4
    with nc.sbuf_tensor("SDB", [128, D], F32) as SDB, \
         nc.sbuf_tensor("SDROW", [1, D], F32) as SDROW, \
         nc.sbuf_tensor("G8", [128, 2, D], U8) as G8b, \
         nc.sbuf_tensor("U8T", [128, 2, D], U8) as U8b, \
         nc.sbuf_tensor("D8", [128, 2, D], U8) as D8b, \
         nc.sbuf_tensor("GW", [128, 2, 16, 128], BF16) as GWb, \
         nc.sbuf_tensor("UW", [128, 2, 16, 128], BF16) as UWb, \
         nc.sbuf_tensor("SILU", [128, 2, OWN], BF16) as SILUb, \
         nc.sbuf_tensor("MQ", [128, QHC, OWN], BF16) as MQ, \
         nc.sbuf_tensor("DWQ", [128, QHC, D], BF16) as DWQ, \
         nc.sbuf_tensor("YSB", [128, D], F32) as YSB, \
         nc.sbuf_tensor("YOLD", [128, D], F32) as YOLD, \
         nc.psum_tensor("G_ps", [128, 1024], F32) as G_ps, \
         nc.psum_tensor("U_ps", [128, 1024], F32) as U_ps, \
         nc.psum_tensor("Y_ps", [128, 2048], F32) as Y_ps:

        @block.sync
        def _(sync):
            sy.dma(sync.dma_start(out=SDROW[:], in_=sdr[:, :]))
        ldsd = sy.n

        @block.tensor
        def _(t):
            sy.wait(t, ldsd)
            for j in range(4):
                ins = t.matmul(Y_ps[:, j * 512:(j + 1) * 512], P_ones[:],
                               SDROW[:, j * 512:(j + 1) * 512], start=True, stop=True)
            pe.inc(ins)

        @block.vector
        def _(v):
            pe.wait(v)
            dv.inc(v.tensor_copy(SDB[:], Y_ps[:]))
        sdb_done = dv.n

        fold_done, gu_done, silu_done, m_done = {}, {}, {}, {}
        down_done, yevac_done, dfold_done = {}, {}, {}
        yadd_done, yst_done, gd_ld, gfold_done = {}, {}, {}, {}
        for q4 in range(NQ4):
            for i in range(QHC):
                hc = q4 * QHC + i
                par = hc % 2
                G8, U8T, D8 = G8b[:, par], U8b[:, par], D8b[:, par]
                GW, UW, SILU = GWb[:, par], UWb[:, par], SILUb[:, par]

                @block.gpsimd
                def _(g, hc=hc, i=i, q4=q4, G8=G8, U8T=U8T, D8=D8):
                    if hc >= 2:
                        dv.wait(g, dfold_done[hc - 2])
                    if i == 0 and q4 > 0:
                        pe.wait(g, down_done[q4 - 1])
                    gd.dma(g.dma_start(out=G8, in_=fg[hc].rearrange("p a c -> p (a c)")))
                    gd.dma(g.dma_start(out=U8T, in_=fu[hc].rearrange("p a c -> p (a c)")))
                    gd.dma(g.dma_start(out=D8, in_=fd[hc]))
                    gd_ld[hc] = gd.n

                @block.vector
                def _(v, hc=hc, i=i, q4=q4, D8=D8, U8T=U8T, G8=G8, GW=GW, UW=UW):
                    gd.wait(v, gd_ld[hc])
                    if hc == 0:
                        dv.wait(v, sdb_done)
                    if hc >= 2:
                        pe.wait(v, gu_done[hc - 2])
                    v.tensor_scalar_add(GW.rearrange("p a c -> p (a c)"), G8, -128.0)
                    v.tensor_scalar_add(UW.rearrange("p a c -> p (a c)"), U8T, -128.0)
                    dv.inc(v.scalar_tensor_tensor(DWQ[:, i], D8, -128.0, SDB[:],
                                                  ALU.add, ALU.mult))
                    dfold_done[hc] = dv.n
                    gfold_done[hc] = dv.n

                @block.tensor
                def _(t, hc=hc, GW=GW, UW=UW):
                    dv.wait(t, dfold_done[hc])
                    if hc > 0:
                        ac.wait(t, silu_done[hc - 1])
                        dv.wait(t, m_done[hc - 1])
                    for dc in range(16):
                        for j in range(2):
                            t.matmul(G_ps[:, j * 512:(j + 1) * 512], GW[:, dc],
                                     xn_fm[:, dc, j * 512:(j + 1) * 512],
                                     start=(dc == 0), stop=(dc == 15))
                    for dc in range(16):
                        for j in range(2):
                            ins = t.matmul(U_ps[:, j * 512:(j + 1) * 512], UW[:, dc],
                                           xn_fm[:, dc, j * 512:(j + 1) * 512],
                                           start=(dc == 0), stop=(dc == 15))
                    pe.inc(ins)
                    gu_done[hc] = pe.n

                @block.scalar
                def _(s, hc=hc, SILU=SILU):
                    pe.wait(s, gu_done[hc])
                    if hc >= 2:
                        dv.wait(s, m_done[hc - 2])
                    ac.inc(s.activation(SILU, G_ps[:], AF.Silu, scale=P_sg[:, hc:hc + 1]))
                    silu_done[hc] = ac.n

                @block.vector
                def _(v, hc=hc, i=i, SILU=SILU):
                    ac.wait(v, silu_done[hc])
                    dv.inc(v.scalar_tensor_tensor(MQ[:, i], U_ps[:], P_su[:, hc:hc + 1],
                                                  SILU, ALU.mult, ALU.mult))
                    m_done[hc] = dv.n

            for tt in range(NTQ):
                if q4 > 0:
                    @block.sync
                    def _(sync, q4=q4, tt=tt):
                        if not (q4 == 1 and tt == 0):
                            dv.wait(sync, yadd_done.get(tt - 1, yadd_done.get(NTQ - 1, 0)))
                        sy.dma(sync.dma_start(out=YOLD[:], in_=ffn_y[tt * 128:(tt + 1) * 128, :]))
                    ldy = sy.n
                for ys in range(2):
                    YH = Y_ps[:, ys * 1024:(ys + 1) * 1024]

                    @block.tensor
                    def _(t, q4=q4, tt=tt, ys=ys, YH=YH):
                        if tt == 0 and ys == 0:
                            dv.wait(t, m_done[q4 * QHC + QHC - 1])
                            dv.wait(t, dfold_done[q4 * QHC + QHC - 1])
                        pk = (q4, tt - 1, ys)
                        pv_ = yevac_done.get(pk, yevac_done.get((q4 - 1, NTQ - 1, ys), None))
                        if pv_ is not None:
                            s_, n_ = pv_
                            s_.wait(t, n_)
                        for i in range(QHC):
                            for j in range(2):
                                ins = t.matmul(YH[:, j * 512:(j + 1) * 512],
                                               MQ[:, i, tt * 128:(tt + 1) * 128],
                                               DWQ[:, i, (2 * ys + j) * 512:(2 * ys + j + 1) * 512],
                                               start=(i == 0), stop=(i == QHC - 1))
                        pe.inc(ins)
                        if tt == NTQ - 1 and ys == 1:
                            down_done[q4] = pe.n

                    if q4 == 0:
                        @block.vector
                        def _(v, q4=q4, tt=tt, ys=ys, YH=YH):
                            pe.wait(v)
                            sy.wait(v, yst_done.get((q4, tt - 1), yst_done.get((q4 - 1, NTQ - 1), 0)))
                            dv.inc(v.tensor_copy(YSB[:, ys * 1024:(ys + 1) * 1024], YH))
                            yevac_done[(0, tt, ys)] = (dv, dv.n)
                    else:
                        @block.vector
                        def _(v, ldy=ldy, q4=q4, tt=tt, ys=ys, YH=YH):
                            sy.wait(v, ldy)
                            sy.wait(v, yst_done.get((q4, tt - 1), yst_done.get((q4 - 1, NTQ - 1), 0)))
                            pe.wait(v)
                            dv.inc(v.tensor_tensor(YSB[:, ys * 1024:(ys + 1) * 1024], YH,
                                                   YOLD[:, ys * 1024:(ys + 1) * 1024], ALU.add))
                            yevac_done[(q4, tt, ys)] = (dv, dv.n)
                            if ys == 1:
                                yadd_done[tt] = dv.n

                @block.sync
                def _(sync, q4=q4, tt=tt):
                    s_, n_ = yevac_done[(q4, tt, 1)]
                    s_.wait(sync, n_)
                    s_, n_ = yevac_done[(q4, tt, 0)]
                    s_.wait(sync, n_)
                    sy.dma(sync.dma_start(out=ffn_y[tt * 128:(tt + 1) * 128, :], in_=YSB[:]))
                    yst_done[(q4, tt)] = sy.n
        barrier()

    if debug:
        with nc.sbuf_tensor("DBGF", [128, D], F32) as DBGF:
            for tt in range(NTQ):
                @block.sync
                def _(sync, tt=tt):
                    sy.dma(sync.dma_start(out=DBGF[:], in_=ffn_y[tt * 128:(tt + 1) * 128, :]))
                    sy.wait(sync)
                    sy.dma(sync.dma_start(out=taps["t_ffn"][tt * 128:(tt + 1) * 128, :], in_=DBGF[:]))
                barrier()

    stack_b = contextlib.ExitStack()
    cq_fm = stack_b.enter_context(nc.sbuf_tensor("cq_fm", [128, 12, OWN], BF16))
    with nc.sbuf_tensor("WQD", [128, 16, QR], BF16) as WQD, \
         nc.sbuf_tensor("SQc", [128, QR], F32) as SQ, \
         nc.sbuf_tensor("CQT", [128, QR], BF16) as CQT, \
         nc.psum_tensor("CQ_ps", [128, QR], F32) as CQ_ps, \
         nc.psum_tensor("TR3", [128, QR], BF16) as TR3:

        @block.gpsimd
        def _(g):
            for a in range(4):
                gd.dma(g.dma_start(out=WQD[:, 4 * a:4 * (a + 1)],
                                   in_=wqd[a * 512:(a + 1) * 512, :].rearrange("(a p) c -> p a c", p=128)))
        ldw = gd.n

        cq_evac, cqt_done = {}, {}
        for t in range(NTQ):
            @block.tensor
            def _(t_, t=t):
                if t == 0:
                    gd.wait(t_, ldw)
                else:
                    dv.wait(t_, cqt_done[t - 1])
                for dc in range(16):
                    for j in range(3):
                        ins = t_.matmul(CQ_ps[:, j * 512:(j + 1) * 512],
                                        xn_fm[:, dc, t * 128:(t + 1) * 128],
                                        WQD[:, dc, j * 512:(j + 1) * 512],
                                        start=(dc == 0), stop=(dc == 15))
                pe.inc(ins)

            @block.vector
            def _(v):
                pe.wait(v)
            emit_ln(CQ_ps[:], QR, SQ[:, 0:QR], ST2, CQT[:])
            cqt_done[t] = dv.n

            @block.tensor
            def _(t_, t=t):
                dv.wait(t_, cqt_done[t])
                if t >= 1:
                    dv.wait(t_, cq_evac[t - 1])
                for rc in range(12):
                    ins = t_.transpose(TR3[:, rc * 128:(rc + 1) * 128],
                                       CQT[:, rc * 128:(rc + 1) * 128], P_ident[:])
                pe.inc(ins)

            @block.vector
            def _(v, t=t):
                pe.wait(v)
                dv.inc(v.tensor_copy(cq_fm[:, :, t * 128:(t + 1) * 128],
                                     TR3[:].rearrange("p (a c) -> p a c", c=128)))
                cq_evac[t] = dv.n
        barrier()

    if debug:
        with nc.sbuf_tensor("DBG1", [128, 2048], F32) as DBG1:
            dump_tap(lambda a: cq_fm[:].rearrange("p a c -> p (a c)")[:, a * 2048:(a + 1) * 2048],
                     taps["t_cq"], 6, 2048, DBG1)

    with nc.sbuf_tensor("WQU", [128, 12, D], BF16) as WQU, \
         nc.sbuf_tensor("BROW", [1, D], F32) as BROW, \
         nc.sbuf_tensor("B16", [16, 128], F32) as B16, \
         nc.sbuf_tensor("QCS", [128, OWN], BF16) as QCS, \
         nc.psum_tensor("QP", [128, 1024], F32) as QP, \
         nc.psum_tensor("BP", [128, 2048], F32) as BP:

        @block.gpsimd
        def _(g):
            for a in range(3):
                gd.dma(g.dma_start(out=WQU[:, 4 * a:4 * (a + 1)],
                                   in_=wqu[a * 512:(a + 1) * 512, :].rearrange("(a p) c -> p a c", p=128)))
        ldw3 = gd.n

        @block.vector
        def _(v):
            sy.wait(v, ld0)
            gd.wait(v, ldw3)
            v.tensor_copy(P_qbb[:], P_qb[:])
            dv.inc(v.tensor_copy(P_kbb[:], P_kb[:]))
        beta_done = dv.n

        @block.tensor
        def _(t):
            dv.wait(t, beta_done)
            for rc in range(12):
                for j in range(4):
                    ins = t.matmul(BP[0:1, j * 512:(j + 1) * 512],
                                   P_qbb[:, rc:rc + 1], WQU[:, rc, j * 512:(j + 1) * 512],
                                   start=(rc == 0), stop=(rc == 11))
            pe.inc(ins)

        @block.scalar
        def _(s):
            pe.wait(s)
            ac.inc(s.activation(BROW[:], BP[0:1, :], AF.Identity, scale=SCALE))

        @block.sync
        def _(sync):
            ac.wait(sync)
            sy.dma(sync.dma_start(out=bb[0:1, :], in_=BROW[:]))
            sy.wait(sync)
            sy.dma(sync.dma_start(out=B16[:], in_=bb[0, :].rearrange("(a c) -> a c", c=128)))
        ldb = sy.n

        @block.tensor
        def _(t):
            sy.wait(t, ldb)
            pe.inc(t.transpose(BP[:, 0:16], B16[:], P_identf[0:16, 0:16]))

        @block.vector
        def _(v):
            pe.wait(v)
            dv.inc(v.tensor_copy(P_qcb[:], BP[:, 0:16]))
        qcb_done = dv.n

        @block.vector
        def _(v):
            dv.wait(v, qcb_done)
            dv.inc(v.tensor_scalar_mul(ST1[:, 0:12], P_qg[:], SCALE))
        qgs_done = dv.n

        @block.gpsimd
        def _(g):
            dv.wait(g, qgs_done)
            for rc in range(12):
                ins = g.tensor_scalar_mul(WQU[:, rc], WQU[:, rc], ST1[:, rc:rc + 1])
            gp.inc(ins)
        fold_q = gp.n

        qc_done, qcs_st = {}, {}
        for h in range(H):
            @block.tensor
            def _(t, h=h):
                gp.wait(t, fold_q)
                if h > 0:
                    ac.wait(t, qc_done[h - 1])
                for rc in range(12):
                    for j in range(2):
                        ins = t.matmul(QP[:, j * 512:(j + 1) * 512],
                                       WQU[:, rc, h * 128:(h + 1) * 128],
                                       cq_fm[:, rc, j * 512:(j + 1) * 512],
                                       start=(rc == 0), stop=(rc == 11))
                pe.inc(ins)

            @block.scalar
            def _(s, h=h):
                pe.wait(s)
                if h > 0:
                    sy.wait(s, qcs_st[h - 1])
                ac.inc(s.activation(QCS[:], QP[:], AF.Identity, bias=P_qcb[:, h:h + 1]))
                qc_done[h] = ac.n

            @block.sync
            def _(sync, h=h):
                ac.wait(sync, qc_done[h])
                sy.dma(sync.dma_start(out=qc_hbm[h], in_=QCS[:]))
                qcs_st[h] = sy.n

        barrier()

    with nc.sbuf_tensor("WQR", [128, 12, H * DR], BF16) as WQR, \
         nc.sbuf_tensor("P_qrb", [128, H * DR], F32) as P_qrb, \
         nc.sbuf_tensor("BROWr", [1, H * DR], F32) as BROWr, \
         nc.sbuf_tensor("QRT", [128, OWN], F32) as QRT, \
         nc.sbuf_tensor("RTMP", [128, OWN], F32) as RTMP, \
         nc.sbuf_tensor("RT2", [128, 512], F32) as RT2, \
         nc.sbuf_tensor("QRB", [128, OWN], BF16) as QRB, \
         nc.sbuf_tensor("QRE", [128, OWN], BF16) as QRE, \
         nc.psum_tensor("QPr", [128, 1024], F32) as QP, \
         nc.psum_tensor("BPr", [128, 1024], F32) as BPr, \
         nc.psum_tensor("TR4r", [128, 1024], BF16) as TR4:

        @block.gpsimd
        def _(g):
            for a in range(3):
                gd.dma(g.dma_start(out=WQR[:, 4 * a:4 * (a + 1)],
                                   in_=wqr[a * 512:(a + 1) * 512, :].rearrange("(a p) c -> p a c", p=128)))
        ldw3r = gd.n

        @block.tensor
        def _(t):
            gd.wait(t, ldw3r)
            for rc in range(12):
                for j in range(2):
                    ins = t.matmul(BPr[0:1, j * 512:(j + 1) * 512],
                                   P_qbb[:, rc:rc + 1], WQR[:, rc, j * 512:(j + 1) * 512],
                                   start=(rc == 0), stop=(rc == 11))
            pe.inc(ins)

        @block.scalar
        def _(s):
            pe.wait(s)
            ac.inc(s.activation(BROWr[:], BPr[0:1, :], AF.Identity, scale=SCALE))

        @block.tensor
        def _(t):
            ac.wait(t)
            for j in range(2):
                ins = t.matmul(BPr[:, j * 512:(j + 1) * 512], P_ones[:],
                               BROWr[:, j * 512:(j + 1) * 512], start=True, stop=True)
            pe.inc(ins)

        @block.vector
        def _(v):
            pe.wait(v)
            dv.inc(v.tensor_copy(P_qrb[:], BPr[:]))
        qrb_done = dv.n

        @block.gpsimd
        def _(g):
            dv.wait(g, qrb_done)
            for rc in range(12):
                ins = g.tensor_scalar_mul(WQR[:, rc], WQR[:, rc], ST1[:, rc:rc + 1])
            gp.inc(ins)
        fold_qr = gp.n

        qr_evac, qre_st = {}, {}
        for tt in range(NTQ):
            @block.tensor
            def _(t, tt=tt):
                gp.wait(t, fold_qr)
                if tt > 0:
                    dv.wait(t, qr_evac[tt - 1])
                for rc in range(12):
                    for j in range(2):
                        ins = t.matmul(QP[:, j * 512:(j + 1) * 512],
                                       cq_fm[:, rc, tt * 128:(tt + 1) * 128],
                                       WQR[:, rc, j * 512:(j + 1) * 512],
                                       start=(rc == 0), stop=(rc == 11))
                pe.inc(ins)

            @block.vector
            def _(v, tt=tt):
                pe.wait(v)
                v.tensor_tensor(QRT[:], QP[:], P_qrb[:], ALU.add)
                v.drain()
                q4 = QRT[:].rearrange("p (h d f) -> p h d f", h=16, d=2)
                t4 = RTMP[:].rearrange("p (h d f) -> p h d f", h=16, d=2)
                o4 = QRB[:].rearrange("p (h d f) -> p h d f", h=16, d=2)
                cosb = P_cos[:, tt, None, None, :].broadcast_to((128, 16, 2, 32))
                sinh = P_sin[:, tt, None, :].broadcast_to((128, 16, 32))
                r2 = RT2[:].rearrange("p (h f) -> p h f", h=16)
                v.tensor_tensor(t4, q4, cosb, ALU.mult)
                v.scalar_tensor_tensor(r2, q4[:, :, 1, :], -1.0, sinh, ALU.mult, ALU.mult)
                v.drain()
                v.tensor_tensor(o4[:, :, 0, :], t4[:, :, 0, :], r2, ALU.add)
                v.drain()
                v.tensor_tensor(r2, q4[:, :, 0, :], sinh, ALU.mult)
                v.drain()
                v.tensor_tensor(o4[:, :, 1, :], t4[:, :, 1, :], r2, ALU.add)
                dv.inc(v.drain())

            @block.tensor
            def _(t, tt=tt):
                dv.wait(t)
                if tt >= 1:
                    dv.wait(t, qr_evac[tt - 1])
                for j in range(8):
                    ins = t.transpose(TR4[:, j * 128:(j + 1) * 128],
                                      QRB[:, j * 128:(j + 1) * 128], P_ident[:])
                pe.inc(ins)

            @block.vector
            def _(v, tt=tt):
                pe.wait(v)
                if tt > 0:
                    sy.wait(v, qre_st[tt - 1])
                dv.inc(v.tensor_copy(QRE[:], TR4[:]))
                qr_evac[tt] = dv.n

            @block.sync
            def _(sync, tt=tt):
                dv.wait(sync, qr_evac[tt])
                sy.dma(sync.dma_start(out=qr_hbm[:, :, tt * 128:(tt + 1) * 128].rearrange("a p c -> p a c"),
                                      in_=QRE[:].rearrange("p (a c) -> p a c", c=128)))
                qre_st[tt] = sy.n
        barrier()

    stack_b.close()

    if debug:
        with nc.sbuf_tensor("DBG3", [128, 2048], F32) as DBG3:
            for a in range(4):
                @block.gpsimd
                def _(g, a=a):
                    gd.dma(g.dma_start(out=DBG3[:].rearrange("p (a c) -> p a c", c=OWN), in_=qr_hbm[2 * a:2 * a + 2].rearrange("a p c -> p a c")))

                @block.sync
                def _(sync, a=a):
                    gd.wait(sync)
                    sy.dma(sync.dma_start(out=taps["t_qr"][:, a * 2048:(a + 1) * 2048], in_=DBG3[:]))
                barrier()

    with nc.sbuf_tensor("WKD", [128, 16, KVR], BF16) as WKD, \
         nc.sbuf_tensor("SQk", [128, 2048], F32) as SQ, \
         nc.sbuf_tensor("XT0b", [128, D], F32) as XT0b, \
         nc.sbuf_tensor("XT1b", [128, D], F32) as XT1b, \
         nc.sbuf_tensor("XNB0b", [128, D], BF16) as XNB0b, \
         nc.sbuf_tensor("XNB1b", [128, D], BF16) as XNB1b, \
         nc.sbuf_tensor("XFM0", [128, 16, 128], BF16) as XFM0, \
         nc.sbuf_tensor("XFM1", [128, 16, 128], BF16) as XFM1, \
         nc.sbuf_tensor("CKVT", [128, KVR], BF16) as CKVT, \
         nc.psum_tensor("TRPb", [128, 2048], BF16) as TRPb, \
         nc.psum_tensor("CKV_ps", [128, KVR], F32) as CKV_ps, \
         nc.psum_tensor("TR2", [128, KVR], BF16) as TR2:

        @block.gpsimd
        def _(g):
            for a in range(4):
                gd.dma(g.dma_start(out=WKD[:, 4 * a:4 * (a + 1)],
                                   in_=wkd[a * 512:(a + 1) * 512, :].rearrange("(a p) c -> p a c", p=128)))
        ldw2 = gd.n

        XTb, XNBb, XFM = [XT0b, XT1b], [XNB0b, XNB1b], [XFM0, XFM1]
        xb_done, xfm_done, ckvt_done, kv_evac = {}, {}, {}, {}
        for t in range(NT):
            bi = t % 2
            if t >= NTQ:
                @block.sync
                def _(sync, t=t, bi=bi):
                    if t >= NTQ + 2:
                        dv.wait(sync, xb_done[t - 2])
                    sy.dma(sync.dma_start(out=XTb[bi][:], in_=xb[t * 128:(t + 1) * 128, :]))
                ld = sy.n

                @block.vector
                def _(v, ld=ld):
                    sy.wait(v, ld)
                emit_ln(XTb[bi][:], D, SQ[:], ST1[:, 8 * bi:8 * bi + 8], XNBb[bi][:])
                xb_done[t] = dv.n

                @block.tensor
                def _(t_, t=t, bi=bi):
                    dv.wait(t_, xb_done[t])
                    if t >= NTQ + 2:
                        dv.wait(t_, xfm_done[t - 2])
                    for fc in range(16):
                        ins = t_.transpose(TRPb[:, fc * 128:(fc + 1) * 128],
                                           XNBb[bi][:, fc * 128:(fc + 1) * 128], P_ident[:])
                    pe.inc(ins)

                @block.vector
                def _(v, t=t, bi=bi):
                    pe.wait(v)
                    dv.inc(v.tensor_copy(XFM[bi][:], TRPb[:].rearrange("p (a c) -> p a c", c=128)))
                    xfm_done[t] = dv.n
                xsrc = XFM[bi]
            else:
                xsrc = None

            @block.tensor
            def _(t_, t=t, xsrc=xsrc):
                if t == 0:
                    gd.wait(t_, ldw2)
                else:
                    dv.wait(t_, ckvt_done[t - 1])
                if xsrc is not None:
                    dv.wait(t_, xfm_done[t])
                for dc in range(16):
                    src = xsrc[:, dc] if xsrc is not None else xn_fm[:, dc, t * 128:(t + 1) * 128]
                    ins = t_.matmul(CKV_ps[:], src, WKD[:, dc], start=(dc == 0), stop=(dc == 15))
                pe.inc(ins)

            @block.vector
            def _(v):
                pe.wait(v)
            emit_ln(CKV_ps[:], KVR, SQ[:, 0:KVR], ST2, CKVT[:])
            ckvt_done[t] = dv.n

            @block.tensor
            def _(t_, t=t):
                dv.wait(t_, ckvt_done[t])
                if t >= 1:
                    dv.wait(t_, kv_evac[t - 1])
                for rc in range(4):
                    ins = t_.transpose(TR2[:, rc * 128:(rc + 1) * 128],
                                       CKVT[:, rc * 128:(rc + 1) * 128], P_ident[:])
                pe.inc(ins)

            @block.vector
            def _(v, t=t):
                pe.wait(v)
                dv.inc(v.tensor_copy(ckv_fm[:, :, t * 128:(t + 1) * 128],
                                     TR2[:].rearrange("p (a c) -> p a c", c=128)))
                kv_evac[t] = dv.n
        barrier()

    stack_a.close()

    if debug:
        with nc.sbuf_tensor("DBG2", [128, 2048], F32) as DBG2:
            dump_tap(lambda a: ckv_fm[:].rearrange("p a c -> p (a c)")[:, a * 2048:(a + 1) * 2048],
                     taps["t_ckv"], 8, 2048, DBG2)

    o_fm = E(nc.sbuf_tensor("o_fm", [128, H, OWN], BF16))
    stack_c = contextlib.ExitStack()
    Wku = stack_c.enter_context(nc.sbuf_tensor("Wku", [128, 4, D], BF16))
    Wvu = stack_c.enter_context(nc.sbuf_tensor("Wvu", [128, 4, D], BF16))
    with nc.sbuf_tensor("WKR", [128, 4, H * DR], BF16) as WKR, \
         nc.sbuf_tensor("BROW2", [1, D], F32) as BROW2, \
         nc.sbuf_tensor("B16b", [16, 128], F32) as B16b, \
         nc.sbuf_tensor("P_krb", [128, H * DR], F32) as P_krb, \
         nc.sbuf_tensor("KRT", [128, 1024], F32) as KRT, \
         nc.sbuf_tensor("RTMPb", [128, 1024], F32) as RTMPb, \
         nc.sbuf_tensor("RT2b", [128, 512], F32) as RT2b, \
         nc.sbuf_tensor("KRB", [128, 1024], BF16) as KRB, \
         nc.psum_tensor("KP", [128, 1024], F32) as KP, \
         nc.psum_tensor("BPb", [128, 1024], F32) as BPb:

        @block.gpsimd
        def _(g):
            gd.dma(g.dma_start(out=Wku[:], in_=wku[:, :].rearrange("(a p) c -> p a c", p=128)))
            gd.dma(g.dma_start(out=Wvu[:], in_=wvu[:, :].rearrange("(a p) c -> p a c", p=128)))
            gd.dma(g.dma_start(out=WKR[:], in_=wkr[:, :].rearrange("(a p) c -> p a c", p=128)))
        ldw4 = gd.n

        vb_done = {}
        for half in range(2):
            @block.tensor
            def _(t, half=half):
                gd.wait(t, ldw4)
                if half == 1:
                    ac.wait(t, vb_done[0])
                for rc in range(4):
                    for j in range(2):
                        ins = t.matmul(BPb[0:1, j * 512:(j + 1) * 512],
                                       P_kbb[:, rc:rc + 1],
                                       Wvu[:, rc, (half * 2 + j) * 512:(half * 2 + j + 1) * 512],
                                       start=(rc == 0), stop=(rc == 3))
                pe.inc(ins)

            @block.scalar
            def _(s, half=half):
                pe.wait(s)
                ac.inc(s.copy(BROW2[:, half * 1024:(half + 1) * 1024], BPb[0:1, :]))
                vb_done[half] = ac.n

        @block.sync
        def _(sync):
            ac.wait(sync, vb_done[1])
            sy.dma(sync.dma_start(out=bb[1:2, :], in_=BROW2[:]))
            sy.wait(sync)
            sy.dma(sync.dma_start(out=B16b[:], in_=bb[1, :].rearrange("(a c) -> a c", c=128)))
        ldb2 = sy.n

        @block.tensor
        def _(t):
            sy.wait(t, ldb2)
            pe.inc(t.transpose(BPb[:, 0:16], B16b[:], P_identf[0:16, 0:16]))

        @block.vector
        def _(v):
            pe.wait(v)
            dv.inc(v.tensor_copy(P_vb[:], BPb[:, 0:16]))
        pvb_done = dv.n

        @block.tensor
        def _(t):
            dv.wait(t, pvb_done)
            for rc in range(4):
                for j in range(2):
                    ins = t.matmul(BPb[0:1, j * 512:(j + 1) * 512],
                                   P_kbb[:, rc:rc + 1], WKR[:, rc, j * 512:(j + 1) * 512],
                                   start=(rc == 0), stop=(rc == 3))
            pe.inc(ins)

        @block.scalar
        def _(s):
            pe.wait(s)
            ac.inc(s.copy(BROW2[:, 0:1024], BPb[0:1, :]))

        @block.tensor
        def _(t):
            ac.wait(t)
            for j in range(2):
                ins = t.matmul(BPb[:, j * 512:(j + 1) * 512], P_ones[:],
                               BROW2[:, j * 512:(j + 1) * 512], start=True, stop=True)
            pe.inc(ins)

        @block.vector
        def _(v):
            pe.wait(v)
            dv.inc(v.tensor_copy(P_krb[:], BPb[:]))
        krb_done = dv.n

        @block.gpsimd
        def _(g):
            dv.wait(g, krb_done)
            for rc in range(4):
                g.tensor_scalar_mul(Wku[:, rc], Wku[:, rc], P_kg[:, rc:rc + 1])
            for rc in range(4):
                g.tensor_scalar_mul(Wvu[:, rc], Wvu[:, rc], P_kg[:, rc:rc + 1])
            for rc in range(4):
                ins = g.tensor_scalar_mul(WKR[:, rc], WKR[:, rc], P_kg[:, rc:rc + 1])
            gp.inc(ins)
        fold_k = gp.n

        kr_done, kr_st = {}, {}
        for tt in range(NT):
            @block.tensor
            def _(t, tt=tt):
                gp.wait(t, fold_k)
                if tt > 0:
                    dv.wait(t, kr_done[tt - 1])
                for rc in range(4):
                    for j in range(2):
                        ins = t.matmul(KP[:, j * 512:(j + 1) * 512],
                                       ckv_fm[:, rc, tt * 128:(tt + 1) * 128],
                                       WKR[:, rc, j * 512:(j + 1) * 512],
                                       start=(rc == 0), stop=(rc == 3))
                pe.inc(ins)

            @block.vector
            def _(v, tt=tt):
                pe.wait(v)
                if tt > 0:
                    sy.wait(v, kr_st[tt - 1])
                v.tensor_tensor(KRT[:], KP[:], P_krb[:], ALU.add)
                v.drain()
                q4 = KRT[:].rearrange("p (h d f) -> p h d f", h=16, d=2)
                t4 = RTMPb[:].rearrange("p (h d f) -> p h d f", h=16, d=2)
                o4 = KRB[:].rearrange("p (h d f) -> p h d f", h=16, d=2)
                cosb = P_cos[:, tt, None, None, :].broadcast_to((128, 16, 2, 32))
                sinh = P_sin[:, tt, None, :].broadcast_to((128, 16, 32))
                r2 = RT2b[:].rearrange("p (h f) -> p h f", h=16)
                v.tensor_tensor(t4, q4, cosb, ALU.mult)
                v.scalar_tensor_tensor(r2, q4[:, :, 1, :], -1.0, sinh, ALU.mult, ALU.mult)
                v.drain()
                v.tensor_tensor(o4[:, :, 0, :], t4[:, :, 0, :], r2, ALU.add)
                v.drain()
                v.tensor_tensor(r2, q4[:, :, 0, :], sinh, ALU.mult)
                v.drain()
                v.tensor_tensor(o4[:, :, 1, :], t4[:, :, 1, :], r2, ALU.add)
                dv.inc(v.drain())
                kr_done[tt] = dv.n

            @block.sync
            def _(sync, tt=tt):
                dv.wait(sync, kr_done[tt])
                sy.dma(sync.dma_start(out=kr_tm[tt * 128:(tt + 1) * 128, :], in_=KRB[:]))
                kr_st[tt] = sy.n
        barrier()

    with contextlib.ExitStack() as _attn_stack:
        _AE = _attn_stack.enter_context
        KRF0 = _AE(nc.sbuf_tensor("KRF0", [128, S], BF16))
        KRF1 = _AE(nc.sbuf_tensor("KRF1", [128, S], BF16))
        QRP0 = _AE(nc.sbuf_tensor("QRP0", [128, OWN], BF16))
        QRP1 = _AE(nc.sbuf_tensor("QRP1", [128, OWN], BF16))
        QCL0 = _AE(nc.sbuf_tensor("QCL0", [128, OWN], BF16))
        QCL1 = _AE(nc.sbuf_tensor("QCL1", [128, OWN], BF16))
        KC = _AE(nc.sbuf_tensor("KC", [128, S], BF16))
        VSB = _AE(nc.sbuf_tensor("VSB", [128, 32, 128], BF16))
        PA0 = _AE(nc.sbuf_tensor("PA0", [128, 2048], BF16))
        PA1 = _AE(nc.sbuf_tensor("PA1", [128, 2048], BF16))
        PB0 = _AE(nc.sbuf_tensor("PB0", [128, 2048], BF16))
        PB1 = _AE(nc.sbuf_tensor("PB1", [128, 2048], BF16))
        PTS = _AE(nc.sbuf_tensor("PTS", [128, 2048], BF16))
        SSTT = _AE(nc.sbuf_tensor("SSTT", [128, 48], F32))
        OT1 = _AE(nc.sbuf_tensor("OT1", [128, 128], F32))
        OT2 = _AE(nc.sbuf_tensor("OT2", [128, 128], BF16))
        SC_ps = _AE(nc.psum_tensor("SC_ps", [128, 2048], F32))
        PT_ps = _AE(nc.psum_tensor("PT_ps", [128, 2048], BF16))
        O_ps = _AE(nc.psum_tensor("O_ps", [128, 256], F32))
        KB_ps = _AE(nc.psum_tensor("KB_ps", [128, 512], F32))
        KRF = [KRF0, KRF1]
        QRP = [QRP0, QRP1]
        QCL = [QCL0, QCL1]
        krf_used, qcl_used = {}, {}
        kc_evac, v_evac = {}, {}
        exp_done, pts_evac, pv_done, ocomb, ofm_evac, pts_pe = {}, {}, {}, {}, {}, {}
        stats_done = {0: 0, 1: 0}
        for h in range(H):
            hb = h % 2
            pj = h // 2
            if hb == 0:
                @block.sync
                def _(sync, pj=pj):
                    if pj >= 2:
                        pe.wait(sync, krf_used[pj - 2])
                    sy.dma(sync.dma_start_transpose(KRF[pj % 2][:],
                                                    kr_tm[:, pj * 128:(pj + 1) * 128]))
                    sy.dma(sync.dma_start(out=QRP[pj % 2][:], in_=qr_hbm[pj]))

            @block.sync
            def _(sync, h=h, hb=hb):
                if h >= 2:
                    pe.wait(sync, qcl_used[h - 2])
                sy.dma(sync.dma_start(out=QCL[hb][:], in_=qc_hbm[h]))
            ldq = sy.n

            for kb2 in range(2):
                @block.tensor
                def _(t, h=h, kb2=kb2):
                    if kb2 == 0:
                        if h > 0:
                            ac.wait(t, exp_done[(h - 1, NTQ - 1, 1)])
                    else:
                        dv.wait(t, kc_evac[(h, 0)])
                    for nt4 in range(4):
                        nt = 4 * kb2 + nt4
                        for rc in range(4):
                            ins = t.matmul(SC_ps[:, nt4 * 512:(nt4 + 1) * 512],
                                           Wku[:, rc, h * 128:(h + 1) * 128],
                                           ckv_fm[:, rc, nt * 512:(nt + 1) * 512],
                                           start=(rc == 0), stop=(rc == 3))
                    pe.inc(ins)

                @block.vector
                def _(v, h=h, kb2=kb2):
                    pe.wait(v)
                    dv.inc(v.tensor_copy(KC[:, kb2 * 2048:(kb2 + 1) * 2048], SC_ps[:]))
                    kc_evac[(h, kb2)] = dv.n

            for g8 in range(8):
                VB = KB_ps[:] if g8 % 2 == 0 else SC_ps[:, 0:512]
                @block.tensor
                def _(t, h=h, g8=g8, VB=VB):
                    if g8 == 1:
                        dv.wait(t, kc_evac[(h, 1)])
                    if g8 >= 2:
                        dv.wait(t, v_evac[(h, g8 - 2)])
                    if g8 == 0 and h > 0:
                        dv.wait(t, v_evac[(h - 1, 7)])
                    for kt in range(4):
                        for rc in range(4):
                            ins = t.matmul(VB[:, kt * 128:(kt + 1) * 128],
                                           ckv_fm[:, rc, (4 * g8 + kt) * 128:(4 * g8 + kt + 1) * 128],
                                           Wvu[:, rc, h * 128:(h + 1) * 128],
                                           start=(rc == 0), stop=(rc == 3))
                    pe.inc(ins)

                @block.vector
                def _(v, h=h, g8=g8, VB=VB):
                    pe.wait(v)
                    dv.inc(v.tensor_copy(VSB[:, 4 * g8:4 * (g8 + 1)],
                                         VB.rearrange("p (a c) -> p a c", c=128)))
                    v_evac[(h, g8)] = dv.n

            PBUF = [[PA0, PA1], [PB0, PB1]]

            def emit_scores_half(qt, half, h=h, hb=hb, pj=pj, ldq=ldq):
                par = qt % 2
                SST = SSTT[:, 24 * par:24 * par + 24]
                chunk_pe = {}

                @block.tensor
                def _(t, qt=qt, half=half):
                    if qt == 0 and half == 0:
                        sy.wait(t, ldq)
                        dv.wait(t, v_evac[(h, 7)])
                    prev = exp_done.get((h, qt, half - 1),
                                        exp_done.get((h, qt - 1, 1),
                                                     exp_done.get((h - 1, NTQ - 1, 1), 0)))
                    ac.wait(t, prev)
                    for kc in range(4):
                        t.matmul(SC_ps[:, kc * 512:(kc + 1) * 512],
                                 QCL[hb][:, qt * 128:(qt + 1) * 128],
                                 KC[:, (half * 4 + kc) * 512:(half * 4 + kc + 1) * 512],
                                 start=True, stop=False)
                    qr_l = QRP[pj % 2][0:64, qt * 128:(qt + 1) * 128] if hb == 0 else \
                        QRP[pj % 2][64:128, qt * 128:(qt + 1) * 128]
                    for kc in range(4):
                        kr_r = KRF[pj % 2][0:64, (half * 4 + kc) * 512:(half * 4 + kc + 1) * 512] \
                            if hb == 0 else \
                            KRF[pj % 2][64:128, (half * 4 + kc) * 512:(half * 4 + kc + 1) * 512]
                        ins = t.matmul(SC_ps[:, kc * 512:(kc + 1) * 512], qr_l, kr_r,
                                       start=False, stop=True)
                        pe.inc(ins)
                        chunk_pe[kc] = pe.n
                    if qt == NTQ - 1 and half == 1:
                        qcl_used[h] = pe.n
                        if hb == 1:
                            krf_used[pj] = pe.n

                @block.vector
                def _(v, qt=qt, half=half, SST=SST, chunk_pe=chunk_pe):
                    pe.wait(v, chunk_pe[3])
                    v.reduce_max(SST[:, 8 + half:9 + half], SC_ps[:], axis=AX, negate=True)
                    dv.inc(v.drain())

                @block.scalar
                def _(s, qt=qt, half=half, SST=SST, par=par):
                    dv.wait(s)
                    PH = PBUF[half][par]
                    pe.wait(s, pts_pe.get((half, par), 0))
                    ac.inc(s.activation(PH[:], SC_ps[:], AF.Exp,
                                        bias=SST[:, 8 + half:9 + half],
                                        accum_out=SST[:, 20 + half:21 + half]))
                    exp_done[(h, qt, half)] = ac.n

            def emit_stats(qt, h=h):
                par = qt % 2
                SST = SSTT[:, 24 * par:24 * par + 24]

                @block.vector
                def _(v, qt=qt, SST=SST):
                    ac.wait(v, exp_done[(h, qt, 1)])
                    v.tensor_tensor(SST[:, 10:11], SST[:, 8:9], SST[:, 9:10], ALU.min)
                    v.drain()
                    v.tensor_sub(SST[:, 11:12], SST[:, 10:11], SST[:, 8:9])
                    v.tensor_sub(SST[:, 12:13], SST[:, 10:11], SST[:, 9:10])
                    dv.inc(v.drain())

                @block.scalar
                def _(s, SST=SST):
                    dv.wait(s)
                    ac.inc(s.activation(SST[:, 13:15], SST[:, 11:13], AF.Exp))

                @block.vector
                def _(v, qt=qt, SST=SST):
                    ac.wait(v)
                    v.tensor_tensor(SST[:, 15:16], SST[:, 13:14], SST[:, 20:21], ALU.mult)
                    v.drain()
                    v.scalar_tensor_tensor(SST[:, 16:17], SST[:, 14:15], SST[:, 21:22],
                                           SST[:, 15:16], ALU.mult, ALU.add)
                    v.drain()
                    v.reciprocal(SST[:, 17:18], SST[:, 16:17])
                    v.drain()
                    v.tensor_tensor(SST[:, 18:20], SST[:, 13:15],
                                    SST[:, 17:18].broadcast_to((128, 2)), ALU.mult)
                    dv.inc(v.drain())
                stats_done[par] = dv.n

            def emit_pv_half(qt, half, h=h):
                par = qt % 2
                PH = PBUF[half][par]

                @block.tensor
                def _(t, qt=qt, half=half, PH=PH, par=par):
                    ac.wait(t, exp_done[(h, qt, half)])
                    if "last" in pts_evac:
                        s_, n_ = pts_evac["last"]
                        s_.wait(t, n_)
                    dv.wait(t, ofm_evac.get("last", 0))
                    if half == 0:
                        dv.wait(t, ocomb.get("last", 0))
                    for i in range(16):
                        ins = t.transpose(PT_ps[:, i * 128:(i + 1) * 128],
                                          PH[:, i * 128:(i + 1) * 128], P_ident[:])
                    pe.inc(ins)
                    pts_pe[(half, par)] = pe.n

                @block.vector
                def _(v):
                    pe.wait(v)
                    dv.inc(v.tensor_copy(PTS[:], PT_ps[:]))
                    pts_evac["last"] = (dv, dv.n)

                @block.tensor
                def _(t, qt=qt, half=half):
                    s_, n_ = pts_evac["last"]
                    s_.wait(t, n_)
                    for kc in range(16):
                        ins = t.matmul(O_ps[:, half * 128:(half + 1) * 128],
                                       PTS[:, kc * 128:(kc + 1) * 128],
                                       VSB[:, half * 16 + kc],
                                       start=(kc == 0), stop=(kc == 15))
                    pe.inc(ins)
                    pv_done[half] = pe.n

            def emit_ocombine(qt, h=h):
                par = qt % 2
                SST = SSTT[:, 24 * par:24 * par + 24]

                @block.vector
                def _(v, qt=qt, SST=SST, par=par):
                    pe.wait(v, pv_done[1])
                    dv.wait(v, stats_done[par])
                    v.tensor_scalar_mul(OT1[:], O_ps[:, 0:128], SST[:, 18:19])
                    v.drain()
                    v.scalar_tensor_tensor(OT2[:], O_ps[:, 128:256], SST[:, 19:20],
                                           OT1[:], ALU.mult, ALU.add)
                    dv.inc(v.drain())
                    ocomb["last"] = dv.n

                @block.tensor
                def _(t, qt=qt):
                    dv.wait(t, ocomb["last"])
                    dv.wait(t, ofm_evac.get("last", 0))
                    pe.inc(t.transpose(PT_ps[:, 0:128], OT2[:], P_ident[:]))

                @block.vector
                def _(v, qt=qt):
                    pe.wait(v)
                    dv.inc(v.tensor_scalar_add(o_fm[:, h, qt * 128:(qt + 1) * 128],
                                               PT_ps[:, 0:128], P_vb[:, h:h + 1]))
                    ofm_evac["last"] = dv.n

            emit_scores_half(0, 0)
            emit_scores_half(0, 1)
            emit_stats(0)
            for qt in range(1, NTQ):
                emit_scores_half(qt, 0)
                emit_pv_half(qt - 1, 0)
                emit_scores_half(qt, 1)
                emit_pv_half(qt - 1, 1)
                emit_stats(qt)
                emit_ocombine(qt - 1)
            emit_pv_half(NTQ - 1, 0)
            emit_pv_half(NTQ - 1, 1)
            emit_ocombine(NTQ - 1)
        barrier()

    stack_c.close()

    if debug:
        with nc.sbuf_tensor("DBG4", [128, 2048], F32) as DBG4:
            dump_tap(lambda a: o_fm[:].rearrange("p a c -> p (a c)")[:, a * 2048:(a + 1) * 2048],
                     taps["t_ofm"], 8, 2048, DBG4)

    with nc.sbuf_tensor("WOL", [128, 16, D], BF16) as WOL, \
         nc.sbuf_tensor("XRES", [128, D], F32) as XRES, \
         nc.sbuf_tensor("FRES", [128, D], F32) as FRES, \
         nc.sbuf_tensor("S1", [128, D], F32) as S1, \
         nc.sbuf_tensor("OUTB", [128, D], F32) as OUTB, \
         nc.psum_tensor("Y2", [128, 2048], F32) as Y2:

        @block.gpsimd
        def _(g):
            for a in range(4):
                gd.dma(g.dma_start(out=WOL[:, 4 * a:4 * (a + 1)],
                                   in_=wo[a * 512:(a + 1) * 512, :].rearrange("(a p) c -> p a c", p=128)))
        ldwo = gd.n

        res_used = {}
        for qt in range(NTQ):
            @block.sync
            def _(sync, qt=qt):
                if qt > 0:
                    dv.wait(sync, res_used[qt - 1])
                sy.dma(sync.dma_start(out=XRES[:], in_=xb[qt * 128:(qt + 1) * 128, :]))
                sy.dma(sync.dma_start(out=FRES[:], in_=ffn_y[qt * 128:(qt + 1) * 128, :]))
            ldr = sy.n

            @block.tensor
            def _(t, qt=qt):
                if qt == 0:
                    gd.wait(t, ldwo)
                else:
                    dv.wait(t, res_used[qt - 1])
                for fc in range(H):
                    for jj in range(4):
                        ins = t.matmul(Y2[:, jj * 512:(jj + 1) * 512],
                                       o_fm[:, fc, qt * 128:(qt + 1) * 128],
                                       WOL[:, fc, jj * 512:(jj + 1) * 512],
                                       start=(fc == 0), stop=(fc == H - 1))
                pe.inc(ins)

            @block.vector
            def _(v, qt=qt, ldr=ldr):
                sy.wait(v, ldr)
                v.tensor_tensor(S1[:], XRES[:], FRES[:], ALU.add)
                pe.wait(v)
                v.drain()
                v.tensor_tensor(OUTB[:], S1[:], Y2[:], ALU.add)
                dv.inc(v.drain())
                res_used[qt] = dv.n

            @block.sync
            def _(sync, qt=qt):
                dv.wait(sync, res_used[qt])
                sy.dma(sync.dma_start(out=y[qt * 128:(qt + 1) * 128, :], in_=OUTB[:]))

        @block.sync
        def _(sync):
            sy.wait(sync)

    stx.close()
    return nc, taps


def _prep_inputs(inputs):
    x = np.asarray(inputs["L0_attn_x"])
    f32 = np.float32
    wg_u8 = np.asarray(inputs["L0_ffn_wq_gate"]).view(np.uint8).reshape(HID, D, 4)[:, :, 0]
    wu_u8 = np.asarray(inputs["L0_ffn_wq_up"]).view(np.uint8).reshape(HID, D, 4)[:, :, 0]
    wd_u8 = np.asarray(inputs["L0_ffn_wq_down"]).view(np.uint8).reshape(D, HID, 4)[:, :, 0]
    shared = {
        "wqd": np.ascontiguousarray(inputs["L0_attn_Wq_down"], f32),
        "wqu": np.ascontiguousarray(inputs["L0_attn_Wq_up"], f32),
        "wqr": np.ascontiguousarray(inputs["L0_attn_Wq_rope"], f32),
        "wkd": np.ascontiguousarray(inputs["L0_attn_Wkv_down"], f32),
        "wku": np.ascontiguousarray(inputs["L0_attn_Wk_up"], f32),
        "wvu": np.ascontiguousarray(inputs["L0_attn_Wv_up"], f32),
        "wkr": np.ascontiguousarray(inputs["L0_attn_Wk_rope"], f32),
        "wo": np.ascontiguousarray(inputs["L0_attn_Wout"], f32),
        "qg2": np.ascontiguousarray(np.asarray(inputs["L0_attn_q_gamma"], f32).reshape(12, 128).T),
        "qb2": np.ascontiguousarray(np.asarray(inputs["L0_attn_q_beta"], f32).reshape(12, 128).T),
        "kg2": np.ascontiguousarray(np.asarray(inputs["L0_attn_kv_gamma"], f32).reshape(4, 128).T),
        "kb2": np.ascontiguousarray(np.asarray(inputs["L0_attn_kv_beta"], f32).reshape(4, 128).T),
        "fg": np.ascontiguousarray(wg_u8.reshape(NHC, 128, 16, 128).transpose(0, 3, 2, 1)),
        "fu": np.ascontiguousarray(wu_u8.reshape(NHC, 128, 16, 128).transpose(0, 3, 2, 1)),
        "fd": np.ascontiguousarray(wd_u8.reshape(D, NHC, 128).transpose(1, 2, 0)),
        "sg2": np.ascontiguousarray(np.asarray(inputs["L0_ffn_s_gate"], f32).reshape(NHC, 128).T),
        "su2": np.ascontiguousarray(np.asarray(inputs["L0_ffn_s_up"], f32).reshape(NHC, 128).T),
        "sdr": np.ascontiguousarray(np.asarray(inputs["L0_ffn_s_down"], f32).reshape(1, D)),
    }
    per_core = []
    for c in range(8):
        b, qs = c // 4, c % 4
        perm = np.concatenate([np.arange(qs * OWN, (qs + 1) * OWN),
                               np.arange(0, qs * OWN),
                               np.arange((qs + 1) * OWN, S)])
        m = dict(shared)
        m["xb"] = np.ascontiguousarray(x[b][perm], f32)
        m["posP"] = np.ascontiguousarray(perm.astype(f32).reshape(NT, 128).T)
        per_core.append(m)
    return per_core


def kernel(**inputs):
    per_core = _prep_inputs(inputs)
    nc, _ = build_program(debug=DEBUG)
    res = run_bass_kernel_spmd(nc, per_core, core_ids=list(range(8)))
    out = np.empty((B, S, D), np.float32)
    for c in range(8):
        b, qs = c // 4, c % 4
        out[b, qs * OWN:(qs + 1) * OWN] = res.results[c]["y"]
    return out


# version 06
# speedup vs baseline: 1.6470x; 1.0816x over previous
import contextlib
import numpy as np
import concourse.bass as bass
import concourse.mybir as mybir
from concourse.bass_utils import run_bass_kernel_spmd

F32 = mybir.dt.float32
BF16 = mybir.dt.bfloat16
U8 = mybir.dt.uint8
I32 = mybir.dt.int32
AF = mybir.ActivationFunctionType
ALU = mybir.AluOpType
AX = mybir.AxisListType.X

B, S, D = 2, 4096, 2048
H, DH, DR = 16, 128, 64
QR, KVR = 1536, 512
HID, NHC = 5632, 44
NT, NTQ = 32, 8
OWN = 1024
SCALE = float((DH + DR) ** -0.5)
EPS = 1e-5
TWO_PI = 2 * np.pi
CW0 = float(np.float32(6.28125))
CW1 = float(np.float32(TWO_PI - 6.28125))
CW2 = float(TWO_PI - np.float64(np.float32(6.28125)) - np.float64(np.float32(TWO_PI - 6.28125)))
MAGIC = float(np.float32(1.5 * 2 ** 23))

DEBUG = False


class Sem:
    __slots__ = ("h", "n")

    def __init__(self, h):
        self.h = h
        self.n = 0

    def inc(self, instr, k=1):
        instr.then_inc(self.h, k)
        self.n += k
        return self.n

    def dma(self, instr):
        return self.inc(instr, 16)

    def wait(self, eng, n=None):
        eng.wait_ge(self.h, self.n if n is None else n)


def build_program(debug=False):
    nc = bass.Bass("TRN2", target_bir_lowering=False)

    xb = nc.dram_tensor("xb", [S, D], F32, kind="ExternalInput").ap()
    wqd = nc.dram_tensor("wqd", [D, QR], F32, kind="ExternalInput").ap()
    wqu = nc.dram_tensor("wqu", [QR, D], F32, kind="ExternalInput").ap()
    wqr = nc.dram_tensor("wqr", [QR, H * DR], F32, kind="ExternalInput").ap()
    wkd = nc.dram_tensor("wkd", [D, KVR], F32, kind="ExternalInput").ap()
    wku = nc.dram_tensor("wku", [KVR, D], F32, kind="ExternalInput").ap()
    wvu = nc.dram_tensor("wvu", [KVR, D], F32, kind="ExternalInput").ap()
    wkr = nc.dram_tensor("wkr", [KVR, H * DR], F32, kind="ExternalInput").ap()
    wo = nc.dram_tensor("wo", [D, D], F32, kind="ExternalInput").ap()
    qg2 = nc.dram_tensor("qg2", [128, 12], F32, kind="ExternalInput").ap()
    qb2 = nc.dram_tensor("qb2", [128, 12], F32, kind="ExternalInput").ap()
    kg2 = nc.dram_tensor("kg2", [128, 4], F32, kind="ExternalInput").ap()
    kb2 = nc.dram_tensor("kb2", [128, 4], F32, kind="ExternalInput").ap()
    posP = nc.dram_tensor("posP", [128, NT], F32, kind="ExternalInput").ap()
    fg = nc.dram_tensor("fg", [NHC, 128, 16, 128], U8, kind="ExternalInput").ap()
    fu = nc.dram_tensor("fu", [NHC, 128, 16, 128], U8, kind="ExternalInput").ap()
    fd = nc.dram_tensor("fd", [NHC, 128, D], U8, kind="ExternalInput").ap()
    sg2 = nc.dram_tensor("sg2", [128, NHC], F32, kind="ExternalInput").ap()
    su2 = nc.dram_tensor("su2", [128, NHC], F32, kind="ExternalInput").ap()
    sdr = nc.dram_tensor("sdr", [1, D], F32, kind="ExternalInput").ap()

    y = nc.dram_tensor("y", [OWN, D], F32, kind="ExternalOutput").ap()

    kr_tm = nc.dram_tensor("kr_tm", [S, H * DR], BF16, kind="Internal").ap()
    qc_hbm = nc.dram_tensor("qc_hbm", [H, 128, OWN], BF16, kind="Internal").ap()
    qr_hbm = nc.dram_tensor("qr_hbm", [8, 128, OWN], BF16, kind="Internal").ap()
    ffn_y = nc.dram_tensor("ffn_y", [OWN, D], F32, kind="Internal").ap()
    bb = nc.dram_tensor("bb", [4, D], F32, kind="Internal").ap()

    taps = {}
    if debug:
        taps["t_xn"] = nc.dram_tensor("t_xn", [128, 16 * OWN], F32, kind="ExternalOutput").ap()
        taps["t_ckv"] = nc.dram_tensor("t_ckv", [128, 4 * S], F32, kind="ExternalOutput").ap()
        taps["t_cq"] = nc.dram_tensor("t_cq", [128, 12 * OWN], F32, kind="ExternalOutput").ap()
        taps["t_qr"] = nc.dram_tensor("t_qr", [128, 8 * OWN], F32, kind="ExternalOutput").ap()
        taps["t_ffn"] = nc.dram_tensor("t_ffn", [OWN, D], F32, kind="ExternalOutput").ap()
        taps["t_ofm"] = nc.dram_tensor("t_ofm", [128, H * OWN], F32, kind="ExternalOutput").ap()
        taps["t_cos"] = nc.dram_tensor("t_cos", [128, NT * 32], F32, kind="ExternalOutput").ap()
        taps["t_ifr"] = nc.dram_tensor("t_ifr", [1, 32], F32, kind="ExternalOutput").ap()

    stx = contextlib.ExitStack()
    E = stx.enter_context

    sy = Sem(E(nc.semaphore("sy")))
    gd = Sem(E(nc.semaphore("gd")))
    pe = Sem(E(nc.semaphore("pe")))
    dv = Sem(E(nc.semaphore("dv")))
    ac = Sem(E(nc.semaphore("ac")))
    gp = Sem(E(nc.semaphore("gp")))
    sems = (sy, gd, pe, dv, ac, gp)

    block = E(nc.Block())

    def barrier():
        snap = [(s.h, s.n) for s in sems]
        for name in ("sync", "vector", "scalar", "tensor", "gpsimd"):
            def f(e, snap=snap):
                for h, n in snap:
                    e.wait_ge(h, n)
            getattr(block, name)(f)

    def dump_tap(src_ap_fn, dst_ap, nchunks, chunk_cols, scratch):
        for a in range(nchunks):
            @block.scalar
            def _(s, a=a):
                ac.inc(s.copy(scratch[:, 0:chunk_cols], src_ap_fn(a)))

            @block.sync
            def _(sync, a=a):
                ac.wait(sync)
                sy.dma(sync.dma_start(out=dst_ap[:, a * chunk_cols:(a + 1) * chunk_cols],
                                      in_=scratch[:, 0:chunk_cols]))
            barrier()

    P_ident = E(nc.sbuf_tensor("P_ident", [128, 128], BF16))
    P_identf = E(nc.sbuf_tensor("P_identf", [128, 128], F32))
    P_ones = E(nc.sbuf_tensor("P_ones", [1, 128], F32))
    P_cos = E(nc.sbuf_tensor("P_cos", [128, NT, 32], F32))
    P_sin = E(nc.sbuf_tensor("P_sin", [128, NT, 32], F32))
    P_qg = E(nc.sbuf_tensor("P_qg", [128, 12], F32))
    P_qb = E(nc.sbuf_tensor("P_qb", [128, 12], F32))
    P_kg = E(nc.sbuf_tensor("P_kg", [128, 4], F32))
    P_kb = E(nc.sbuf_tensor("P_kb", [128, 4], F32))
    P_qbb = E(nc.sbuf_tensor("P_qbb", [128, 12], BF16))
    P_kbb = E(nc.sbuf_tensor("P_kbb", [128, 4], BF16))
    P_sg = E(nc.sbuf_tensor("P_sg", [128, NHC], F32))
    P_su = E(nc.sbuf_tensor("P_su", [128, NHC], F32))
    P_qcb = E(nc.sbuf_tensor("P_qcb", [128, 16], F32))
    P_n128 = E(nc.sbuf_tensor("P_n128", [128, 1], F32))
    P_vb = E(nc.sbuf_tensor("P_vb", [128, 16], F32))

    ckv_fm = E(nc.sbuf_tensor("ckv_fm", [128, 4, S], BF16))
    ST1 = E(nc.sbuf_tensor("ST1", [128, 16], F32))
    ST2 = E(nc.sbuf_tensor("ST2", [128, 8], F32))
    stack_a = contextlib.ExitStack()
    xn_fm = stack_a.enter_context(nc.sbuf_tensor("xn_fm", [128, 16, OWN], BF16))

    @block.sync
    def _(sync):
        sy.dma(sync.dma_start(out=P_qg[:], in_=qg2[:, :]))
        sy.dma(sync.dma_start(out=P_qb[:], in_=qb2[:, :]))
        sy.dma(sync.dma_start(out=P_kg[:], in_=kg2[:, :]))
        sy.dma(sync.dma_start(out=P_kb[:], in_=kb2[:, :]))
        sy.dma(sync.dma_start(out=P_sg[:], in_=sg2[:, :]))
        sy.dma(sync.dma_start(out=P_su[:], in_=su2[:, :]))
    ld0 = sy.n

    with nc.sbuf_tensor("T_pos", [128, NT], F32) as T_pos, \
         nc.sbuf_tensor("T_ioti", [1, 32], I32) as T_ioti, \
         nc.sbuf_tensor("T_iot", [1, 32], F32) as T_iot, \
         nc.sbuf_tensor("T_g", [1, 32], F32) as T_g, \
         nc.sbuf_tensor("T_m", [1, 32], F32) as T_m, \
         nc.sbuf_tensor("T_f", [1, 32], F32) as T_f, \
         nc.sbuf_tensor("T_ifb", [128, 32], F32) as T_ifb, \
         nc.sbuf_tensor("T_th", [128, NT, 32], F32) as T_th, \
         nc.sbuf_tensor("T_k", [128, NT, 32], F32) as T_k, \
         nc.sbuf_tensor("T_w", [128, NT, 32], F32) as T_w, \
         nc.psum_tensor("T_ps", [128, 512], F32) as T_ps:

        @block.sync
        def _(sync):
            sy.dma(sync.dma_start(out=T_pos[:], in_=posP[:, :]))
        ld1 = sy.n

        @block.gpsimd
        def _(g):
            g.memset(P_ident[:], 0.0)
            g.affine_select(out=P_ident[:], in_=P_ident[:], compare_op=ALU.not_equal,
                            fill=1.0, base=0, pattern=[[-1, 128]], channel_multiplier=1)
            g.memset(P_identf[:], 0.0)
            g.affine_select(out=P_identf[:], in_=P_identf[:], compare_op=ALU.not_equal,
                            fill=1.0, base=0, pattern=[[-1, 128]], channel_multiplier=1)
            g.memset(P_ones[:], 1.0)
            g.memset(P_n128[:], -128.0)
            g.iota(T_ioti[:], pattern=[[1, 32]], base=0, channel_multiplier=0)
            g.tensor_copy(T_iot[:], T_ioti[:])
            gp.inc(g.memset(T_m[:], float(-np.log(10000.0) / 32.0)))

        @block.scalar
        def _(s):
            gp.wait(s)
            s.preload_act_table(AF.Exp)
            ac.inc(s.activation(T_g[:], T_m[:], AF.Exp))

        @block.vector
        def _(v):
            ac.wait(v)
            v.tensor_tensor(T_m[:], T_g[:], T_g[:], ALU.mult)
            for _sq in range(4):
                v.drain()
                v.tensor_tensor(T_m[:], T_m[:], T_m[:], ALU.mult)
            v.drain()
            v.tensor_scalar(T_m[:], T_m[:], -10000.0, 1.0, ALU.mult, ALU.add)
            v.drain()
            v.tensor_scalar(T_m[:], T_m[:], float(1.0 / 32.0), 1.0, ALU.mult, ALU.add)
            v.drain()
            v.tensor_tensor(T_g[:], T_g[:], T_m[:], ALU.mult)
            v.drain()
            v.memset(T_f[:], 1.0)
            for k in range(5):
                b = 2 ** k
                sl = T_f[0:1, :].rearrange("p (a b) -> p a b", b=2 * b)[:, :, b:2 * b]
                v.tensor_scalar_mul(sl, sl, T_g[0:1, 0:1])
                v.drain()
                if k < 4:
                    v.tensor_tensor(T_g[:], T_g[:], T_g[:], ALU.mult)
                    v.drain()
            dv.inc(v.drain())

        @block.tensor
        def _(t):
            dv.wait(t)
            pe.inc(t.matmul(T_ps[:, 0:32], P_ones[:], T_f[:], start=True, stop=True))

        @block.vector
        def _(v):
            pe.wait(v)
            v.tensor_copy(T_ifb[:], T_ps[:, 0:32])
            sy.wait(v, ld1)
            v.drain()
            for t in range(NT):
                v.tensor_scalar_mul(T_th[:, t], T_ifb[:], T_pos[:, t:t + 1])
            v.drain()
            TH = T_th[:].rearrange("p a c -> p (a c)")
            K_ = T_k[:].rearrange("p a c -> p (a c)")
            W_ = T_w[:].rearrange("p a c -> p (a c)")
            v.tensor_scalar(K_, TH, float(1.0 / TWO_PI), MAGIC, ALU.mult, ALU.add)
            v.drain()
            v.tensor_scalar_add(K_, K_, -MAGIC)
            v.drain()
            v.scalar_tensor_tensor(W_, K_, -CW0, TH, ALU.mult, ALU.add)
            v.drain()
            v.scalar_tensor_tensor(W_, K_, -CW1, W_, ALU.mult, ALU.add)
            v.drain()
            v.scalar_tensor_tensor(W_, K_, -CW2, W_, ALU.mult, ALU.add)
            v.drain()
            v.tensor_scalar_min(W_, W_, float(np.pi))
            v.drain()
            v.tensor_scalar_max(W_, W_, float(-np.pi))
            v.drain()
            v.tensor_scalar_add(K_, W_, float(np.pi / 2))
            v.drain()
            v.tensor_scalar(TH, K_, float(np.pi), None, ALU.is_gt)
            v.drain()
            v.scalar_tensor_tensor(K_, TH, float(-TWO_PI), K_, ALU.mult, ALU.add)
            dv.inc(v.drain())

        @block.scalar
        def _(s):
            dv.wait(s)
            s.preload_act_table(AF.Sin)
            s.activation(P_sin[:].rearrange("p a c -> p (a c)"),
                         T_w[:].rearrange("p a c -> p (a c)"), AF.Sin)
            ac.inc(s.activation(P_cos[:].rearrange("p a c -> p (a c)"),
                                T_k[:].rearrange("p a c -> p (a c)"), AF.Sin))

        if debug:
            @block.sync
            def _(sync):
                ac.wait(sync)
                dv.wait(sync)
                sy.dma(sync.dma_start(out=taps["t_cos"][:, :],
                                      in_=P_cos[:].rearrange("p a c -> p (a c)")))
                sy.dma(sync.dma_start(out=taps["t_ifr"][:, :], in_=T_f[:]))
        barrier()

    def emit_ln(src_ap, n_feat, sq_ap, stats, out_ap):
        @block.vector
        def _(v):
            v.reduce_sum(stats[:, 0:1], src_ap, axis=AX)
            v.drain()
            v.tensor_scalar_mul(stats[:, 1:2], stats[:, 0:1], float(-1.0 / n_feat))
            dv.inc(v.drain())

        @block.scalar
        def _(s):
            dv.wait(s)
            ac.inc(s.activation(sq_ap, src_ap, AF.Square, bias=stats[:, 1:2],
                                accum_out=stats[:, 2:3]))

        @block.vector
        def _(v):
            ac.wait(v)
            v.tensor_scalar(stats[:, 3:4], stats[:, 2:3], float(1.0 / n_feat), EPS,
                            ALU.mult, ALU.add)
            dv.inc(v.drain())

        @block.scalar
        def _(s):
            dv.wait(s)
            ac.inc(s.activation(stats[:, 4:5], stats[:, 3:4], AF.Sqrt))

        @block.vector
        def _(v):
            ac.wait(v)
            v.reciprocal(stats[:, 5:6], stats[:, 4:5])
            v.drain()
            v.tensor_tensor(stats[:, 6:7], stats[:, 1:2], stats[:, 5:6], ALU.mult)
            v.drain()
            v.tensor_scalar(out_ap, src_ap, stats[:, 5:6], stats[:, 6:7],
                            ALU.mult, ALU.add)
            dv.inc(v.drain())

    with nc.sbuf_tensor("XT0", [128, D], F32) as XT0, \
         nc.sbuf_tensor("XT1", [128, D], F32) as XT1, \
         nc.sbuf_tensor("XNB0", [128, D], BF16) as XNB0, \
         nc.sbuf_tensor("XNB1", [128, D], BF16) as XNB1, \
         nc.sbuf_tensor("SQ", [128, 2048], F32) as SQ, \
         nc.psum_tensor("TRP", [128, 2048], BF16) as TRP:

        @block.scalar
        def _(s):
            ac.inc(s.preload_act_table(AF.Sqrt))

        XT, XNB = [XT0, XT1], [XNB0, XNB1]
        x_done, evac_done = {}, {}
        for t in range(NTQ):
            bi = t % 2

            @block.sync
            def _(sync, t=t, bi=bi):
                if t >= 2:
                    dv.wait(sync, x_done[t - 2])
                sy.dma(sync.dma_start(out=XT[bi][:], in_=xb[t * 128:(t + 1) * 128, :]))
            ld = sy.n

            @block.vector
            def _(v, ld=ld):
                sy.wait(v, ld)
            emit_ln(XT[bi][:], D, SQ[:], ST1[:, 8 * bi:8 * bi + 8], XNB[bi][:])
            x_done[t] = dv.n

            @block.tensor
            def _(t_, t=t, bi=bi):
                dv.wait(t_)
                if t >= 1:
                    dv.wait(t_, evac_done[t - 1])
                for fc in range(16):
                    ins = t_.transpose(TRP[:, fc * 128:(fc + 1) * 128],
                                       XNB[bi][:, fc * 128:(fc + 1) * 128], P_ident[:])
                pe.inc(ins)

            @block.vector
            def _(v, t=t):
                pe.wait(v)
                dv.inc(v.tensor_copy(xn_fm[:, :, t * 128:(t + 1) * 128],
                                     TRP[:].rearrange("p (a c) -> p a c", c=128)))
                evac_done[t] = dv.n
        barrier()

    if debug:
        with nc.sbuf_tensor("DBG0", [128, 2048], F32) as DBG0:
            dump_tap(lambda a: xn_fm[:].rearrange("p a c -> p (a c)")[:, a * 2048:(a + 1) * 2048],
                     taps["t_xn"], 8, 2048, DBG0)

    NQ4 = 4
    QHC = NHC // NQ4
    with nc.sbuf_tensor("SDB", [128, D], F32) as SDB, \
         nc.sbuf_tensor("SDROW", [1, D], F32) as SDROW, \
         nc.sbuf_tensor("G8", [128, 2, D], U8) as G8b, \
         nc.sbuf_tensor("U8T", [128, 2, D], U8) as U8b, \
         nc.sbuf_tensor("D8", [128, 2, D], U8) as D8b, \
         nc.sbuf_tensor("GW", [128, 2, 16, 128], BF16) as GWb, \
         nc.sbuf_tensor("UW", [128, 2, 16, 128], BF16) as UWb, \
         nc.sbuf_tensor("SILU", [128, 2, OWN], BF16) as SILUb, \
         nc.sbuf_tensor("MQ", [128, QHC, OWN], BF16) as MQ, \
         nc.sbuf_tensor("DWQ", [128, QHC, D], BF16) as DWQ, \
         nc.sbuf_tensor("YSB", [128, D], F32) as YSB, \
         nc.sbuf_tensor("YOLD", [128, D], F32) as YOLD, \
         nc.psum_tensor("G_ps", [128, 1024], F32) as G_ps, \
         nc.psum_tensor("U_ps", [128, 1024], F32) as U_ps, \
         nc.psum_tensor("Y_ps", [128, 2048], F32) as Y_ps:

        @block.scalar
        def _(s):
            ac.inc(s.preload_act_table(AF.Silu))

        @block.sync
        def _(sync):
            sy.dma(sync.dma_start(out=SDROW[:], in_=sdr[:, :]))
        ldsd = sy.n

        @block.tensor
        def _(t):
            sy.wait(t, ldsd)
            for j in range(4):
                ins = t.matmul(Y_ps[:, j * 512:(j + 1) * 512], P_ones[:],
                               SDROW[:, j * 512:(j + 1) * 512], start=True, stop=True)
            pe.inc(ins)

        @block.vector
        def _(v):
            pe.wait(v)
            dv.inc(v.tensor_copy(SDB[:], Y_ps[:]))
        sdb_done = dv.n

        fold_done, gu_done, silu_done, m_done = {}, {}, {}, {}
        down_done, yevac_done, dfold_done = {}, {}, {}
        yadd_done, yst_done, gd_ld, gfold_done = {}, {}, {}, {}
        for q4 in range(NQ4):
            for i in range(QHC):
                hc = q4 * QHC + i
                par = hc % 2
                G8, U8T, D8 = G8b[:, par], U8b[:, par], D8b[:, par]
                GW, UW, SILU = GWb[:, par], UWb[:, par], SILUb[:, par]

                @block.gpsimd
                def _(g, hc=hc, i=i, q4=q4, G8=G8, U8T=U8T, D8=D8):
                    if hc >= 2:
                        dv.wait(g, dfold_done[hc - 2])
                    if i == 0 and q4 > 0:
                        pe.wait(g, down_done[q4 - 1])
                    gd.dma(g.dma_start(out=G8, in_=fg[hc].rearrange("p a c -> p (a c)")))
                    gd.dma(g.dma_start(out=U8T, in_=fu[hc].rearrange("p a c -> p (a c)")))
                    gd.dma(g.dma_start(out=D8, in_=fd[hc]))
                    gd_ld[hc] = gd.n

                @block.vector
                def _(v, hc=hc, i=i, q4=q4, D8=D8, U8T=U8T, G8=G8, GW=GW, UW=UW):
                    gd.wait(v, gd_ld[hc])
                    if hc == 0:
                        dv.wait(v, sdb_done)
                    if hc >= 2:
                        pe.wait(v, gu_done[hc - 2])
                    v.tensor_scalar_add(GW.rearrange("p a c -> p (a c)"), G8, -128.0)
                    v.tensor_scalar_add(UW.rearrange("p a c -> p (a c)"), U8T, -128.0)
                    dv.inc(v.scalar_tensor_tensor(DWQ[:, i], D8, -128.0, SDB[:],
                                                  ALU.add, ALU.mult))
                    dfold_done[hc] = dv.n
                    gfold_done[hc] = dv.n

                @block.tensor
                def _(t, hc=hc, GW=GW, UW=UW):
                    dv.wait(t, dfold_done[hc])
                    if hc > 0:
                        ac.wait(t, silu_done[hc - 1])
                        dv.wait(t, m_done[hc - 1])
                    for dc in range(16):
                        for j in range(2):
                            t.matmul(G_ps[:, j * 512:(j + 1) * 512], GW[:, dc],
                                     xn_fm[:, dc, j * 512:(j + 1) * 512],
                                     start=(dc == 0), stop=(dc == 15))
                    for dc in range(16):
                        for j in range(2):
                            ins = t.matmul(U_ps[:, j * 512:(j + 1) * 512], UW[:, dc],
                                           xn_fm[:, dc, j * 512:(j + 1) * 512],
                                           start=(dc == 0), stop=(dc == 15))
                    pe.inc(ins)
                    gu_done[hc] = pe.n

                @block.scalar
                def _(s, hc=hc, SILU=SILU):
                    pe.wait(s, gu_done[hc])
                    if hc >= 2:
                        dv.wait(s, m_done[hc - 2])
                    ac.inc(s.activation(SILU, G_ps[:], AF.Silu, scale=P_sg[:, hc:hc + 1]))
                    silu_done[hc] = ac.n

                @block.vector
                def _(v, hc=hc, i=i, SILU=SILU):
                    ac.wait(v, silu_done[hc])
                    dv.inc(v.scalar_tensor_tensor(MQ[:, i], U_ps[:], P_su[:, hc:hc + 1],
                                                  SILU, ALU.mult, ALU.mult))
                    m_done[hc] = dv.n

            for tt in range(NTQ):
                if q4 > 0:
                    @block.sync
                    def _(sync, q4=q4, tt=tt):
                        if not (q4 == 1 and tt == 0):
                            dv.wait(sync, yadd_done.get(tt - 1, yadd_done.get(NTQ - 1, 0)))
                        sy.dma(sync.dma_start(out=YOLD[:], in_=ffn_y[tt * 128:(tt + 1) * 128, :]))
                    ldy = sy.n
                for ys in range(2):
                    YH = Y_ps[:, ys * 1024:(ys + 1) * 1024]

                    @block.tensor
                    def _(t, q4=q4, tt=tt, ys=ys, YH=YH):
                        if tt == 0 and ys == 0:
                            dv.wait(t, m_done[q4 * QHC + QHC - 1])
                            dv.wait(t, dfold_done[q4 * QHC + QHC - 1])
                        pk = (q4, tt - 1, ys)
                        pv_ = yevac_done.get(pk, yevac_done.get((q4 - 1, NTQ - 1, ys), None))
                        if pv_ is not None:
                            s_, n_ = pv_
                            s_.wait(t, n_)
                        for i in range(QHC):
                            for j in range(2):
                                ins = t.matmul(YH[:, j * 512:(j + 1) * 512],
                                               MQ[:, i, tt * 128:(tt + 1) * 128],
                                               DWQ[:, i, (2 * ys + j) * 512:(2 * ys + j + 1) * 512],
                                               start=(i == 0), stop=(i == QHC - 1))
                        pe.inc(ins)
                        if tt == NTQ - 1 and ys == 1:
                            down_done[q4] = pe.n

                    if q4 == 0:
                        @block.vector
                        def _(v, q4=q4, tt=tt, ys=ys, YH=YH):
                            pe.wait(v)
                            sy.wait(v, yst_done.get((q4, tt - 1), yst_done.get((q4 - 1, NTQ - 1), 0)))
                            dv.inc(v.tensor_copy(YSB[:, ys * 1024:(ys + 1) * 1024], YH))
                            yevac_done[(0, tt, ys)] = (dv, dv.n)
                    else:
                        @block.vector
                        def _(v, ldy=ldy, q4=q4, tt=tt, ys=ys, YH=YH):
                            sy.wait(v, ldy)
                            sy.wait(v, yst_done.get((q4, tt - 1), yst_done.get((q4 - 1, NTQ - 1), 0)))
                            pe.wait(v)
                            dv.inc(v.tensor_tensor(YSB[:, ys * 1024:(ys + 1) * 1024], YH,
                                                   YOLD[:, ys * 1024:(ys + 1) * 1024], ALU.add))
                            yevac_done[(q4, tt, ys)] = (dv, dv.n)
                            if ys == 1:
                                yadd_done[tt] = dv.n

                @block.sync
                def _(sync, q4=q4, tt=tt):
                    s_, n_ = yevac_done[(q4, tt, 1)]
                    s_.wait(sync, n_)
                    s_, n_ = yevac_done[(q4, tt, 0)]
                    s_.wait(sync, n_)
                    sy.dma(sync.dma_start(out=ffn_y[tt * 128:(tt + 1) * 128, :], in_=YSB[:]))
                    yst_done[(q4, tt)] = sy.n
        barrier()

    if debug:
        with nc.sbuf_tensor("DBGF", [128, D], F32) as DBGF:
            for tt in range(NTQ):
                @block.sync
                def _(sync, tt=tt):
                    sy.dma(sync.dma_start(out=DBGF[:], in_=ffn_y[tt * 128:(tt + 1) * 128, :]))
                    sy.wait(sync)
                    sy.dma(sync.dma_start(out=taps["t_ffn"][tt * 128:(tt + 1) * 128, :], in_=DBGF[:]))
                barrier()

    stack_b = contextlib.ExitStack()
    cq_fm = stack_b.enter_context(nc.sbuf_tensor("cq_fm", [128, 12, OWN], BF16))
    with nc.sbuf_tensor("WQD", [128, 16, QR], BF16) as WQD, \
         nc.sbuf_tensor("SQc", [128, QR], F32) as SQ, \
         nc.sbuf_tensor("CQT", [128, QR], BF16) as CQT, \
         nc.psum_tensor("CQ_ps", [128, QR], F32) as CQ_ps, \
         nc.psum_tensor("TR3", [128, QR], BF16) as TR3:

        @block.gpsimd
        def _(g):
            for a in range(4):
                gd.dma(g.dma_start(out=WQD[:, 4 * a:4 * (a + 1)],
                                   in_=wqd[a * 512:(a + 1) * 512, :].rearrange("(a p) c -> p a c", p=128)))
        ldw = gd.n

        @block.scalar
        def _(s):
            ac.inc(s.preload_act_table(AF.Sqrt))

        cq_evac, cqt_done = {}, {}
        for t in range(NTQ):
            @block.tensor
            def _(t_, t=t):
                if t == 0:
                    gd.wait(t_, ldw)
                else:
                    dv.wait(t_, cqt_done[t - 1])
                for dc in range(16):
                    for j in range(3):
                        ins = t_.matmul(CQ_ps[:, j * 512:(j + 1) * 512],
                                        xn_fm[:, dc, t * 128:(t + 1) * 128],
                                        WQD[:, dc, j * 512:(j + 1) * 512],
                                        start=(dc == 0), stop=(dc == 15))
                pe.inc(ins)

            @block.vector
            def _(v):
                pe.wait(v)
            emit_ln(CQ_ps[:], QR, SQ[:, 0:QR], ST2, CQT[:])
            cqt_done[t] = dv.n

            @block.tensor
            def _(t_, t=t):
                dv.wait(t_, cqt_done[t])
                if t >= 1:
                    dv.wait(t_, cq_evac[t - 1])
                for rc in range(12):
                    ins = t_.transpose(TR3[:, rc * 128:(rc + 1) * 128],
                                       CQT[:, rc * 128:(rc + 1) * 128], P_ident[:])
                pe.inc(ins)

            @block.vector
            def _(v, t=t):
                pe.wait(v)
                dv.inc(v.tensor_copy(cq_fm[:, :, t * 128:(t + 1) * 128],
                                     TR3[:].rearrange("p (a c) -> p a c", c=128)))
                cq_evac[t] = dv.n
        barrier()

    if debug:
        with nc.sbuf_tensor("DBG1", [128, 2048], F32) as DBG1:
            dump_tap(lambda a: cq_fm[:].rearrange("p a c -> p (a c)")[:, a * 2048:(a + 1) * 2048],
                     taps["t_cq"], 6, 2048, DBG1)

    with nc.sbuf_tensor("WQU", [128, 12, D], BF16) as WQU, \
         nc.sbuf_tensor("BROW", [1, D], F32) as BROW, \
         nc.sbuf_tensor("B16", [16, 128], F32) as B16, \
         nc.sbuf_tensor("QCS", [128, OWN], BF16) as QCS, \
         nc.psum_tensor("QP", [128, 1024], F32) as QP, \
         nc.psum_tensor("BP", [128, 2048], F32) as BP:

        @block.gpsimd
        def _(g):
            for a in range(3):
                gd.dma(g.dma_start(out=WQU[:, 4 * a:4 * (a + 1)],
                                   in_=wqu[a * 512:(a + 1) * 512, :].rearrange("(a p) c -> p a c", p=128)))
        ldw3 = gd.n

        @block.vector
        def _(v):
            sy.wait(v, ld0)
            gd.wait(v, ldw3)
            v.tensor_copy(P_qbb[:], P_qb[:])
            dv.inc(v.tensor_copy(P_kbb[:], P_kb[:]))
        beta_done = dv.n

        @block.tensor
        def _(t):
            dv.wait(t, beta_done)
            for rc in range(12):
                for j in range(4):
                    ins = t.matmul(BP[0:1, j * 512:(j + 1) * 512],
                                   P_qbb[:, rc:rc + 1], WQU[:, rc, j * 512:(j + 1) * 512],
                                   start=(rc == 0), stop=(rc == 11))
            pe.inc(ins)

        @block.scalar
        def _(s):
            pe.wait(s)
            ac.inc(s.activation(BROW[:], BP[0:1, :], AF.Identity, scale=SCALE))

        @block.sync
        def _(sync):
            ac.wait(sync)
            sy.dma(sync.dma_start(out=bb[0:1, :], in_=BROW[:]))
            sy.wait(sync)
            sy.dma(sync.dma_start(out=B16[:], in_=bb[0, :].rearrange("(a c) -> a c", c=128)))
        ldb = sy.n

        @block.tensor
        def _(t):
            sy.wait(t, ldb)
            pe.inc(t.transpose(BP[:, 0:16], B16[:], P_identf[0:16, 0:16]))

        @block.vector
        def _(v):
            pe.wait(v)
            dv.inc(v.tensor_copy(P_qcb[:], BP[:, 0:16]))
        qcb_done = dv.n

        @block.vector
        def _(v):
            dv.wait(v, qcb_done)
            dv.inc(v.tensor_scalar_mul(ST1[:, 0:12], P_qg[:], SCALE))
        qgs_done = dv.n

        @block.gpsimd
        def _(g):
            dv.wait(g, qgs_done)
            for rc in range(12):
                ins = g.tensor_scalar_mul(WQU[:, rc], WQU[:, rc], ST1[:, rc:rc + 1])
            gp.inc(ins)
        fold_q = gp.n

        qc_done, qcs_st = {}, {}
        for h in range(H):
            @block.tensor
            def _(t, h=h):
                gp.wait(t, fold_q)
                if h > 0:
                    ac.wait(t, qc_done[h - 1])
                for rc in range(12):
                    for j in range(2):
                        ins = t.matmul(QP[:, j * 512:(j + 1) * 512],
                                       WQU[:, rc, h * 128:(h + 1) * 128],
                                       cq_fm[:, rc, j * 512:(j + 1) * 512],
                                       start=(rc == 0), stop=(rc == 11))
                pe.inc(ins)

            @block.scalar
            def _(s, h=h):
                pe.wait(s)
                if h > 0:
                    sy.wait(s, qcs_st[h - 1])
                ac.inc(s.activation(QCS[:], QP[:], AF.Identity, bias=P_qcb[:, h:h + 1]))
                qc_done[h] = ac.n

            @block.sync
            def _(sync, h=h):
                ac.wait(sync, qc_done[h])
                sy.dma(sync.dma_start(out=qc_hbm[h], in_=QCS[:]))
                qcs_st[h] = sy.n

        barrier()

    with nc.sbuf_tensor("WQR", [128, 12, H * DR], BF16) as WQR, \
         nc.sbuf_tensor("P_qrb", [128, H * DR], F32) as P_qrb, \
         nc.sbuf_tensor("BROWr", [1, H * DR], F32) as BROWr, \
         nc.sbuf_tensor("QRT", [128, OWN], F32) as QRT, \
         nc.sbuf_tensor("RTMP", [128, OWN], F32) as RTMP, \
         nc.sbuf_tensor("RT2", [128, 512], F32) as RT2, \
         nc.sbuf_tensor("QRB", [128, OWN], BF16) as QRB, \
         nc.sbuf_tensor("QRE", [128, OWN], BF16) as QRE, \
         nc.psum_tensor("QPr", [128, 1024], F32) as QP, \
         nc.psum_tensor("BPr", [128, 1024], F32) as BPr, \
         nc.psum_tensor("TR4r", [128, 1024], BF16) as TR4:

        @block.gpsimd
        def _(g):
            for a in range(3):
                gd.dma(g.dma_start(out=WQR[:, 4 * a:4 * (a + 1)],
                                   in_=wqr[a * 512:(a + 1) * 512, :].rearrange("(a p) c -> p a c", p=128)))
        ldw3r = gd.n

        @block.tensor
        def _(t):
            gd.wait(t, ldw3r)
            for rc in range(12):
                for j in range(2):
                    ins = t.matmul(BPr[0:1, j * 512:(j + 1) * 512],
                                   P_qbb[:, rc:rc + 1], WQR[:, rc, j * 512:(j + 1) * 512],
                                   start=(rc == 0), stop=(rc == 11))
            pe.inc(ins)

        @block.scalar
        def _(s):
            pe.wait(s)
            ac.inc(s.activation(BROWr[:], BPr[0:1, :], AF.Identity, scale=SCALE))

        @block.tensor
        def _(t):
            ac.wait(t)
            for j in range(2):
                ins = t.matmul(BPr[:, j * 512:(j + 1) * 512], P_ones[:],
                               BROWr[:, j * 512:(j + 1) * 512], start=True, stop=True)
            pe.inc(ins)

        @block.vector
        def _(v):
            pe.wait(v)
            dv.inc(v.tensor_copy(P_qrb[:], BPr[:]))
        qrb_done = dv.n

        @block.gpsimd
        def _(g):
            dv.wait(g, qrb_done)
            for rc in range(12):
                ins = g.tensor_scalar_mul(WQR[:, rc], WQR[:, rc], ST1[:, rc:rc + 1])
            gp.inc(ins)
        fold_qr = gp.n

        qr_evac, qre_st = {}, {}
        for tt in range(NTQ):
            @block.tensor
            def _(t, tt=tt):
                gp.wait(t, fold_qr)
                if tt > 0:
                    dv.wait(t, qr_evac[tt - 1])
                for rc in range(12):
                    for j in range(2):
                        ins = t.matmul(QP[:, j * 512:(j + 1) * 512],
                                       cq_fm[:, rc, tt * 128:(tt + 1) * 128],
                                       WQR[:, rc, j * 512:(j + 1) * 512],
                                       start=(rc == 0), stop=(rc == 11))
                pe.inc(ins)

            @block.vector
            def _(v, tt=tt):
                pe.wait(v)
                v.tensor_tensor(QRT[:], QP[:], P_qrb[:], ALU.add)
                v.drain()
                q4 = QRT[:].rearrange("p (h d f) -> p h d f", h=16, d=2)
                t4 = RTMP[:].rearrange("p (h d f) -> p h d f", h=16, d=2)
                o4 = QRB[:].rearrange("p (h d f) -> p h d f", h=16, d=2)
                cosb = P_cos[:, tt, None, None, :].broadcast_to((128, 16, 2, 32))
                sinh = P_sin[:, tt, None, :].broadcast_to((128, 16, 32))
                r2 = RT2[:].rearrange("p (h f) -> p h f", h=16)
                v.tensor_tensor(t4, q4, cosb, ALU.mult)
                v.scalar_tensor_tensor(r2, q4[:, :, 1, :], -1.0, sinh, ALU.mult, ALU.mult)
                v.drain()
                v.tensor_tensor(o4[:, :, 0, :], t4[:, :, 0, :], r2, ALU.add)
                v.drain()
                v.tensor_tensor(r2, q4[:, :, 0, :], sinh, ALU.mult)
                v.drain()
                v.tensor_tensor(o4[:, :, 1, :], t4[:, :, 1, :], r2, ALU.add)
                dv.inc(v.drain())

            @block.tensor
            def _(t, tt=tt):
                dv.wait(t)
                if tt >= 1:
                    dv.wait(t, qr_evac[tt - 1])
                for j in range(8):
                    ins = t.transpose(TR4[:, j * 128:(j + 1) * 128],
                                      QRB[:, j * 128:(j + 1) * 128], P_ident[:])
                pe.inc(ins)

            @block.vector
            def _(v, tt=tt):
                pe.wait(v)
                if tt > 0:
                    sy.wait(v, qre_st[tt - 1])
                dv.inc(v.tensor_copy(QRE[:], TR4[:]))
                qr_evac[tt] = dv.n

            @block.sync
            def _(sync, tt=tt):
                dv.wait(sync, qr_evac[tt])
                sy.dma(sync.dma_start(out=qr_hbm[:, :, tt * 128:(tt + 1) * 128].rearrange("a p c -> p a c"),
                                      in_=QRE[:].rearrange("p (a c) -> p a c", c=128)))
                qre_st[tt] = sy.n
        barrier()

    stack_b.close()

    if debug:
        with nc.sbuf_tensor("DBG3", [128, 2048], F32) as DBG3:
            for a in range(4):
                @block.gpsimd
                def _(g, a=a):
                    gd.dma(g.dma_start(out=DBG3[:].rearrange("p (a c) -> p a c", c=OWN), in_=qr_hbm[2 * a:2 * a + 2].rearrange("a p c -> p a c")))

                @block.sync
                def _(sync, a=a):
                    gd.wait(sync)
                    sy.dma(sync.dma_start(out=taps["t_qr"][:, a * 2048:(a + 1) * 2048], in_=DBG3[:]))
                barrier()

    with nc.sbuf_tensor("WKD", [128, 16, KVR], BF16) as WKD, \
         nc.sbuf_tensor("SQk", [128, 2048], F32) as SQ, \
         nc.sbuf_tensor("XT0b", [128, D], F32) as XT0b, \
         nc.sbuf_tensor("XT1b", [128, D], F32) as XT1b, \
         nc.sbuf_tensor("XNB0b", [128, D], BF16) as XNB0b, \
         nc.sbuf_tensor("XNB1b", [128, D], BF16) as XNB1b, \
         nc.sbuf_tensor("XFM0", [128, 16, 128], BF16) as XFM0, \
         nc.sbuf_tensor("XFM1", [128, 16, 128], BF16) as XFM1, \
         nc.sbuf_tensor("CKVT", [128, KVR], BF16) as CKVT, \
         nc.psum_tensor("TRPb", [128, 2048], BF16) as TRPb, \
         nc.psum_tensor("CKV_ps", [128, KVR], F32) as CKV_ps, \
         nc.psum_tensor("TR2", [128, KVR], BF16) as TR2:

        @block.gpsimd
        def _(g):
            for a in range(4):
                gd.dma(g.dma_start(out=WKD[:, 4 * a:4 * (a + 1)],
                                   in_=wkd[a * 512:(a + 1) * 512, :].rearrange("(a p) c -> p a c", p=128)))
        ldw2 = gd.n

        XTb, XNBb, XFM = [XT0b, XT1b], [XNB0b, XNB1b], [XFM0, XFM1]
        xb_done, xfm_done, ckvt_done, kv_evac = {}, {}, {}, {}
        for t in range(NT):
            bi = t % 2
            if t >= NTQ:
                @block.sync
                def _(sync, t=t, bi=bi):
                    if t >= NTQ + 2:
                        dv.wait(sync, xb_done[t - 2])
                    sy.dma(sync.dma_start(out=XTb[bi][:], in_=xb[t * 128:(t + 1) * 128, :]))
                ld = sy.n

                @block.vector
                def _(v, ld=ld):
                    sy.wait(v, ld)
                emit_ln(XTb[bi][:], D, SQ[:], ST1[:, 8 * bi:8 * bi + 8], XNBb[bi][:])
                xb_done[t] = dv.n

                @block.tensor
                def _(t_, t=t, bi=bi):
                    dv.wait(t_, xb_done[t])
                    if t >= NTQ + 2:
                        dv.wait(t_, xfm_done[t - 2])
                    for fc in range(16):
                        ins = t_.transpose(TRPb[:, fc * 128:(fc + 1) * 128],
                                           XNBb[bi][:, fc * 128:(fc + 1) * 128], P_ident[:])
                    pe.inc(ins)

                @block.vector
                def _(v, t=t, bi=bi):
                    pe.wait(v)
                    dv.inc(v.tensor_copy(XFM[bi][:], TRPb[:].rearrange("p (a c) -> p a c", c=128)))
                    xfm_done[t] = dv.n
                xsrc = XFM[bi]
            else:
                xsrc = None

            @block.tensor
            def _(t_, t=t, xsrc=xsrc):
                if t == 0:
                    gd.wait(t_, ldw2)
                else:
                    dv.wait(t_, ckvt_done[t - 1])
                if xsrc is not None:
                    dv.wait(t_, xfm_done[t])
                for dc in range(16):
                    src = xsrc[:, dc] if xsrc is not None else xn_fm[:, dc, t * 128:(t + 1) * 128]
                    ins = t_.matmul(CKV_ps[:], src, WKD[:, dc], start=(dc == 0), stop=(dc == 15))
                pe.inc(ins)

            @block.vector
            def _(v):
                pe.wait(v)
            emit_ln(CKV_ps[:], KVR, SQ[:, 0:KVR], ST2, CKVT[:])
            ckvt_done[t] = dv.n

            @block.tensor
            def _(t_, t=t):
                dv.wait(t_, ckvt_done[t])
                if t >= 1:
                    dv.wait(t_, kv_evac[t - 1])
                for rc in range(4):
                    ins = t_.transpose(TR2[:, rc * 128:(rc + 1) * 128],
                                       CKVT[:, rc * 128:(rc + 1) * 128], P_ident[:])
                pe.inc(ins)

            @block.vector
            def _(v, t=t):
                pe.wait(v)
                dv.inc(v.tensor_copy(ckv_fm[:, :, t * 128:(t + 1) * 128],
                                     TR2[:].rearrange("p (a c) -> p a c", c=128)))
                kv_evac[t] = dv.n
        barrier()

    stack_a.close()

    if debug:
        with nc.sbuf_tensor("DBG2", [128, 2048], F32) as DBG2:
            dump_tap(lambda a: ckv_fm[:].rearrange("p a c -> p (a c)")[:, a * 2048:(a + 1) * 2048],
                     taps["t_ckv"], 8, 2048, DBG2)

    o_fm = E(nc.sbuf_tensor("o_fm", [128, H, OWN], BF16))
    stack_c = contextlib.ExitStack()
    Wku = stack_c.enter_context(nc.sbuf_tensor("Wku", [128, 4, D], BF16))
    Wvu = stack_c.enter_context(nc.sbuf_tensor("Wvu", [128, 4, D], BF16))
    with nc.sbuf_tensor("WKR", [128, 4, H * DR], BF16) as WKR, \
         nc.sbuf_tensor("BROW2", [1, D], F32) as BROW2, \
         nc.sbuf_tensor("B16b", [16, 128], F32) as B16b, \
         nc.sbuf_tensor("P_krb", [128, H * DR], F32) as P_krb, \
         nc.sbuf_tensor("KRT", [128, 1024], F32) as KRT, \
         nc.sbuf_tensor("RTMPb", [128, 1024], F32) as RTMPb, \
         nc.sbuf_tensor("RT2b", [128, 512], F32) as RT2b, \
         nc.sbuf_tensor("KRB", [128, 1024], BF16) as KRB, \
         nc.psum_tensor("KP", [128, 1024], F32) as KP, \
         nc.psum_tensor("BPb", [128, 1024], F32) as BPb:

        @block.gpsimd
        def _(g):
            gd.dma(g.dma_start(out=Wku[:], in_=wku[:, :].rearrange("(a p) c -> p a c", p=128)))
            gd.dma(g.dma_start(out=Wvu[:], in_=wvu[:, :].rearrange("(a p) c -> p a c", p=128)))
            gd.dma(g.dma_start(out=WKR[:], in_=wkr[:, :].rearrange("(a p) c -> p a c", p=128)))
        ldw4 = gd.n

        vb_done = {}
        for half in range(2):
            @block.tensor
            def _(t, half=half):
                gd.wait(t, ldw4)
                if half == 1:
                    ac.wait(t, vb_done[0])
                for rc in range(4):
                    for j in range(2):
                        ins = t.matmul(BPb[0:1, j * 512:(j + 1) * 512],
                                       P_kbb[:, rc:rc + 1],
                                       Wvu[:, rc, (half * 2 + j) * 512:(half * 2 + j + 1) * 512],
                                       start=(rc == 0), stop=(rc == 3))
                pe.inc(ins)

            @block.scalar
            def _(s, half=half):
                pe.wait(s)
                ac.inc(s.copy(BROW2[:, half * 1024:(half + 1) * 1024], BPb[0:1, :]))
                vb_done[half] = ac.n

        @block.sync
        def _(sync):
            ac.wait(sync, vb_done[1])
            sy.dma(sync.dma_start(out=bb[1:2, :], in_=BROW2[:]))
            sy.wait(sync)
            sy.dma(sync.dma_start(out=B16b[:], in_=bb[1, :].rearrange("(a c) -> a c", c=128)))
        ldb2 = sy.n

        @block.tensor
        def _(t):
            sy.wait(t, ldb2)
            pe.inc(t.transpose(BPb[:, 0:16], B16b[:], P_identf[0:16, 0:16]))

        @block.vector
        def _(v):
            pe.wait(v)
            dv.inc(v.tensor_copy(P_vb[:], BPb[:, 0:16]))
        pvb_done = dv.n

        @block.tensor
        def _(t):
            dv.wait(t, pvb_done)
            for rc in range(4):
                for j in range(2):
                    ins = t.matmul(BPb[0:1, j * 512:(j + 1) * 512],
                                   P_kbb[:, rc:rc + 1], WKR[:, rc, j * 512:(j + 1) * 512],
                                   start=(rc == 0), stop=(rc == 3))
            pe.inc(ins)

        @block.scalar
        def _(s):
            pe.wait(s)
            ac.inc(s.copy(BROW2[:, 0:1024], BPb[0:1, :]))

        @block.tensor
        def _(t):
            ac.wait(t)
            for j in range(2):
                ins = t.matmul(BPb[:, j * 512:(j + 1) * 512], P_ones[:],
                               BROW2[:, j * 512:(j + 1) * 512], start=True, stop=True)
            pe.inc(ins)

        @block.vector
        def _(v):
            pe.wait(v)
            dv.inc(v.tensor_copy(P_krb[:], BPb[:]))
        krb_done = dv.n

        @block.gpsimd
        def _(g):
            dv.wait(g, krb_done)
            for rc in range(4):
                g.tensor_scalar_mul(Wku[:, rc], Wku[:, rc], P_kg[:, rc:rc + 1])
            for rc in range(4):
                g.tensor_scalar_mul(Wvu[:, rc], Wvu[:, rc], P_kg[:, rc:rc + 1])
            for rc in range(4):
                ins = g.tensor_scalar_mul(WKR[:, rc], WKR[:, rc], P_kg[:, rc:rc + 1])
            gp.inc(ins)
        fold_k = gp.n

        kr_done, kr_st = {}, {}
        for tt in range(NT):
            @block.tensor
            def _(t, tt=tt):
                gp.wait(t, fold_k)
                if tt > 0:
                    dv.wait(t, kr_done[tt - 1])
                for rc in range(4):
                    for j in range(2):
                        ins = t.matmul(KP[:, j * 512:(j + 1) * 512],
                                       ckv_fm[:, rc, tt * 128:(tt + 1) * 128],
                                       WKR[:, rc, j * 512:(j + 1) * 512],
                                       start=(rc == 0), stop=(rc == 3))
                pe.inc(ins)

            @block.vector
            def _(v, tt=tt):
                pe.wait(v)
                if tt > 0:
                    sy.wait(v, kr_st[tt - 1])
                v.tensor_tensor(KRT[:], KP[:], P_krb[:], ALU.add)
                v.drain()
                q4 = KRT[:].rearrange("p (h d f) -> p h d f", h=16, d=2)
                t4 = RTMPb[:].rearrange("p (h d f) -> p h d f", h=16, d=2)
                o4 = KRB[:].rearrange("p (h d f) -> p h d f", h=16, d=2)
                cosb = P_cos[:, tt, None, None, :].broadcast_to((128, 16, 2, 32))
                sinh = P_sin[:, tt, None, :].broadcast_to((128, 16, 32))
                r2 = RT2b[:].rearrange("p (h f) -> p h f", h=16)
                v.tensor_tensor(t4, q4, cosb, ALU.mult)
                v.scalar_tensor_tensor(r2, q4[:, :, 1, :], -1.0, sinh, ALU.mult, ALU.mult)
                v.drain()
                v.tensor_tensor(o4[:, :, 0, :], t4[:, :, 0, :], r2, ALU.add)
                v.drain()
                v.tensor_tensor(r2, q4[:, :, 0, :], sinh, ALU.mult)
                v.drain()
                v.tensor_tensor(o4[:, :, 1, :], t4[:, :, 1, :], r2, ALU.add)
                dv.inc(v.drain())
                kr_done[tt] = dv.n

            @block.sync
            def _(sync, tt=tt):
                dv.wait(sync, kr_done[tt])
                sy.dma(sync.dma_start(out=kr_tm[tt * 128:(tt + 1) * 128, :], in_=KRB[:]))
                kr_st[tt] = sy.n
        barrier()

    with contextlib.ExitStack() as _attn_stack:
        _AE = _attn_stack.enter_context
        KRF0 = _AE(nc.sbuf_tensor("KRF0", [128, S], BF16))
        KRF1 = _AE(nc.sbuf_tensor("KRF1", [128, S], BF16))
        QRP0 = _AE(nc.sbuf_tensor("QRP0", [128, OWN], BF16))
        QRP1 = _AE(nc.sbuf_tensor("QRP1", [128, OWN], BF16))
        QCL0 = _AE(nc.sbuf_tensor("QCL0", [128, OWN], BF16))
        QCL1 = _AE(nc.sbuf_tensor("QCL1", [128, OWN], BF16))
        KC = _AE(nc.sbuf_tensor("KC", [128, S], BF16))
        VSB = _AE(nc.sbuf_tensor("VSB", [128, 32, 128], BF16))
        PA0 = _AE(nc.sbuf_tensor("PA0", [128, 2048], BF16))
        PA1 = _AE(nc.sbuf_tensor("PA1", [128, 2048], BF16))
        PB0 = _AE(nc.sbuf_tensor("PB0", [128, 2048], BF16))
        PB1 = _AE(nc.sbuf_tensor("PB1", [128, 2048], BF16))
        PTS = _AE(nc.sbuf_tensor("PTS", [128, 2048], BF16))
        SSTT = _AE(nc.sbuf_tensor("SSTT", [128, 48], F32))
        OT1 = _AE(nc.sbuf_tensor("OT1", [128, 128], F32))
        OT2 = _AE(nc.sbuf_tensor("OT2", [128, 128], BF16))
        SC_ps = _AE(nc.psum_tensor("SC_ps", [128, 2048], F32))
        PT_ps = _AE(nc.psum_tensor("PT_ps", [128, 2048], BF16))
        O_ps = _AE(nc.psum_tensor("O_ps", [128, 256], F32))
        KB_ps = _AE(nc.psum_tensor("KB_ps", [128, 512], F32))
        @block.scalar
        def _(s):
            ac.inc(s.preload_act_table(AF.Exp))

        KRF = [KRF0, KRF1]
        QRP = [QRP0, QRP1]
        QCL = [QCL0, QCL1]
        krf_used, qcl_used = {}, {}
        kc_evac, v_evac = {}, {}
        exp_done, pts_evac, pv_done, ocomb, ofm_evac, pts_pe = {}, {}, {}, {}, {}, {}
        stats_done = {0: 0, 1: 0}
        for h in range(H):
            hb = h % 2
            pj = h // 2
            if hb == 0:
                @block.sync
                def _(sync, pj=pj):
                    if pj >= 2:
                        pe.wait(sync, krf_used[pj - 2])
                    sy.dma(sync.dma_start_transpose(KRF[pj % 2][:],
                                                    kr_tm[:, pj * 128:(pj + 1) * 128]))
                    sy.dma(sync.dma_start(out=QRP[pj % 2][:], in_=qr_hbm[pj]))

            @block.sync
            def _(sync, h=h, hb=hb):
                if h >= 2:
                    pe.wait(sync, qcl_used[h - 2])
                sy.dma(sync.dma_start(out=QCL[hb][:], in_=qc_hbm[h]))
            ldq = sy.n

            for kb2 in range(2):
                @block.tensor
                def _(t, h=h, kb2=kb2):
                    if kb2 == 0:
                        if h > 0:
                            ac.wait(t, exp_done[(h - 1, NTQ - 1, 1)])
                    else:
                        dv.wait(t, kc_evac[(h, 0)])
                    for nt4 in range(4):
                        nt = 4 * kb2 + nt4
                        for rc in range(4):
                            ins = t.matmul(SC_ps[:, nt4 * 512:(nt4 + 1) * 512],
                                           Wku[:, rc, h * 128:(h + 1) * 128],
                                           ckv_fm[:, rc, nt * 512:(nt + 1) * 512],
                                           start=(rc == 0), stop=(rc == 3))
                    pe.inc(ins)

                @block.vector
                def _(v, h=h, kb2=kb2):
                    pe.wait(v)
                    dv.inc(v.tensor_copy(KC[:, kb2 * 2048:(kb2 + 1) * 2048], SC_ps[:]))
                    kc_evac[(h, kb2)] = dv.n

            for g8 in range(8):
                VB = KB_ps[:] if g8 % 2 == 0 else SC_ps[:, 0:512]
                @block.tensor
                def _(t, h=h, g8=g8, VB=VB):
                    if g8 == 1:
                        dv.wait(t, kc_evac[(h, 1)])
                    if g8 >= 2:
                        dv.wait(t, v_evac[(h, g8 - 2)])
                    if g8 == 0 and h > 0:
                        dv.wait(t, v_evac[(h - 1, 7)])
                    for kt in range(4):
                        for rc in range(4):
                            ins = t.matmul(VB[:, kt * 128:(kt + 1) * 128],
                                           ckv_fm[:, rc, (4 * g8 + kt) * 128:(4 * g8 + kt + 1) * 128],
                                           Wvu[:, rc, h * 128:(h + 1) * 128],
                                           start=(rc == 0), stop=(rc == 3))
                    pe.inc(ins)

                @block.vector
                def _(v, h=h, g8=g8, VB=VB):
                    pe.wait(v)
                    dv.inc(v.tensor_copy(VSB[:, 4 * g8:4 * (g8 + 1)],
                                         VB.rearrange("p (a c) -> p a c", c=128)))
                    v_evac[(h, g8)] = dv.n

            PBUF = [[PA0, PA1], [PB0, PB1]]

            def emit_scores_half(qt, half, h=h, hb=hb, pj=pj, ldq=ldq):
                par = qt % 2
                SST = SSTT[:, 24 * par:24 * par + 24]
                chunk_pe = {}

                @block.tensor
                def _(t, qt=qt, half=half):
                    if qt == 0 and half == 0:
                        sy.wait(t, ldq)
                        dv.wait(t, v_evac[(h, 7)])
                    prev = exp_done.get((h, qt, half - 1),
                                        exp_done.get((h, qt - 1, 1),
                                                     exp_done.get((h - 1, NTQ - 1, 1), 0)))
                    ac.wait(t, prev)
                    for kc in range(4):
                        t.matmul(SC_ps[:, kc * 512:(kc + 1) * 512],
                                 QCL[hb][:, qt * 128:(qt + 1) * 128],
                                 KC[:, (half * 4 + kc) * 512:(half * 4 + kc + 1) * 512],
                                 start=True, stop=False)
                    qr_l = QRP[pj % 2][0:64, qt * 128:(qt + 1) * 128] if hb == 0 else \
                        QRP[pj % 2][64:128, qt * 128:(qt + 1) * 128]
                    for kc in range(4):
                        kr_r = KRF[pj % 2][0:64, (half * 4 + kc) * 512:(half * 4 + kc + 1) * 512] \
                            if hb == 0 else \
                            KRF[pj % 2][64:128, (half * 4 + kc) * 512:(half * 4 + kc + 1) * 512]
                        ins = t.matmul(SC_ps[:, kc * 512:(kc + 1) * 512], qr_l, kr_r,
                                       start=False, stop=True)
                        pe.inc(ins)
                        chunk_pe[kc] = pe.n
                    if qt == NTQ - 1 and half == 1:
                        qcl_used[h] = pe.n
                        if hb == 1:
                            krf_used[pj] = pe.n

                @block.vector
                def _(v, qt=qt, half=half, SST=SST, chunk_pe=chunk_pe):
                    pe.wait(v, chunk_pe[3])
                    v.reduce_max(SST[:, 8 + half:9 + half], SC_ps[:], axis=AX, negate=True)
                    dv.inc(v.drain())

                @block.scalar
                def _(s, qt=qt, half=half, SST=SST, par=par):
                    dv.wait(s)
                    PH = PBUF[half][par]
                    pe.wait(s, pts_pe.get((half, par), 0))
                    ac.inc(s.activation(PH[:], SC_ps[:], AF.Exp,
                                        bias=SST[:, 8 + half:9 + half],
                                        accum_out=SST[:, 20 + half:21 + half]))
                    exp_done[(h, qt, half)] = ac.n

            def emit_stats(qt, h=h):
                par = qt % 2
                SST = SSTT[:, 24 * par:24 * par + 24]

                @block.vector
                def _(v, qt=qt, SST=SST):
                    ac.wait(v, exp_done[(h, qt, 1)])
                    v.tensor_tensor(SST[:, 10:11], SST[:, 8:9], SST[:, 9:10], ALU.min)
                    v.drain()
                    v.tensor_sub(SST[:, 11:12], SST[:, 10:11], SST[:, 8:9])
                    v.tensor_sub(SST[:, 12:13], SST[:, 10:11], SST[:, 9:10])
                    dv.inc(v.drain())

                @block.scalar
                def _(s, SST=SST):
                    dv.wait(s)
                    ac.inc(s.activation(SST[:, 13:15], SST[:, 11:13], AF.Exp))

                @block.vector
                def _(v, qt=qt, SST=SST):
                    ac.wait(v)
                    v.tensor_tensor(SST[:, 15:16], SST[:, 13:14], SST[:, 20:21], ALU.mult)
                    v.drain()
                    v.scalar_tensor_tensor(SST[:, 16:17], SST[:, 14:15], SST[:, 21:22],
                                           SST[:, 15:16], ALU.mult, ALU.add)
                    v.drain()
                    v.reciprocal(SST[:, 17:18], SST[:, 16:17])
                    v.drain()
                    v.tensor_tensor(SST[:, 18:20], SST[:, 13:15],
                                    SST[:, 17:18].broadcast_to((128, 2)), ALU.mult)
                    dv.inc(v.drain())
                stats_done[par] = dv.n

            def emit_pv_half(qt, half, h=h):
                par = qt % 2
                PH = PBUF[half][par]

                @block.tensor
                def _(t, qt=qt, half=half, PH=PH, par=par):
                    ac.wait(t, exp_done[(h, qt, half)])
                    if "last" in pts_evac:
                        s_, n_ = pts_evac["last"]
                        s_.wait(t, n_)
                    dv.wait(t, ofm_evac.get("last", 0))
                    if half == 0:
                        dv.wait(t, ocomb.get("last", 0))
                    for i in range(16):
                        ins = t.transpose(PT_ps[:, i * 128:(i + 1) * 128],
                                          PH[:, i * 128:(i + 1) * 128], P_ident[:])
                    pe.inc(ins)
                    pts_pe[(half, par)] = pe.n

                @block.vector
                def _(v):
                    pe.wait(v)
                    dv.inc(v.tensor_copy(PTS[:], PT_ps[:]))
                    pts_evac["last"] = (dv, dv.n)

                @block.tensor
                def _(t, qt=qt, half=half):
                    s_, n_ = pts_evac["last"]
                    s_.wait(t, n_)
                    for kc in range(16):
                        ins = t.matmul(O_ps[:, half * 128:(half + 1) * 128],
                                       PTS[:, kc * 128:(kc + 1) * 128],
                                       VSB[:, half * 16 + kc],
                                       start=(kc == 0), stop=(kc == 15))
                    pe.inc(ins)
                    pv_done[half] = pe.n

            def emit_ocombine(qt, h=h):
                par = qt % 2
                SST = SSTT[:, 24 * par:24 * par + 24]

                @block.vector
                def _(v, qt=qt, SST=SST, par=par):
                    pe.wait(v, pv_done[1])
                    dv.wait(v, stats_done[par])
                    v.tensor_scalar_mul(OT1[:], O_ps[:, 0:128], SST[:, 18:19])
                    v.drain()
                    v.scalar_tensor_tensor(OT2[:], O_ps[:, 128:256], SST[:, 19:20],
                                           OT1[:], ALU.mult, ALU.add)
                    dv.inc(v.drain())
                    ocomb["last"] = dv.n

                @block.tensor
                def _(t, qt=qt):
                    dv.wait(t, ocomb["last"])
                    dv.wait(t, ofm_evac.get("last", 0))
                    pe.inc(t.transpose(PT_ps[:, 0:128], OT2[:], P_ident[:]))

                @block.vector
                def _(v, qt=qt):
                    pe.wait(v)
                    dv.inc(v.tensor_scalar_add(o_fm[:, h, qt * 128:(qt + 1) * 128],
                                               PT_ps[:, 0:128], P_vb[:, h:h + 1]))
                    ofm_evac["last"] = dv.n

            emit_scores_half(0, 0)
            emit_scores_half(0, 1)
            emit_stats(0)
            for qt in range(1, NTQ):
                emit_scores_half(qt, 0)
                emit_pv_half(qt - 1, 0)
                emit_scores_half(qt, 1)
                emit_pv_half(qt - 1, 1)
                emit_stats(qt)
                emit_ocombine(qt - 1)
            emit_pv_half(NTQ - 1, 0)
            emit_pv_half(NTQ - 1, 1)
            emit_ocombine(NTQ - 1)
        barrier()

    stack_c.close()

    if debug:
        with nc.sbuf_tensor("DBG4", [128, 2048], F32) as DBG4:
            dump_tap(lambda a: o_fm[:].rearrange("p a c -> p (a c)")[:, a * 2048:(a + 1) * 2048],
                     taps["t_ofm"], 8, 2048, DBG4)

    with nc.sbuf_tensor("WOL", [128, 16, D], BF16) as WOL, \
         nc.sbuf_tensor("XRES", [128, D], F32) as XRES, \
         nc.sbuf_tensor("FRES", [128, D], F32) as FRES, \
         nc.sbuf_tensor("S1", [128, D], F32) as S1, \
         nc.sbuf_tensor("OUTB", [128, D], F32) as OUTB, \
         nc.psum_tensor("Y2", [128, 2048], F32) as Y2:

        @block.gpsimd
        def _(g):
            for a in range(4):
                gd.dma(g.dma_start(out=WOL[:, 4 * a:4 * (a + 1)],
                                   in_=wo[a * 512:(a + 1) * 512, :].rearrange("(a p) c -> p a c", p=128)))
        ldwo = gd.n

        res_used = {}
        for qt in range(NTQ):
            @block.sync
            def _(sync, qt=qt):
                if qt > 0:
                    dv.wait(sync, res_used[qt - 1])
                sy.dma(sync.dma_start(out=XRES[:], in_=xb[qt * 128:(qt + 1) * 128, :]))
                sy.dma(sync.dma_start(out=FRES[:], in_=ffn_y[qt * 128:(qt + 1) * 128, :]))
            ldr = sy.n

            @block.tensor
            def _(t, qt=qt):
                if qt == 0:
                    gd.wait(t, ldwo)
                else:
                    dv.wait(t, res_used[qt - 1])
                for fc in range(H):
                    for jj in range(4):
                        ins = t.matmul(Y2[:, jj * 512:(jj + 1) * 512],
                                       o_fm[:, fc, qt * 128:(qt + 1) * 128],
                                       WOL[:, fc, jj * 512:(jj + 1) * 512],
                                       start=(fc == 0), stop=(fc == H - 1))
                pe.inc(ins)

            @block.vector
            def _(v, qt=qt, ldr=ldr):
                sy.wait(v, ldr)
                v.tensor_tensor(S1[:], XRES[:], FRES[:], ALU.add)
                pe.wait(v)
                v.drain()
                v.tensor_tensor(OUTB[:], S1[:], Y2[:], ALU.add)
                dv.inc(v.drain())
                res_used[qt] = dv.n

            @block.sync
            def _(sync, qt=qt):
                dv.wait(sync, res_used[qt])
                sy.dma(sync.dma_start(out=y[qt * 128:(qt + 1) * 128, :], in_=OUTB[:]))

        @block.sync
        def _(sync):
            sy.wait(sync)

    stx.close()
    return nc, taps


def _prep_inputs(inputs):
    x = np.asarray(inputs["L0_attn_x"])
    f32 = np.float32
    wg_u8 = np.asarray(inputs["L0_ffn_wq_gate"]).view(np.uint8).reshape(HID, D, 4)[:, :, 0]
    wu_u8 = np.asarray(inputs["L0_ffn_wq_up"]).view(np.uint8).reshape(HID, D, 4)[:, :, 0]
    wd_u8 = np.asarray(inputs["L0_ffn_wq_down"]).view(np.uint8).reshape(D, HID, 4)[:, :, 0]
    shared = {
        "wqd": np.ascontiguousarray(inputs["L0_attn_Wq_down"], f32),
        "wqu": np.ascontiguousarray(inputs["L0_attn_Wq_up"], f32),
        "wqr": np.ascontiguousarray(inputs["L0_attn_Wq_rope"], f32),
        "wkd": np.ascontiguousarray(inputs["L0_attn_Wkv_down"], f32),
        "wku": np.ascontiguousarray(inputs["L0_attn_Wk_up"], f32),
        "wvu": np.ascontiguousarray(inputs["L0_attn_Wv_up"], f32),
        "wkr": np.ascontiguousarray(inputs["L0_attn_Wk_rope"], f32),
        "wo": np.ascontiguousarray(inputs["L0_attn_Wout"], f32),
        "qg2": np.ascontiguousarray(np.asarray(inputs["L0_attn_q_gamma"], f32).reshape(12, 128).T),
        "qb2": np.ascontiguousarray(np.asarray(inputs["L0_attn_q_beta"], f32).reshape(12, 128).T),
        "kg2": np.ascontiguousarray(np.asarray(inputs["L0_attn_kv_gamma"], f32).reshape(4, 128).T),
        "kb2": np.ascontiguousarray(np.asarray(inputs["L0_attn_kv_beta"], f32).reshape(4, 128).T),
        "fg": np.ascontiguousarray(wg_u8.reshape(NHC, 128, 16, 128).transpose(0, 3, 2, 1)),
        "fu": np.ascontiguousarray(wu_u8.reshape(NHC, 128, 16, 128).transpose(0, 3, 2, 1)),
        "fd": np.ascontiguousarray(wd_u8.reshape(D, NHC, 128).transpose(1, 2, 0)),
        "sg2": np.ascontiguousarray(np.asarray(inputs["L0_ffn_s_gate"], f32).reshape(NHC, 128).T),
        "su2": np.ascontiguousarray(np.asarray(inputs["L0_ffn_s_up"], f32).reshape(NHC, 128).T),
        "sdr": np.ascontiguousarray(np.asarray(inputs["L0_ffn_s_down"], f32).reshape(1, D)),
    }
    per_core = []
    for c in range(8):
        b, qs = c // 4, c % 4
        perm = np.concatenate([np.arange(qs * OWN, (qs + 1) * OWN),
                               np.arange(0, qs * OWN),
                               np.arange((qs + 1) * OWN, S)])
        m = dict(shared)
        m["xb"] = np.ascontiguousarray(x[b][perm], f32)
        m["posP"] = np.ascontiguousarray(perm.astype(f32).reshape(NT, 128).T)
        per_core.append(m)
    return per_core


def kernel(**inputs):
    per_core = _prep_inputs(inputs)
    nc, _ = build_program(debug=DEBUG)
    res = run_bass_kernel_spmd(nc, per_core, core_ids=list(range(8)))
    out = np.empty((B, S, D), np.float32)
    for c in range(8):
        b, qs = c // 4, c % 4
        out[b, qs * OWN:(qs + 1) * OWN] = res.results[c]["y"]
    return out
